# version 44
# baseline (speedup 1.0000x reference)
import dataclasses
import numpy as np
import ml_dtypes
import concourse.bass as bass
import concourse.mybir as mybir
from concourse.bass import ds
from concourse.bass_utils import run_bass_kernel_spmd

F32 = mybir.dt.float32
BF16 = mybir.dt.bfloat16
U8 = mybir.dt.uint8
AF = mybir.ActivationFunctionType
ALU = mybir.AluOpType
BF = ml_dtypes.bfloat16

L = 2048
D = 2048
NH = 8
C = 128
SC = 512
NSC = L // SC
NCH = SC // C
KCH = D // 128
QK_CT = 16
N_CT = 24
Z_CT = 8
TOKH = 1024
IT_FF = 64
DT_FF = 16
FGRP = 512


class EngWrap:

    _PASS = {"wait_ge", "nop", "dma_start", "collective_compute", "partition_id",
             "indirect_dma_start", "sem_inc", "drain"}

    def __init__(self, eng, ctr):
        self._e = eng
        self._c = ctr

    def __getattr__(self, name):
        f = getattr(self._e, name)
        if name in self._PASS or name.startswith("_"):
            return f

        def g(*a, **k):
            self._e.wait_ge(self._c.h, self._c.n)
            r = f(*a, **k)
            self._c.inc(r)
            return r
        return g


class Ctr:
    def __init__(self, nc, name):
        self.h = nc.alloc_semaphore(name)
        self.n = 0
        self._w = {}

    def inc(self, instr, k=1):
        instr.then_inc(self.h, k)
        self.n += k
        return instr

    def pre(self, eng):
        eng.wait_ge(self.h, self.n)

    def wait(self, eng, at=None):
        n = self.n if at is None else at
        e = getattr(eng, "_e", eng)
        k = id(e)
        if self._w.get(k, 0) >= n:
            return
        e.wait_ge(self.h, n)
        self._w[k] = n


def _bview(ap, count):
    return dataclasses.replace(ap, ap=[ap.ap[0], [0, count], ap.ap[1]])


def _bview3(ap, inner=128):
    return dataclasses.replace(ap, ap=[ap.ap[0], ap.ap[1], [0, inner]])


def _as3(ap, h=8, inner=128):
    return dataclasses.replace(ap, ap=[ap.ap[0], [inner, h], [1, inner]])


def build_nc(dbg=(), stop_after=None):
    nc = bass.Bass("TRN2", target_bir_lowering=False)
    T, V, A, P, S = nc.tensor, nc.vector, nc.scalar, nc.gpsimd, nc.sync
    _lp = nc.allow_low_precision("bf16 operands with fp32 PSUM accumulation per problem tolerance")
    _lp.__enter__()

    xT = nc.dram_tensor("xT", [D, L], F32, kind="ExternalInput").ap()
    wqkv = nc.dram_tensor("wqkv", [N_CT, 128, KCH, 128], F32, kind="ExternalInput").ap()
    wz = nc.dram_tensor("wz", [Z_CT, 128, KCH, 128], F32, kind="ExternalInput").ap()
    wba = nc.dram_tensor("wba", [128, 2, KCH, 8], F32, kind="ExternalInput").ap()
    convw = nc.dram_tensor("convw", [128, N_CT, 4], F32, kind="ExternalInput").ap()
    alog = nc.dram_tensor("alog", [8, 1], F32, kind="ExternalInput").ap()
    dtb = nc.dram_tensor("dtb", [8, 1], F32, kind="ExternalInput").ap()
    normw = nc.dram_tensor("normw", [128, 1], F32, kind="ExternalInput").ap()
    wout = nc.dram_tensor("wout", [DT_FF, 128, 16, 128], F32, kind="ExternalInput").ap()
    wg_in = nc.dram_tensor("wg", [IT_FF, 128, KCH, 128], U8, kind="ExternalInput").ap()
    wu_in = nc.dram_tensor("wu", [IT_FF, 128, KCH, 128], U8, kind="ExternalInput").ap()
    wd_in = nc.dram_tensor("wd", [DT_FF, 128, IT_FF, 128], U8, kind="ExternalInput").ap()
    sg_in = nc.dram_tensor("sg", [128, IT_FF], F32, kind="ExternalInput").ap()
    su_in = nc.dram_tensor("su", [128, IT_FF], F32, kind="ExternalInput").ap()
    sd_in = nc.dram_tensor("sd", [128, DT_FF], F32, kind="ExternalInput").ap()
    id128_in = nc.dram_tensor("id128", [128, 128], BF16, kind="ExternalInput").ap()
    id8_in = nc.dram_tensor("id8", [8, 8], F32, kind="ExternalInput").ap()
    y_out = nc.dram_tensor("y", [D, TOKH], F32, kind="ExternalOutput").ap()
    dbg_out = {}
    if "proj" in dbg:
        dbg_out["qhat"] = nc.dram_tensor("dbg_qhat", [128, N_CT, SC], BF16, kind="ExternalOutput").ap()
        dbg_out["zsc"] = nc.dram_tensor("dbg_zsc", [128, Z_CT, SC], BF16, kind="ExternalOutput").ap()
        dbg_out["bg"] = nc.dram_tensor("dbg_bg", [8, 3, SC], F32, kind="ExternalOutput").ap()
    if "og" in dbg:
        dbg_out["og"] = nc.dram_tensor("dbg_og", [128, NH, L], BF16, kind="ExternalOutput").ap()
    if "mid" in dbg:
        dbg_out["araw"] = nc.dram_tensor("dbg_araw", [128, KCH, TOKH], BF16, kind="ExternalOutput").ap()
        dbg_out["xmid"] = nc.dram_tensor("dbg_xmid", [128, KCH, TOKH], F32, kind="ExternalOutput").ap()
    if "ln2" in dbg:
        dbg_out["ln2"] = nc.dram_tensor("dbg_ln2", [128, KCH, TOKH], BF16, kind="ExternalOutput").ap()

    dbgb = nc.dram_tensor("dbgb", [4, 128, 1024], BF16, kind="ExternalOutput").ap()
    dbgf = nc.dram_tensor("dbgf", [4, 128, 1024], F32, kind="ExternalOutput").ap()
    cc_in = nc.dram_tensor("cc_in", [NSC, NH * 128, SC], BF16, kind="Internal")
    cc_out = nc.dram_tensor("cc_out", [NSC, 2 * NH * 128, SC], BF16, kind="Internal")
    xmid_dram = nc.dram_tensor("xmid_dram", [D, TOKH], F32, kind="Internal")
    wbf_dram = nc.dram_tensor("wbf_dram", [N_CT + Z_CT, 128, KCH * 128], BF16, kind="Internal")

    import contextlib
    sbctx_ab = contextlib.ExitStack()
    sbctx_2 = contextlib.ExitStack()
    sbctx_3 = contextlib.ExitStack()

    def sb(name, shape, dtype):
        return nc.alloc_sbuf_tensor(name, shape, dtype)

    def sbA(name, shape, dtype):
        return sbctx_ab.enter_context(nc.sbuf_tensor(name, shape, dtype))

    def sb2(name, shape, dtype):
        return sbctx_2.enter_context(nc.sbuf_tensor(name, shape, dtype))

    def sb3(name, shape, dtype):
        return sbctx_3.enter_context(nc.sbuf_tensor(name, shape, dtype))

    ones128_bf = sb("ones128_bf", [128, 1], BF16)
    ones1_bf = sb("ones1_bf", [1, 128], BF16)
    ones8_f = sb("ones8_f", [8, 128], F32)
    tri_ji = sb("tri_ji", [128, 128], F32)
    blockid = sb("blockid", [8, 8 * 128], F32)
    m128row = sb("m128row", [1, 128], BF16)
    id128 = sb("id128s", [128, 128], BF16)
    id8 = sb("id8s", [8, 8], F32)
    convw_sb = sb("convw_sb", [128, N_CT, 4], F32)
    wba_sb = sb("wba_sb", [128, 2, KCH, 8], BF16)
    alog_sb = sb("alog_sb", [8, 1], F32)
    dtb_sb = sb("dtb_sb", [8, 1], F32)
    negA = sb("negA", [8, 1], F32)
    normw_sb = sb("normw_sb", [128, 1], F32)
    sg_sb = sb("sg_sb", [128, IT_FF], F32)
    su_sb = sb("su_sb", [128, IT_FF], F32)
    sd_sb = sb("sd_sb", [128, DT_FF], F32)
    constt = sb("constt", [128, 8], F32)
    stat_f = sb("stat_f", [1, 2 * SC], F32)
    stat_b = sb("stat_b", [1, 2 * SC], BF16)
    ln2 = sb("ln2", [128, KCH, TOKH], BF16)

    ln1 = sbA("ln1", [128, KCH, SC], BF16)
    mrep = sbA("mrep", [128, SC], BF16)
    rrep = sbA("rrep", [128, SC], BF16)
    wslab = sbA("wslab", [128, 3, KCH * 128], BF16)
    qraw = sbA("qraw", [128, N_CT, 520], BF16)
    tails = sbA("tails", [128, N_CT, 4], BF16)
    qhat = sbA("qhat", [128, N_CT, SC], BF16)
    z_sc = sbA("z_sc", [128, Z_CT, SC], BF16)
    bfm = sbA("bfm", [8, SC], F32)
    lnbfm = sbA("lnbfm", [8, SC], F32)
    gfm = sbA("gfm", [8, SC], F32)
    sp8a = sbA("sp8a", [8, SC], F32)
    sp8b = sbA("sp8b", [8, SC], F32)

    og_sc = sbA("og_sc", [128, NH, SC], BF16)
    M_f32 = sbA("M_f32", [128, NH, 128], F32)
    Mbf = sbA("Mbf", [128, NH, 128], BF16)

    g_tm = sbA("g_tm", [128, 8], F32)
    lnb_tm = sbA("lnb_tm", [128, 8], F32)
    G_tm = sbA("G_tm", [128, 8], F32)
    G8 = sbA("G8", [8, 128], F32)
    colA = sbA("colA", [128, 8], F32)
    rowB = sbA("rowB", [8, 128], F32)
    bdG = sbA("bdG", [8, 8 * 128], F32)
    bdB = sbA("bdB", [8, 8 * 128], F32)
    btm3 = sbA("btm3", [128, 3, 8], F32)
    etm3 = sbA("etm3", [128, 3, 8], F32)
    gam3 = sbA("gam3", [128, 3, 8], F32)
    t8a = sbA("t8a", [128, 8], F32)
    eG = sbA("eG", [128, NH, 128], BF16)
    nW = sbA("nW", [128, NH, 128], F32)
    nWy = sbA("nWy", [128, NH, 128], F32)
    eEb = sbA("eEb", [128, NH, 128], BF16)
    kk_sb = sbA("kk_sb", [128, NH, 128], BF16)
    qk_sb = sbA("qk_sb", [128, NH, 128], BF16)
    A_sb2 = sbA("A_sb2", [128, 2, NH, 128], BF16)
    AT_sb2 = sbA("AT_sb2", [128, 2, NH, 128], BF16)
    V6T2 = sbA("V6T2", [128, 2, NH, 128], BF16)
    Pw = [sbA("Pw0", [128, NH, 128], BF16), sbA("Pw1", [128, NH, 128], BF16)]
    PTw = [sbA("PTw0", [128, NH, 128], BF16), sbA("PTw1", [128, NH, 128], BF16)]
    Vcp = [sbA("Vc0", [128, NH, 128], BF16), sbA("Vc1", [128, NH, 128], BF16)]

    def _carve3(base, off, n0):
        b = base[:, :, :]
        return dataclasses.replace(b, offset=b.offset + off,
                                   ap=[b.ap[0], [8 * 128, n0], [128, 8], [1, 128]])

    def _carve1(base, off):
        b = base[:, :, :]
        return dataclasses.replace(b, offset=b.offset + off,
                                   ap=[b.ap[0], [128, 8], [1, 128]])

    vtm3 = _carve3(ln1, 0, 3)
    ktm3 = _carve3(ln1, 3072, 3)
    eEc_t = _carve1(ln1, 6144)
    ST3 = _carve3(qraw, 0, 3)
    qtil3 = _carve3(qraw, 3072, 3)
    ktil3 = _carve3(qraw, 6144, 3)
    R0t = _carve1(qraw, 9216)
    Ut = _carve1(qraw, 10240)
    eEa_t = _carve1(qraw, 11264)

    import contextlib
    pctx = contextlib.ExitStack()
    psA = pctx.enter_context(nc.psum_tensor("psA", [128, 512], F32))
    ps_b1 = pctx.enter_context(nc.psum_tensor("ps_b1", [128, 1024], F32))
    ps_b2 = pctx.enter_context(nc.psum_tensor("ps_b2", [128, 1024], F32))
    ps_m1 = pctx.enter_context(nc.psum_tensor("ps_m1", [128, 1024], F32))
    ps_tr = pctx.enter_context(nc.psum_tensor("ps_tr", [128, 1024], BF16))

    csd = Ctr(nc, "csd")
    cx = Ctr(nc, "cx")
    cw_s = [Ctr(nc, f"cw{i}") for i in range(4)]
    cwo_s = [Ctr(nc, "cwo0"), Ctr(nc, "cwo1")]
    cgu_s = [Ctr(nc, f"cgu{i}") for i in range(4)]
    cwd_s = [Ctr(nc, f"cwd{i}") for i in range(4)]
    cog = Ctr(nc, "cog")
    cph = Ctr(nc, "cph")
    cxm_s = [Ctr(nc, "cxm0"), Ctr(nc, "cxm1")]
    cspill = Ctr(nc, "cspill")
    cwb = Ctr(nc, "cwb")
    cyb = [Ctr(nc, "cy0"), Ctr(nc, "cy1")]
    cdbg = Ctr(nc, "cdbg")
    ct_ = Ctr(nc, "ct")
    cv = Ctr(nc, "cv")
    ca = Ctr(nc, "ca")
    cp = Ctr(nc, "cp")
    ccc = Ctr(nc, "ccc")

    V = EngWrap(V, cv)
    A = EngWrap(A, ca)
    P_raw = P
    P = EngWrap(P, cp)

    pid = P_raw.partition_id()
    half_off = (pid % 2) * TOKH
    peer_row = (pid + 1 - (pid % 2) * 2) * (NH * 128)

    def stats_epilogue(ms_ap, qs_ap, m_dst, r_dst, eps, repl_ps=None):
        ct_.wait(V)
        V.tensor_scalar_mul(stat_f[0:1, 0:512], ms_ap, 1.0 / D)
        V.tensor_copy(stat_b[0:1, 0:512], stat_f[0:1, 0:512])
        V.tensor_mul(stat_f[0:1, 512:1024], stat_f[0:1, 0:512], stat_f[0:1, 0:512])
        V.scalar_tensor_tensor(stat_f[0:1, 0:512], qs_ap, 1.0 / D,
                               stat_f[0:1, 512:1024], ALU.mult, ALU.subtract)
        V.engine_nop()
        cv.wait(A)
        A.activation(stat_f[0:1, 512:1024], stat_f[0:1, 0:512], AF.Sqrt,
                            bias=eps, scale=1.0)
        ca.wait(V)
        V.reciprocal(stat_b[0:1, 512:1024], stat_f[0:1, 512:1024])
        cv.wait(T)
        ca.wait(T)
        rp_ = ps_b2 if repl_ps is None else repl_ps
        _tmm = T.matmul(rp_[:, 0:512], ones1_bf[:, :], stat_b[0:1, 0:512], start=True, stop=True)
        ct_.inc(T.matmul(rp_[:, 512:1024], ones1_bf[:, :], stat_b[0:1, 512:1024],
                         start=True, stop=True))
        ct_.wait(A)
        A.activation(m_dst, rp_[:, 0:512], AF.Copy, bias=0.0, scale=1.0)
        A.activation(r_dst, rp_[:, 512:1024], AF.Copy, bias=0.0, scale=1.0)

    csd.inc(S.dma_start(out=convw_sb[:, :, :], in_=convw), 16)
    cx.inc(P.dma_start(out=wba_sb[:, :, :, :], in_=wba), 16)
    csd.inc(S.dma_start(out=alog_sb[:, :], in_=alog), 16)
    csd.inc(S.dma_start(out=dtb_sb[:, :], in_=dtb), 16)
    csd.inc(S.dma_start(out=normw_sb[:, :], in_=normw), 16)
    csd.inc(S.dma_start(out=sg_sb[:, :], in_=sg_in), 16)
    csd.inc(S.dma_start(out=su_sb[:, :], in_=su_in), 16)
    csd.inc(S.dma_start(out=sd_sb[:, :], in_=sd_in), 16)
    csd.inc(S.dma_start(out=id128[:, :], in_=id128_in), 16)
    csd.inc(S.dma_start(out=id8[:, :], in_=id8_in), 16)

    _cvals = [0.0, 1.0, -1.0, 1e-5, 1e-6, 128.0 * 1e-6, 128.0, 1.0 / 128.0]
    for _i, _v in enumerate(_cvals):
        V.memset(constt[:, _i:_i + 1], _v)
        nc.const_aps.register(F32, _v, constt[:, _i:_i + 1])
    V.memset(ones128_bf[:, :], 1.0)
    V.memset(ones1_bf[:, :], 1.0)
    V.memset(ones8_f[:, :], 1.0)
    V.memset(m128row[:, :], -128.0)
    V.memset(tails[:, :, :], 0.0)
    V.memset(M_f32[:, :, :], 0.0)
    V.memset(tri_ji[:, :], 1.0)
    V.memset(blockid[:, :], 1.0)
    cv.wait(P)
    P.affine_select(tri_ji[:, :], tri_ji[:, :], [[1, 128]], ALU.is_ge, 0.0,
                           channel_multiplier=-1)
    P.affine_select(blockid[:, :], blockid[:, :], [[-1, 8], [0, 128]],
                           ALU.is_equal, 0.0, channel_multiplier=1)
    csd.wait(A)
    cv.wait(A)
    A.activation(negA[:, :], alog_sb[:, :], AF.Exp, bias=0.0, scale=1.0)
    ca.wait(V)
    V.tensor_scalar_mul(negA[:, :], negA[:, :], -1.0)
    cp.wait(T)
    cp.wait(V)
    csd.wait(T)
    csd.wait(V)
    csd.wait(P)

    PAIRS = [[0, 1], [2, 3], [4, 5], [6, 7]]
    w_used = {}
    wb_at = {}
    og_stored = 0
    cc_issued = 0

    def scope(name):
        return nc.named_scope(name)

    for sc in range(NSC):
        _scA = scope(f"A{sc}"); _scA.__enter__()
        tok0 = sc * SC
        if sc > 0:
            ct_.wait(P)
            cv.wait(P)
            ca.wait(P)
            cdbg.wait(P)
        if sc == 0:
            cx.inc(P.dma_start(out=qhat[:, 0:16, :],
                               in_=xT[:, tok0:tok0 + SC].rearrange("(k p) t -> p k t", p=128)), 16)
        if sc == 0:
            l2done = {}
            cviews = {}
            tas = {}
            eg_at = {}
            etm_at = {}
            eea_at = {}
            eeb_at = {}
            eec_at = {}
            amask_at = {}
            stmask_at = {}
            vtm_at = {}
            ktm_at = {}
            ktil_at = {}
            mbf_at = {}
            y_r0 = {}
            u_at = {}
            bank_free = {}
            buf_read = {}
            buf_rdy = {}
            psm1_s = {"a": 0, "v": 0}
            pstr_s = {"a": 0, "v": 0}
            psa_s = {"a": 0, "v": 0}
            rot = [ps_b2, ps_b1]
            vread = {}

        def psm1_take():
            ca.wait(T, at=psm1_s["a"])
            cv.wait(T, at=psm1_s["v"])

        def bwait(eng, key):
            r = buf_rdy.get(key)
            if r is not None:
                (ca if r[0] == "A" else cv if r[0] == "V" else cp).wait(eng, at=r[1])

        def chain_mm(bank_i, lhs, lkey, rhs, rkey):
            if bank_i in bank_free:
                ca.wait(T, at=bank_free[bank_i][0])
                cv.wait(T, at=bank_free[bank_i][1])
            bwait(T, lkey)
            bwait(T, rkey)
            pb = rot[bank_i]
            for h in range(8):
                mm = T.matmul(pb[:, h * 128:(h + 1) * 128], lhs[:, h, :], rhs[:, h, :],
                              start=True, stop=True)
            ct_.inc(mm)
            buf_read[lkey] = ct_.n
            buf_read[rkey] = ct_.n
            return ct_.n

        def evac_copy(bank_i, dst, dkey, prod_at, eng="A"):
            if eng == "A":
                ct_.wait(A, at=max(prod_at, buf_read.get(dkey, 0)))
                A.activation(dst[:, :, :], rot[bank_i][:, 0:1024], AF.Copy, bias=0.0, scale=1.0)
                bank_free[bank_i] = (ca.n, bank_free.get(bank_i, (0, 0))[1])
                buf_rdy[dkey] = ("A", ca.n)
            else:
                ct_.wait(V, at=max(prod_at, buf_read.get(dkey, 0)))
                V.tensor_copy(dst[:, :, :], rot[bank_i][:, 0:1024])
                bank_free[bank_i] = (bank_free.get(bank_i, (0, 0))[0], cv.n)
                buf_rdy[dkey] = ("V", cv.n)

        def evac_add(bank_i, dst, dkey, a_src, prod_at):
            ct_.wait(V, at=max(prod_at, buf_read.get(dkey, 0)))
            V.tensor_add(dst[:, :, :], a_src[:, :, :], _as3(rot[bank_i][:, 0:1024]))
            bank_free[bank_i] = (bank_free.get(bank_i, (0, 0))[0], cv.n)
            buf_rdy[dkey] = ("V", cv.n)

        def u_prep_open(c):
            d2, d3 = c % 2, c % 3
            tas[c] = slice((c % NCH) * C, (c % NCH) * C + C)
            cviews[c] = dict(a=A_sb2[:, d2], at=AT_sb2[:, d2], st=ST3[:, d3],
                             kt=ktil3[:, d3], qt=qtil3[:, d3], vt=vtm3[:, d3],
                             km=ktm3[:, d3], v6=V6T2[:, d2], b=btm3[:, d3, :],
                             e=etm3[:, d3, :], g=gam3[:, d3, :])

        def u_p1(c):
            vw = cviews[c]; ta = tas[c]
            cv.wait(T, at=psa_s["v"])
            ca.wait(T, at=psa_s["a"])
            _tmm = T.transpose(psA[:, 0:8], gfm[:, ta], id8[:, :])
            _tmm = T.transpose(psA[:, 8:16], bfm[:, ta], id8[:, :])
            ct_.inc(T.transpose(psA[:, 16:24], lnbfm[:, ta], id8[:, :]))
            ct_.wait(V)
            V.tensor_copy(g_tm[:, :], psA[:, 0:8])
            V.tensor_copy(vw["b"], psA[:, 8:16])
            V.tensor_copy(lnb_tm[:, :], psA[:, 16:24])

        def u_p2(c):
            ta = tas[c]
            cv.wait(T)
            _tmm = T.matmul(psA[:, 24:32], tri_ji[:, :], g_tm[:, :], start=True, stop=True)
            ct_.inc(T.matmul(psA[0:8, 32:160], g_tm[:, :], tri_ji[:, :], start=True, stop=True))
            ct_.wait(V)
            V.tensor_copy(G_tm[:, :], psA[:, 24:32])
            V.tensor_copy(G8[:, :], psA[0:8, 32:160])
            V.tensor_add(colA[:, :], G_tm[:, :], lnb_tm[:, :])
            V.tensor_add(rowB[:, :], G8[:, :], lnbfm[:, ta])
            V.tensor_mul(bdG[:, :], _bview(G8[:, :], 8), blockid[:, :])
            V.tensor_mul(bdB[:, :], _bview(rowB[:, :], 8), blockid[:, :])
            psa_s["v"] = cv.n

        def u_p3a(c):
            vw = cviews[c]
            psm1_take()
            cv.wait(T)
            _tmm = T.matmul(ps_m1[:, 0:512], ones8_f[:, :], bdG[:, 0:512], start=True, stop=True)
            ct_.inc(T.matmul(ps_m1[:, 512:1024], ones8_f[:, :], bdG[:, 512:1024],
                             start=True, stop=True))
            ct_.wait(A)
            cv.wait(A)
            A.activation(eG[:, :, :], ps_m1[:, 0:1024], AF.Exp, bias=0.0, scale=1.0)
            eg_at[c] = ca.n
            pm1 = ps_m1[:, 0:1024]
            gb_last = dataclasses.replace(pm1, ap=[pm1.ap[0], [128, 8]],
                                          offset=pm1.offset + 127)
            ct_.wait(V)
            ca.wait(V)
            V.tensor_sub(t8a[:, :], gb_last, G_tm[:, :])
            cv.wait(A)
            A.activation(vw["e"], t8a[:, :], AF.Exp, bias=0.0, scale=1.0)
            A.activation(vw["g"], gb_last, AF.Exp, bias=0.0, scale=1.0)
            etm_at[c] = ca.n
            ca.wait(V)
            V.tensor_sub(nW[:, :, :], _as3(pm1), _bview3(colA[:, :]))
            V.tensor_scalar_max(nW[:, :, :], nW[:, :, :], 0.0)
            cv.wait(A)
            A.activation(eEa_t[:, :, :], nW[:, :, :], AF.Exp, bias=0.0, scale=-1.0)
            eea_at[c] = ca.n
            ca.wait(V)
            V.tensor_sub(nW[:, :, :], _as3(pm1), _bview3(G_tm[:, :]))
            V.tensor_scalar_min(nW[:, :, :], nW[:, :, :], 0.0)
            cv.wait(A)
            A.activation(eEc_t[:, :, :], nW[:, :, :], AF.Exp, bias=0.0, scale=1.0)
            eec_at[c] = ca.n
            psm1_s["a"] = ca.n
            psm1_s["v"] = cv.n

        def u_p3b(c):
            psm1_take()
            _tmm = T.matmul(ps_m1[:, 0:512], ones8_f[:, :], bdB[:, 0:512], start=True, stop=True)
            ct_.inc(T.matmul(ps_m1[:, 512:1024], ones8_f[:, :], bdB[:, 512:1024],
                             start=True, stop=True))
            ct_.wait(V)
            ca.wait(V, at=eec_at[c])
            V.tensor_sub(nW[:, :, :], _as3(ps_m1[:, 0:1024]), _bview3(G_tm[:, :]))
            V.tensor_scalar_min(nW[:, :, :], nW[:, :, :], 0.0)
            cv.wait(A)
            A.activation(eEb[:, :, :], nW[:, :, :], AF.Exp, bias=0.0, scale=1.0)
            eeb_at[c] = ca.n
            psm1_s["v"] = cv.n

        def u_p4(c):
            vw = cviews[c]; ta = tas[c]; d2 = c % 2
            psm1_take()
            cv.wait(T, at=l2done[c // NCH])
            for h in range(8):
                _tmm = T.matmul(ps_m1[:, h * 128:(h + 1) * 128], qhat[:, 8 + h, ta],
                                qhat[:, 8 + h, ta], start=True, stop=True)
            ct_.inc(_tmm)
            ct_.wait(A)
            cv.wait(A)
            A.activation(kk_sb[:, :, :], ps_m1[:, 0:1024], AF.Copy, bias=0.0, scale=1.0)
            kk_at = ca.n
            ca.wait(V, at=kk_at)
            ct_.wait(V, at=buf_read.get(f"A2_{d2}", 0))
            bwait(V, f"A2_{d2}")
            V.tensor_mul(vw["a"], eEa_t[:, :, :], kk_sb[:, :, :])
            cv.wait(P)
            P.affine_select(vw["a"], vw["a"], [[0, 8], [-1, 128]],
                            ALU.is_gt, 0.0, channel_multiplier=1)
            ca.wait(V, at=eeb_at[c])
            ct_.wait(V, at=buf_read.get(f"AT2_{d2}", 0))
            bwait(V, f"AT2_{d2}")
            V.tensor_mul(vw["at"], eEb[:, :, :], kk_sb[:, :, :])
            cv.wait(P)
            P.affine_select(vw["at"], vw["at"], [[0, 8], [1, 128]],
                            ALU.is_gt, 0.0, channel_multiplier=-1)
            amask_at[c] = cp.n
            buf_rdy[f"A2_{d2}"] = ("P", cp.n)
            buf_rdy[f"AT2_{d2}"] = ("P", cp.n)
            psm1_s["a"] = kk_at

        def u_p5(c):
            vw = cviews[c]; ta = tas[c]; d3 = c % 3
            psm1_take()
            cv.wait(T, at=l2done[c // NCH])
            for h in range(8):
                _tmm = T.matmul(ps_m1[:, h * 128:(h + 1) * 128], qhat[:, 8 + h, ta],
                                qhat[:, h, ta], start=True, stop=True)
            ct_.inc(_tmm)
            ct_.wait(A)
            cv.wait(A)
            A.activation(qk_sb[:, :, :], ps_m1[:, 0:1024], AF.Copy, bias=0.0, scale=1.0)
            qk_at = ca.n
            ca.wait(V, at=qk_at)
            ct_.wait(V, at=buf_read.get(f"ST_{d3}", 0))
            bwait(V, f"ST_{d3}")
            V.tensor_mul(vw["st"], eEc_t[:, :, :], qk_sb[:, :, :])
            cv.wait(P)
            P.affine_select(vw["st"], vw["st"], [[0, 8], [1, 128]],
                            ALU.is_ge, 0.0, channel_multiplier=-1)
            stmask_at[c] = cp.n
            psm1_s["a"] = qk_at

        def u_p6a(c):
            vw = cviews[c]; ta = tas[c]; d3 = c % 3
            ca.wait(T, at=pstr_s["a"])
            for h in range(8):
                _tmm = T.transpose(ps_tr[:, h * 128:(h + 1) * 128], qhat[:, 16 + h, ta],
                                   id128[:, :])
            ct_.inc(_tmm)
            ct_.wait(A)
            ct_.wait(A, at=buf_read.get(f"VT_{d3}", 0))
            cv.wait(A, at=vread.get(f"VT_{d3}", 0))
            A.activation(vw["vt"], ps_tr[:, :], AF.Copy, bias=0.0, scale=1.0)
            vtm_at[c] = ca.n
            pstr_s["a"] = ca.n

        def u_p6b(c):
            vw = cviews[c]; ta = tas[c]; d3 = c % 3
            ca.wait(T, at=pstr_s["a"])
            cv.wait(T, at=l2done[c // NCH])
            for h in range(8):
                _tmm = T.transpose(ps_tr[:, h * 128:(h + 1) * 128], qhat[:, 8 + h, ta],
                                   id128[:, :])
            ct_.inc(_tmm)
            ct_.wait(A)
            ct_.wait(A, at=buf_read.get(f"KM_{d3}", 0))
            A.activation(vw["km"], ps_tr[:, :], AF.Copy, bias=0.0, scale=1.0)
            ktm_at[c] = ca.n
            pstr_s["a"] = ca.n

        def u_p7(c):
            vw = cviews[c]; ta = tas[c]; d3 = c % 3
            ca.wait(V, at=eg_at[c])
            ct_.wait(V, at=max(buf_read.get(f"KT_{d3}", 0), buf_read.get(f"QT_{d3}", 0)))
            V.tensor_mul(vw["kt"], qhat[:, 8:16, ta], eG[:, :, :])
            V.tensor_mul(vw["qt"], qhat[:, 0:8, ta], eG[:, :, :])
            ktil_at[c] = cv.n

        def build_chain(c):
            vw = cviews[c]; d2 = c % 2
            units = []
            rp = [0]

            def c0():
                cp.wait(V, at=amask_at[c])
                ct_.wait(V, at=buf_read.get("VcA", 0))
                V.tensor_sub(Vcp[0][:, :, :], _bview(id128[:, :], 8), vw["a"])
                buf_rdy["VcA"] = ("V", cv.n)
            units.append(c0)
            cur, nxt = "VcA", "VcB"
            tl = {"VcA": Vcp[0], "VcB": Vcp[1], "Pw0": Pw[0], "Pw1": Pw[1],
                  "PTw0": PTw[0], "PTw1": PTw[1]}
            pk, ptk = f"A2_{d2}", f"AT2_{d2}"
            pt_, p_ = vw["at"], vw["a"]
            pend_v = []
            for lev in range(1, 7):
                npk, nptk = f"Pw{lev % 2}", f"PTw{lev % 2}"

                def s_unit(pk=pk, ptk=ptk, p_=p_, pt_=pt_, nptk=nptk, lev=lev):
                    b = rp[0] % 2
                    at = chain_mm(b, p_, pk, pt_, ptk)
                    evac_copy(b, tl[nptk], nptk, at)
                    rp[0] += 1
                units.append(s_unit)

                def v_unit(nptk=nptk, cur=cur, nxt=nxt):
                    b = rp[0] % 2
                    at = chain_mm(b, tl[nptk], nptk, tl[cur], cur)
                    evac_add(b, tl[nxt], nxt, tl[cur], at)
                    rp[0] += 1
                pend_v.append(v_unit)
                if lev < 6:
                    def p_unit(pk=pk, ptk=ptk, p_=p_, pt_=pt_, npk=npk):
                        b = rp[0] % 2
                        at = chain_mm(b, pt_, ptk, p_, pk)
                        evac_copy(b, tl[npk], npk, at, eng="V")
                        rp[0] += 1
                    units.append(p_unit)
                if len(pend_v) > 1:
                    units.append(pend_v.pop(0))
                cur, nxt = nxt, cur
                pk, ptk, p_, pt_ = npk, nptk, tl[npk], tl[nptk]
            while pend_v:
                units.append(pend_v.pop(0))

            def v6t(cur=cur):
                ca.wait(T, at=pstr_s["a"])
                bwait(T, cur)
                for h in range(8):
                    _tmm = T.transpose(ps_tr[:, h * 128:(h + 1) * 128], tl[cur][:, h, :],
                                       id128[:, :])
                ct_.inc(_tmm)
                buf_read[cur] = ct_.n
                ct_.wait(A)
                ct_.wait(A, at=buf_read.get(f"V6_{d2}", 0))
                A.activation(vw["v6"], ps_tr[:, :], AF.Copy, bias=0.0, scale=1.0)
                buf_rdy[f"V6_{d2}"] = ("A", ca.n)
                pstr_s["a"] = ca.n
            units.append(v6t)
            return units

        def u_y0(c):
            ct_.wait(V, at=buf_read.get("Mbf", 0))
            V.tensor_copy(Mbf[:, :, :], M_f32[:, :, :])
            mbf_at[c] = cv.n

        def u_y1(c):
            vw = cviews[c]; d3 = c % 3
            psm1_take()
            cv.wait(T, at=mbf_at[c])
            cv.wait(T, at=ktil_at[c])
            for h in range(8):
                _tmm = T.matmul(ps_m1[:, h * 128:(h + 1) * 128], vw["kt"][:, h, :],
                                Mbf[:, h, :], start=True, stop=True)
            ct_.inc(_tmm)
            buf_read[f"KT_{d3}"] = ct_.n
            buf_read["Mbf"] = ct_.n
            ct_.wait(V)
            ca.wait(V, at=vtm_at[c])
            ct_.wait(V, at=buf_read.get("R0", 0))
            V.tensor_sub(nWy[:, :, :], vw["vt"], _as3(ps_m1[:, 0:1024]))
            V.tensor_mul(R0t[:, :, :], nWy[:, :, :], _bview3(vw["b"]))
            vread[f"VT_{d3}"] = cv.n
            y_r0[c] = cv.n
            psm1_s["v"] = cv.n

        def u_y2(c):
            d2 = c % 2
            psm1_take()
            bwait(T, f"V6_{d2}")
            cv.wait(T, at=y_r0[c])
            for h in range(8):
                _tmm = T.matmul(ps_m1[:, h * 128:(h + 1) * 128], V6T2[:, d2][:, h, :],
                                R0t[:, h, :], start=True, stop=True)
            ct_.inc(_tmm)
            buf_read[f"V6_{d2}"] = ct_.n
            buf_read["R0"] = ct_.n
            ct_.wait(A)
            ct_.wait(A, at=buf_read.get("Ut", 0))
            cv.wait(A, at=vread.get("Ut", 0))
            A.activation(Ut[:, :, :], ps_m1[:, 0:1024], AF.Copy, bias=0.0, scale=1.0)
            u_at[c] = ca.n
            psm1_s["a"] = ca.n

        def u_y3(c):
            vw = cviews[c]; ta = tas[c]; d3 = c % 3
            psm1_take()
            ca.wait(T, at=u_at[c])
            cv.wait(T, at=ktil_at[c])
            cp.wait(T, at=stmask_at[c])
            for h in range(8):
                _tmm = T.matmul(ps_m1[:, h * 128:(h + 1) * 128], Mbf[:, h, :],
                                vw["qt"][:, h, :], start=True, stop=False)
                _tmm = T.matmul(ps_m1[:, h * 128:(h + 1) * 128], Ut[:, h, :],
                                vw["st"][:, h, :], start=False, stop=True)
            ct_.inc(_tmm)
            buf_read[f"QT_{d3}"] = ct_.n
            buf_read[f"ST_{d3}"] = ct_.n
            buf_read["Ut"] = ct_.n
            buf_read["Mbf"] = ct_.n
            ct_.wait(A)
            if c == 0 and sc > 0:
                cog.wait(A, at=og_stored)
            A.activation(og_sc[:, :, ta], ps_m1[:, 0:1024], AF.Copy, bias=0.0, scale=1.0)
            psm1_s["a"] = ca.n

        def u_y4(c):
            vw = cviews[c]; d3 = c % 3
            ca.wait(V, at=etm_at[c])
            ct_.wait(V, at=buf_read.get("R0", 0))
            ca.wait(V, at=u_at[c])
            V.tensor_mul(R0t[:, :, :], Ut[:, :, :], _bview3(vw["e"]))
            vread["Ut"] = cv.n
            ee2_at = cv.n
            psm1_take()
            ca.wait(T, at=ktm_at[c])
            cv.wait(T, at=ee2_at)
            for h in range(8):
                _tmm = T.matmul(ps_m1[:, h * 128:(h + 1) * 128], vw["km"][:, h, :],
                                R0t[:, h, :], start=True, stop=True)
            ct_.inc(_tmm)
            buf_read[f"KM_{d3}"] = ct_.n
            buf_read["R0"] = ct_.n
            ct_.wait(V)
            V.tensor_mul(nWy[:, :, :], M_f32[:, :, :], _bview3(vw["g"]))
            V.tensor_add(M_f32[:, :, :], nWy[:, :, :], _as3(ps_m1[:, 0:1024]))
            psm1_s["v"] = cv.n

        chq = []
        lfq = []

        def cpop():
            if lfq:
                lfq.pop(0)()
            elif chq:
                chq.pop(0)()

        cx.wait(T)
        cv.wait(T)
        ca.wait(T)
        for k in range(KCH):
            _tmm = T.matmul(psA[0:1, 0:512], ones128_bf[:, :], qhat[:, k, :],
                     start=(k == 0), stop=(k == KCH - 1))
        ct_.inc(_tmm)
        cx.wait(A)
        if sc > 0:
            ct_.wait(A)
        for k in range(KCH):
            A.activation(ln1[:, k, :], qhat[:, k, :], AF.Square, bias=0.0, scale=1.0)
            _sq = ca.n
            ca.wait(T, at=_sq)
            _tmm = T.matmul(psA[32:33, 0:512], ones128_bf[:, :], ln1[:, k, :],
                     start=(k == 0), stop=(k == KCH - 1))
        ct_.inc(_tmm)
        A.nop()
        stats_epilogue(psA[0:1, 0:512], psA[32:33, 0:512], mrep[:, :], rrep[:, :], 1e-5)
        ca.wait(V)

        w_loaded = {}

        def w_dma(ci, buf):
            if sc == 0:
                srcw = (wqkv[ci] if ci < N_CT else wz[ci - N_CT]).rearrange("p k c -> p (k c)")
                qe = P
            else:
                srcw = wbf_dram.ap()[ci]
                qe = S
            if ("w", buf) in w_used:
                ct_.wait(qe, at=w_used[("w", buf)])
            if buf in wb_at:
                cwb.wait(qe, at=wb_at[buf])
            cw_s[buf].inc(qe.dma_start(out=wslab[:, buf, :], in_=srcw), 16)
            w_loaded[buf] = cw_s[buf].n
            if sc == 0:
                cw_s[buf].wait(S, at=w_loaded[buf])
                cwb.inc(S.dma_start(out=wbf_dram.ap()[ci], in_=wslab[:, buf, :]), 16)
                wb_at[buf] = cwb.n

        for _pf in range(2):
            w_dma(_pf, _pf)

        cv.wait(T)
        ca.wait(T)
        cx.wait(T)
        for k in range(KCH):
            V.tensor_sub(ln1[:, k, :], qhat[:, k, :], mrep[:, :])
            V.tensor_mul(ln1[:, k, :], ln1[:, k, :], rrep[:, :])
            _lnk = cv.n
            cv.wait(T, at=_lnk)
            _tmm = T.matmul(psA[0:8, 0:512], wba_sb[:, 0, k, :], ln1[:, k, :],
                     start=(k == 0), stop=(k == KCH - 1))
            _tmm = T.matmul(psA[32:40, 0:512], wba_sb[:, 1, k, :], ln1[:, k, :],
                     start=(k == 0), stop=(k == KCH - 1))
        ct_.inc(_tmm)
        lnv = cv.n
        ct_.wait(V)
        ct_.wait(A)
        A.activation(bfm[:, :], psA[0:8, 0:512], AF.Sigmoid, bias=0.0, scale=1.0)
        V.tensor_scalar_mul(sp8a[:, :], psA[0:8, 0:512], -1.0)
        cv.wait(A)
        A.activation(sp8b[:, :], sp8a[:, :], AF.Abs, bias=0.0, scale=1.0)
        ca.wait(V)
        V.tensor_scalar_max(sp8a[:, :], sp8a[:, :], 0.0)
        cv.wait(A)
        A.activation(sp8b[:, :], sp8b[:, :], AF.Exp, bias=0.0, scale=-1.0)
        A.activation(sp8b[:, :], sp8b[:, :], AF.Ln, bias=1.0, scale=1.0)
        ca.wait(V)
        V.tensor_add(sp8a[:, :], sp8a[:, :], sp8b[:, :])
        V.tensor_scalar_mul(lnbfm[:, :], sp8a[:, :], -1.0)
        V.tensor_scalar_add(sp8a[:, :], psA[32:40, 0:512], dtb_sb[:, :])
        ba_free = (ca.n, cv.n)
        cv.wait(A)
        A.activation(sp8b[:, :], sp8a[:, :], AF.Abs, bias=0.0, scale=1.0)
        ca.wait(V)
        V.tensor_scalar_max(sp8a[:, :], sp8a[:, :], 0.0)
        cv.wait(A)
        A.activation(sp8b[:, :], sp8b[:, :], AF.Exp, bias=0.0, scale=-1.0)
        A.activation(sp8b[:, :], sp8b[:, :], AF.Ln, bias=1.0, scale=1.0)
        ca.wait(V)
        V.tensor_add(sp8a[:, :], sp8a[:, :], sp8b[:, :])
        V.tensor_scalar_mul(gfm[:, :], sp8a[:, :], negA[:, :])


        projb = [ps_m1[:, 0:512], ps_m1[:, 512:1024], ps_b1[:, 0:512]]
        pevac = {}
        praw_evac = {}
        ca.wait(T)
        cv.wait(T)
        cp.wait(V)
        V.tensor_copy(qraw[:, :, 0:3], tails[:, :, 0:3])
        tails_in_v = cv.n
        cv.wait(P, at=tails_in_v)
        n_tiles = N_CT + Z_CT
        for cti in range(n_tiles):
            buf = cti % 3
            bank = cti % 3
            if cti + 2 < n_tiles:
                w_dma(cti + 2, (cti + 2) % 3)
            cw_s[buf].wait(T, at=w_loaded[buf])
            cv.wait(T, at=lnv)
            if bank in pevac:
                ca.wait(T, at=pevac[bank])
            ph = projb[bank]
            for k in range(KCH):
                _tmm = T.matmul(ph, wslab[:, buf, k * 128:(k + 1) * 128], ln1[:, k, :],
                                start=(k == 0), stop=(k == KCH - 1))
            ct_.inc(_tmm)
            w_used[("w", buf)] = ct_.n
            ct_.wait(A)
            if cti < N_CT:
                A.activation(qraw[:, cti, 3:515], ph, AF.Copy, bias=0.0, scale=1.0)
                praw_evac[cti] = ca.n
                eng = V
                ca.wait(eng, at=praw_evac[cti])
                w4 = [convw_sb[:, cti, j:j + 1] for j in range(4)]
                eng.tensor_scalar_mul(qhat[:, cti, :], qraw[:, cti, 0:512], w4[0])
                for j in range(1, 4):
                    eng.scalar_tensor_tensor(qhat[:, cti, :], qraw[:, cti, j:j + 512], w4[j],
                                             qhat[:, cti, :], ALU.mult, ALU.add)
            else:
                A.activation(z_sc[:, cti - N_CT, :], ph, AF.Copy, bias=0.0, scale=1.0)
            pevac[bank] = ca.n
        cv.wait(A)
        A.activation(qhat[:, 0:12, :], qhat[:, 0:12, :], AF.Silu, bias=0.0, scale=1.0)
        A.activation(qhat[:, 12:24, :], qhat[:, 12:24, :], AF.Silu, bias=0.0, scale=1.0)
        cv.wait(V, at=tails_in_v)
        V.tensor_copy(tails[:, :, 0:3], qraw[:, :, 512:515])

        ss_slots = [psA[0:1, 0:512], psA[32:33, 0:512], psA[64:65, 0:512],
                    ps_b2[0:1, 0:512], ps_b2[32:33, 0:512], ps_b2[64:65, 0:512],
                    ps_b2[0:1, 512:1024], ps_b2[32:33, 512:1024], ps_b2[64:65, 512:1024]]
        sq_at = {}
        ssum_at = {}
        sqrt_at = {}
        rec_at = {}
        rep_at = {}
        mul_at = {}
        ct_.wait(A)
        ca.wait(T)
        cv.wait(T)
        A.activation(ln1[:, 0:8, :], qhat[:, 0:8, :], AF.Square, bias=0.0, scale=1.0)
        sq_at[0] = ca.n
        A.activation(ln1[:, 8:16, :], qhat[:, 8:16, :], AF.Square, bias=0.0, scale=1.0)
        sq_at[1] = ca.n
        stat_bx = [stat_b[0:1, 0:512], stat_b[0:1, 512:1024],
                   qraw[0:1, 2, 3:515], qraw[0:1, 3, 3:515]]
        l2slots = [ps_b2[0:1, 0:512], ps_b2[32:33, 0:512], ps_b2[64:65, 0:512],
                   ps_b2[0:1, 512:1024], ps_b2[32:33, 512:1024], ps_b2[64:65, 512:1024]]
        q_repl = [ps_b1[:, 0:512], ps_b1[:, 512:1024]]
        psm1_s["a"] = ca.n; psm1_s["v"] = cv.n
        pstr_s["a"] = ca.n
        psa_s["a"] = ca.n; psa_s["v"] = cv.n
        c0_ = sc * NCH
        for i in range(QK_CT + 8):
            if i < QK_CT:
                cti = i
                sl = cti % 6
                ca.wait(T, at=sq_at[cti // 8])
                if cti >= 6:
                    ca.wait(T, at=sqrt_at[cti - 6])
                if cti < 3:
                    ca.wait(T, at=ba_free[0])
                    cv.wait(T, at=ba_free[1])
                ct_.inc(T.matmul(l2slots[sl], ones128_bf[:, :], ln1[:, cti, :],
                                 start=True, stop=True))
                ssum_at[cti] = ct_.n
            if i >= 2 and i - 2 < QK_CT:
                cti = i - 2
                sl = cti % 6
                f2 = cti % 2
                ct_.wait(A, at=ssum_at[cti])
                if cti >= 2:
                    cv.wait(A, at=rec_at[cti - 2])
                if cti < 8:
                    A.activation(stat_f[0:1, f2 * 512:(f2 + 1) * 512], l2slots[sl],
                                 AF.Sqrt, bias=128.0 * 1e-6, scale=128.0)
                else:
                    A.activation(stat_f[0:1, f2 * 512:(f2 + 1) * 512], l2slots[sl],
                                 AF.Sqrt, bias=1e-6, scale=1.0)
                sqrt_at[cti] = ca.n
            if i >= 3 and i - 3 < QK_CT:
                cti = i - 3
                f2 = cti % 2
                ca.wait(V, at=sqrt_at[cti])
                if cti >= 4:
                    ct_.wait(V, at=rep_at[cti - 4])
                V.reciprocal(stat_bx[cti % 4], stat_f[0:1, f2 * 512:(f2 + 1) * 512])
                rec_at[cti] = cv.n
            if i >= 8 and i - 8 < QK_CT:
                cti = i - 8
                ct_.wait(V, at=rep_at[cti])
                V.tensor_mul(qhat[:, cti, :], qhat[:, cti, :], q_repl[cti % 2])
                mul_at[cti] = cv.n
            if i >= 6 and i - 6 < QK_CT:
                cti = i - 6
                cv.wait(T, at=rec_at[cti])
                if cti >= 2:
                    cv.wait(T, at=mul_at[cti - 2])
                ct_.inc(T.matmul(q_repl[cti % 2], ones1_bf[:, :],
                                 stat_bx[cti % 4], start=True, stop=True))
                rep_at[cti] = ct_.n
            if i == 2:
                u_prep_open(c0_)
                u_p1(c0_)
            elif i == 5:
                u_p2(c0_)
            elif i == 14:
                u_p3a(c0_)
            elif i == 18:
                u_p3b(c0_)
            elif i == 20:
                u_p6a(c0_)

        l2done[sc] = cv.n

        if "proj" in dbg and sc == 0:
            cv.wait(S)
            ca.wait(S)
            cdbg.inc(S.dma_start(out=dbg_out["qhat"], in_=qhat[:, :, :]), 16)
            cdbg.inc(S.dma_start(out=dbg_out["zsc"], in_=z_sc[:, :, :]), 16)
            cdbg.inc(S.dma_start(out=dbg_out["bg"][:, 0, :], in_=bfm[:, :]), 16)
            cdbg.inc(S.dma_start(out=dbg_out["bg"][:, 1, :], in_=lnbfm[:, :]), 16)
            cdbg.inc(S.dma_start(out=dbg_out["bg"][:, 2, :], in_=gfm[:, :]), 16)

        if stop_after == "A0" and sc == 0:
            cdbg.wait(S)
            return nc, dbg_out

        _scA.__exit__(None, None, None)
        _scB = scope(f"B{sc}"); _scB.__enter__()
        bank_free[0] = (ca.n, cv.n)
        bank_free[1] = (ca.n, cv.n)
        for br in range(NCH + 2):
            pre0 = (br == 0)
            if br == 1 and cc_issued < sc:
                cog.wait(P, at=og_stored)
                ccc.inc(P.collective_compute(
                    "AllGather", ALU.bypass, replica_groups=PAIRS,
                    ins=[cc_in.ap()[sc - 1].opt()], outs=[cc_out.ap()[sc - 1].opt()]))
                cc_issued = sc
            base = sc * NCH
            has_p = br < NCH
            has_c = 1 <= br <= NCH
            has_y = 2 <= br
            cP, cC, cY = base + br, base + br - 1, base + br - 2
            lfq.extend(chq)
            chq = build_chain(cC) if has_c else []
            if has_y:
                u_y0(cY)
            cpop()
            if has_y:
                u_y1(cY)
            cpop()
            if has_p and not pre0:
                u_prep_open(cP)
                u_p1(cP)
            cpop()
            if has_p and not pre0:
                u_p2(cP)
            cpop()
            cpop()
            if has_p and not pre0:
                u_p3a(cP)
            while lfq:
                lfq.pop(0)()
            cpop()
            if has_y:
                u_y2(cY)
            cpop()
            if has_p and not pre0:
                u_p3b(cP)
            cpop()
            if has_y:
                u_y3(cY)
            cpop()
            if has_y:
                u_y4(cY)
            if has_p:
                u_p4(cP)
            cpop()
            cpop()
            if has_p:
                u_p5(cP)
            cpop()
            cpop()
            if has_p and not pre0:
                u_p6a(cP)
            cpop()
            if has_p:
                u_p6b(cP)
            cpop()
            cpop()
            if has_p:
                u_p7(cP)
            cpop()
            cpop()
            if br >= NCH:
                while lfq or chq:
                    cpop()
            if br == NCH and sc + 1 < NSC:
                ct_.wait(P)
                cv.wait(P)
                ca.wait(P)
                cx.inc(P.dma_start(
                    out=qhat[:, 0:16, :],
                    in_=xT[:, (sc + 1) * SC:(sc + 2) * SC]
                    .rearrange("(k p) t -> p k t", p=128)), 16)

        osq = ln1[:, 0:8, :]
        zsil = ln1[:, 8:16, :]
        ca.wait(T)
        cv.wait(T)
        ct_.wait(A)
        A.activation(osq[:, :, :], og_sc[:, :, :], AF.Square, bias=0.0, scale=1.0)
        osq_at = ca.n
        A.activation(zsil[:, :, :], z_sc[:, :, :], AF.Silu, bias=0.0, scale=1.0)
        zsil_at = ca.n
        ssum_at = {}
        sqrt_at = {}
        rec_at = {}
        rep_at = {}
        mul_at = {}
        o_repl = [ps_b1[:, 0:512], ps_b1[:, 512:1024], ps_m1[:, 0:512], ps_m1[:, 512:1024]]
        ca.wait(T, at=osq_at)
        for i in range(16):
            if i < 8:
                h = i
                ct_.inc(T.matmul(ss_slots[h], ones128_bf[:, :], osq[:, h, :],
                                 start=True, stop=True))
                ssum_at[h] = ct_.n
                if h == 7:
                    ct_.wait(V)
                    ca.wait(V, at=zsil_at)
                    V.scalar_tensor_tensor(osq[:, :, :], og_sc[:, :, :], normw_sb[:, :],
                                           zsil[:, :, :], ALU.mult, ALU.mult)
            if i >= 2 and i - 2 < 8:
                h = i - 2
                f2 = h % 2
                ct_.wait(A, at=ssum_at[h])
                if h >= 2:
                    cv.wait(A, at=rec_at[h - 2])
                A.activation(stat_f[0:1, f2 * 512:(f2 + 1) * 512], ss_slots[h],
                             AF.Sqrt, bias=1e-6, scale=1.0 / 128.0)
                sqrt_at[h] = ca.n
            if i >= 3 and i - 3 < 8:
                h = i - 3
                f2 = h % 2
                ca.wait(V, at=sqrt_at[h])
                if h >= 2:
                    ct_.wait(V, at=rep_at[h - 2])
                V.reciprocal(stat_b[0:1, f2 * 512:(f2 + 1) * 512],
                             stat_f[0:1, f2 * 512:(f2 + 1) * 512])
                rec_at[h] = cv.n
            if i >= 8 and i - 8 < 8:
                h = i - 8
                ct_.wait(V, at=rep_at[h])
                V.tensor_mul(og_sc[:, h, :], osq[:, h, :], o_repl[h % 4])
                mul_at[h] = cv.n
            if i >= 4 and i - 4 < 8:
                h = i - 4
                f2 = h % 2
                cv.wait(T, at=rec_at[h])
                if h >= 4:
                    cv.wait(T, at=mul_at[h - 4])
                ct_.inc(T.matmul(o_repl[h % 4], ones1_bf[:, :],
                                 stat_b[0:1, f2 * 512:(f2 + 1) * 512], start=True, stop=True))
                rep_at[h] = ct_.n

        cv.wait(S)
        cog.inc(S.dma_start(
            out=cc_in.ap()[sc].rearrange("(h p) t -> p h t", p=128),
            in_=og_sc[:, :, :]), 16)
        og_stored = cog.n
        _scB.__exit__(None, None, None)

    cog.wait(P, at=og_stored)
    ccc.inc(P.collective_compute(
        "AllGather", ALU.bypass, replica_groups=PAIRS,
        ins=[cc_in.ap()[NSC - 1].opt()], outs=[cc_out.ap()[NSC - 1].opt()]))
    for ctr in (ct_, cv, ca, cp, cx, *cw_s, cog, csd):
        for eng in (T, V, A, P, S):
            ctr.wait(eng)
    sbctx_ab.close()
    sbctx_w = contextlib.ExitStack()
    wgsl = sbctx_w.enter_context(nc.sbuf_tensor("wgsl", [128, 2, KCH * 128], BF16))
    wusl = sbctx_w.enter_context(nc.sbuf_tensor("wusl", [128, 2, KCH * 128], BF16))
    wg8 = sbctx_w.enter_context(nc.sbuf_tensor("wg8", [128, 2, KCH * 128], U8))
    wu8 = sbctx_w.enter_context(nc.sbuf_tensor("wu8", [128, 2, KCH * 128], U8))
    gu_loaded = {}
    cast_a = {}
    cast_v = {}

    def gu_dma(it, buf):
        sl8 = it % 2
        qe = S if sl8 == 0 else P
        if sl8 in cast_a:
            ca.wait(qe, at=cast_a[sl8][1])
            cv.wait(qe, at=cast_v[sl8][1])
        cgu_s[sl8].inc(qe.dma_start(out=wg8[:, sl8, :], in_=wg_in[it].rearrange("p k c -> p (k c)")), 16)
        cgu_s[sl8].inc(qe.dma_start(out=wu8[:, sl8, :], in_=wu_in[it].rearrange("p k c -> p (k c)")), 16)
        gu_loaded[sl8] = cgu_s[sl8].n

    def gu_cast(it, buf):
        sl8 = it % 2
        if ("g", buf) in w_used:
            ct_.wait(A, at=w_used[("g", buf)])
            ct_.wait(V, at=w_used[("g", buf)])
        cgu_s[sl8].wait(A, at=gu_loaded[sl8])
        cgu_s[sl8].wait(V, at=gu_loaded[sl8])
        A.activation(wgsl[:, buf, :], wg8[:, sl8, :], AF.Copy, bias=0.0, scale=1.0)
        cast_a[sl8] = (buf, ca.n)
        V.tensor_copy(wusl[:, buf, :], wu8[:, sl8, :])
        cast_v[sl8] = (buf, cv.n)

    og_my = sb2("og_my", [128, NH, TOKH], BF16)
    og_peer = sb2("og_peer", [128, NH, TOKH], BF16)
    x_half = sb2("x_half", [128, KCH, TOKH], F32)
    araw = sb2("araw", [128, KCH, TOKH], BF16)
    wosl = sb2("wosl", [128, 2, 16 * 128], BF16)
    dsq2 = og_my[:, 2:4, :]
    dcp2 = og_my[:, 0:2, :]
    mrepD = og_my[:, 4, :]
    rrepD = og_my[:, 5, :]
    tD = ps_b1[:, 0:1024]
    hgv = pid % 2
    ccin_r = cc_in.ap().rearrange("s (h p) t -> p (s h) t", p=128)
    ccout_r = cc_out.ap().rearrange("s (b h p) t -> p (s b h) t", p=128, b=2)
    for i in range(2):
        cph.inc(P.dma_start(out=og_my[:, :, i * SC:(i + 1) * SC],
                            in_=ccin_r[:, ds((2 * hgv + i) * NH, NH), :]), 16)

    _scD = scope("D"); _scD.__enter__()
    wo_loaded = {}

    def wout_dma(dt, buf):
        if ("wo", buf) in w_used:
            ct_.wait(P, at=w_used[("wo", buf)])
        cwo_s[buf].inc(P.dma_start(out=wosl[:, buf, :], in_=wout[dt].rearrange("p v d -> p (v d)")), 16)
        wo_loaded[buf] = cwo_s[buf].n

    wout_dma(0, 0)
    wout_dma(1, 1)
    ccc.wait(P)
    for i in range(2):
        cph.inc(P.dma_start(
            out=og_peer[:, :, i * SC:(i + 1) * SC],
            in_=ccout_r[:, ds((2 * hgv + i) * 2 * NH + (1 - hgv) * NH, NH), :]), 16)
    og_done = cph.n
    cph.inc(P.dma_start(out=x_half[:, :, :],
                        in_=xT[:, ds(half_off, TOKH)].rearrange("(k p) t -> p k t", p=128)), 16)
    rotD = [ps_m1, ps_b1]
    sq_at = {}
    stat_at = {}

    def _sqv(dt):
        return araw[:, dt + 1, :] if dt + 1 < DT_FF else og_my[:, 2, :]

    def _araw_stats(dt):
        ca.wait(T, at=sq_at[dt])
        sv = _sqv(dt)
        _tmm = T.matmul(psA[0:1, 0:512], ones128_bf[:, :], araw[:, dt, 0:512],
                        start=(dt == 0), stop=(dt == DT_FF - 1))
        _tmm = T.matmul(psA[64:65, 0:512], ones128_bf[:, :], araw[:, dt, 512:1024],
                        start=(dt == 0), stop=(dt == DT_FF - 1))
        _tmm = T.matmul(psA[32:33, 0:512], ones128_bf[:, :], sv[:, 0:512],
                        start=(dt == 0), stop=(dt == DT_FF - 1))
        _tmm = T.matmul(ps_b2[0:1, 0:512], ones128_bf[:, :], sv[:, 512:1024],
                        start=(dt == 0), stop=(dt == DT_FF - 1))
        ct_.inc(_tmm)
        stat_at[dt % 2] = ct_.n

    for dt in range(DT_FF):
        buf = dt % 2
        if 1 <= dt and dt + 1 < DT_FF:
            wout_dma(dt + 1, (dt + 1) % 2)
        cph.wait(T, at=og_done)
        cwo_s[buf].wait(T, at=wo_loaded[buf])
        if dt >= 2:
            ca.wait(T)
        pb = rotD[dt % 2]
        for vc in range(16):
            og_src = og_my[:, vc, :] if vc < 8 else og_peer[:, vc - 8, :]
            lhs = wosl[:, buf, vc * 128:(vc + 1) * 128]
            _tmm = T.matmul(pb[:, 0:512], lhs, og_src[:, 0:512], start=(vc == 0), stop=(vc == 15))
            _tmm = T.matmul(pb[:, 512:1024], lhs, og_src[:, 512:1024], start=(vc == 0), stop=(vc == 15))
        ct_.inc(_tmm)
        w_used[("wo", buf)] = ct_.n
        if dt >= 1:
            _araw_stats(dt - 1)
        ct_.wait(A)
        A.activation(araw[:, dt, :], pb[:, 0:1024], AF.Copy, bias=0.0, scale=1.0)
        A.activation(_sqv(dt), araw[:, dt, :], AF.Square, bias=0.0, scale=1.0)
        sq_at[dt] = ca.n
    _araw_stats(DT_FF - 1)

    gu_dma(0, 0)
    gu_dma(1, 1)
    stats_epilogue(psA[0:1, 0:512], psA[32:33, 0:512],
                   mrepD[:, 0:512], rrepD[:, 0:512], 1e-5, repl_ps=ps_b1)
    stats_epilogue(psA[64:65, 0:512], ps_b2[0:1, 0:512],
                   mrepD[:, 512:1024], rrepD[:, 512:1024], 1e-5, repl_ps=ps_b1)

    ca.wait(V)
    cph.wait(V)
    ca.wait(T)
    cv.wait(T)
    for k in range(KCH):
        db = k % 2
        V.tensor_sub(tD[:, :], araw[:, k, :], mrepD[:, :])
        V.tensor_mul(tD[:, :], tD[:, :], rrepD[:, :])
        V.tensor_add(x_half[:, k, :], x_half[:, k, :], tD[:, :])
        V.engine_nop()
        mid_at = cv.n
        cv.wait(A, at=mid_at)
        if k >= 2:
            ct_.wait(A, at=stat_at[db])
        A.activation(dcp2[:, db, :], x_half[:, k, :], AF.Copy, bias=0.0, scale=1.0)
        A.activation(dsq2[:, db, :], dcp2[:, db, :], AF.Square, bias=0.0, scale=1.0)
        sq_at[db] = ca.n
        ca.wait(T, at=sq_at[db])
        _tmm = T.matmul(psA[0:1, 0:512], ones128_bf[:, :], dcp2[:, db, 0:512],
                        start=(k == 0), stop=(k == KCH - 1))
        _tmm = T.matmul(psA[64:65, 0:512], ones128_bf[:, :], dcp2[:, db, 512:1024],
                        start=(k == 0), stop=(k == KCH - 1))
        _tmm = T.matmul(psA[32:33, 0:512], ones128_bf[:, :], dsq2[:, db, 0:512],
                        start=(k == 0), stop=(k == KCH - 1))
        _tmm = T.matmul(ps_b2[0:1, 0:512], ones128_bf[:, :], dsq2[:, db, 512:1024],
                        start=(k == 0), stop=(k == KCH - 1))
        ct_.inc(_tmm)
        stat_at[db] = ct_.n
    if "mid" in dbg:
        ca.wait(S)
        cv.wait(S)
        cdbg.inc(S.dma_start(out=dbg_out["araw"], in_=araw[:, :, :]), 16)
        cdbg.inc(S.dma_start(out=dbg_out["xmid"], in_=x_half[:, :, :]), 16)

    stats_epilogue(psA[0:1, 0:512], psA[32:33, 0:512],
                   mrepD[:, 0:512], rrepD[:, 0:512], 1e-5, repl_ps=ps_b1)
    stats_epilogue(psA[64:65, 0:512], ps_b2[0:1, 0:512],
                   mrepD[:, 512:1024], rrepD[:, 512:1024], 1e-5, repl_ps=ps_b1)

    gu_cast(0, 0)
    gu_cast(1, 1)
    tD2 = ps_b2[:, 0:1024]
    ca.wait(V)
    ca.wait(T)
    cv.wait(T)
    ca.wait(T, at=cast_a[0][1])
    cv.wait(T, at=cast_v[0][1])
    ca.wait(T, at=cast_a[1][1])
    cv.wait(T, at=cast_v[1][1])
    pbE2 = [ps_m1, ps_b1]
    for k in range(KCH):
        V.tensor_sub(tD2[:, :], x_half[:, k, :], mrepD[:, :])
        V.tensor_mul(ln2[:, k, :], tD2[:, :], rrepD[:, :])
        ln2_at = cv.n
        cv.wait(T, at=ln2_at)
        _tmm = T.matmul(psA[0:1, 0:512], ones128_bf[:, :], ln2[:, k, 0:512],
                        start=(k == 0), stop=(k == KCH - 1))
        _tmm = T.matmul(psA[32:33, 0:512], ones128_bf[:, :], ln2[:, k, 512:1024],
                        start=(k == 0), stop=(k == KCH - 1))
        for itt in range(2):
            pbE = pbE2[itt]
            _tmm = T.matmul(pbE[:, 0:512], wgsl[:, itt, k * 128:(k + 1) * 128],
                            ln2[:, k, 0:512], start=(k == 0), stop=(k == KCH - 1))
            _tmm = T.matmul(pbE[:, 512:1024], wusl[:, itt, k * 128:(k + 1) * 128],
                            ln2[:, k, 0:512], start=(k == 0), stop=(k == KCH - 1))
        ct_.inc(_tmm)
    V.engine_nop()
    if "ln2" in dbg:
        cv.wait(S)
        cdbg.inc(S.dma_start(out=dbg_out["ln2"], in_=ln2[:, :, :]), 16)
    cv.wait(S)
    cspill.inc(S.dma_start(out=xmid_dram.ap().rearrange("(k p) t -> p k t", p=128),
                         in_=x_half[:, :, :]), 16)

    if stop_after == "D":
        cdbg.wait(S)
        cspill.wait(S)
        return nc, dbg_out

    _scD.__exit__(None, None, None)
    _scE = scope("E"); _scE.__enter__()
    xmid_used = {}
    out_stored = {}
    for ctr in (ct_, cv, ca, cp, cph, cwo_s[0], cwo_s[1], cspill, cdbg):
        for eng in (T, V, A, P, S):
            ctr.wait(eng)
    sbctx_2.close()
    h2 = sb3("h2", [128, IT_FF, FGRP], BF16)
    wdsl = sb3("wdsl", [128, 4, 16 * 128], BF16)
    gact = sb3("gact", [128, 2, FGRP], BF16)
    udeq = sb3("udeq", [128, FGRP], BF16)
    draw = sb3("draw", [128, DT_FF, FGRP], BF16)
    rs2grp = sb3("rs2grp", [1, 2, FGRP], BF16)
    rsrep = sb3("rsrep", [128, FGRP], BF16)
    rs2rep = sb3("rs2rep", [128, FGRP], BF16)
    tEu = sb3("tEu", [128, FGRP], F32)
    rs2_bf = sb3("rs2_bf", [1, FGRP], BF16)
    xmid_sb = sb3("xmid_sb", [128, 2, FGRP], F32)
    outb = sb3("outb", [128, 2, FGRP], F32)
    tE = sb3("tE", [128, FGRP], F32)
    dsq3 = sb3("dsq3", [128, 2, 512], BF16)
    mrepE = sb3("mrepE", [128, 512], BF16)
    rrepE = sb3("rrepE", [128, 512], BF16)
    ct_.wait(V)
    V.tensor_copy(rs2grp[0:1, 0, :], psA[0:1, 0:512])
    V.tensor_copy(rs2grp[0:1, 1, :], psA[32:33, 0:512])
    for grp in range(2):
        g0 = grp * FGRP
        rs_bf = rs2grp[0:1, grp, :]
        cv.wait(T)
        ca.wait(T)
        ct_.inc(T.matmul(psA[:, 0:512], ones1_bf[:, :], rs_bf, start=True, stop=True))
        ct_.wait(A)
        A.activation(rsrep[:, :], psA[:, 0:512], AF.Copy, bias=0.0, scale=128.0)
        rsrep_at = ca.n

        if grp == 1:
            gu_cast(0, 0)
        rotE = [ps_m1, ps_b1, ps_b2]
        evac_done = {}
        gact_used = {}
        h2at = {}
        cv.wait(T)
        ca.wait(T, at=rsrep_at)
        for it in range(IT_FF):
            buf = it % 2
            if it + 2 < IT_FF:
                gu_dma(it + 2, 0)
            if it + 1 < IT_FF and not (grp == 0 and it + 1 < 2):
                gu_cast(it + 1, (it + 1) % 2)
            bank = it % 3
            if bank in evac_done:
                ca.wait(T, at=evac_done[bank][0])
                cv.wait(T, at=evac_done[bank][1])
            if it >= 1:
                cv.wait(T, at=h2at[it - 1])
                ct_.inc(T.matmul(psA[0:1, 0:512], ones128_bf[:, :], h2[:, it - 1, :],
                                 start=(it == 1), stop=False))
            pb = rotE[bank]
            if grp == 0 and it < 2:
                w_used[("g", buf)] = ct_.n
            else:
                ca.wait(T, at=cast_a[it % 2][1])
                cv.wait(T, at=cast_v[it % 2][1])
                for k in range(KCH):
                    _tmm = T.matmul(pb[:, 0:512], wgsl[:, buf, k * 128:(k + 1) * 128],
                             ln2[:, k, g0:g0 + 512], start=(k == 0), stop=(k == KCH - 1))
                ct_.inc(_tmm)
                for k in range(KCH):
                    _tmm = T.matmul(pb[:, 512:1024], wusl[:, buf, k * 128:(k + 1) * 128],
                             ln2[:, k, g0:g0 + 512], start=(k == 0), stop=(k == KCH - 1))
                ct_.inc(_tmm)
                w_used[("g", buf)] = ct_.n
            ct_.wait(V)
            ca.wait(V, at=silu_at if it >= 1 else rsrep_at)
            V.tensor_sub(tE[:, :], pb[:, 0:512], rsrep[:, :])
            V.tensor_sub(tEu[:, :], pb[:, 512:1024], rsrep[:, :])
            sub_at = cv.n
            gb = it % 2
            if gb in gact_used:
                cv.wait(A, at=gact_used[gb])
            cv.wait(A, at=sub_at)
            A.activation(gact[:, gb, :], tE[:, :], AF.Silu, bias=0.0,
                                scale=sg_sb[:, it:it + 1])
            silu_at = ca.n
            ca.wait(V)
            V.scalar_tensor_tensor(h2[:, it, :], tEu[:, :], su_sb[:, it:it + 1],
                                   gact[:, gb, :], ALU.mult, ALU.mult)
            gact_used[gb] = cv.n
            h2at[it] = cv.n
            evac_done[bank] = (ca.n, sub_at)
        cv.wait(T, at=h2at[IT_FF - 1])
        ct_.inc(T.matmul(psA[0:1, 0:512], ones128_bf[:, :], h2[:, IT_FF - 1, :],
                         start=False, stop=True))
        ct_.wait(V)
        V.tensor_copy(rs2_bf[0:1, :], psA[0:1, 0:512])
        cv.wait(T)
        ct_.inc(T.matmul(psA[:, 0:512], ones1_bf[:, :], rs2_bf[0:1, :], start=True, stop=True))
        ct_.wait(A)
        A.activation(rs2rep[:, :], psA[:, 0:512], AF.Copy, bias=0.0, scale=128.0)
        ca.wait(V)

        wd_loaded = {}

        def wd_dma(dt, icg, buf):
            if ("d", buf) in w_used:
                ct_.wait(P, at=w_used[("d", buf)])
            cwd_s[buf].inc(P.dma_start(out=wdsl[:, buf, :],
                                in_=wd_in[dt, :, icg * 16:(icg + 1) * 16, :]
                                .rearrange("p c d -> p (c d)")), 16)
            wd_loaded[buf] = cwd_s[buf].n

        for _pf in range(3):
            wd_dma(0, _pf, _pf)
        if grp == 0:
            gu_dma(0, 0)
            gu_dma(1, 1)
        nslab = 0
        for dt in range(DT_FF):
            bank = dt % 3
            pb = rotE[bank]
            if bank in evac_done:
                ca.wait(T, at=evac_done[bank][0])
                cv.wait(T, at=evac_done[bank][1])
            for icg in range(4):
                buf = nslab % 4
                if nslab + 3 < DT_FF * 4:
                    nxt = nslab + 3
                    wd_dma(nxt // 4, nxt % 4, nxt % 4)
                cwd_s[buf].wait(T, at=wd_loaded[buf])
                for ici in range(16):
                    ic = icg * 16 + ici
                    _tmm = T.matmul(pb[:, 0:512], wdsl[:, buf, ici * 128:(ici + 1) * 128], h2[:, ic, :],
                             start=(ic == 0), stop=(ic == IT_FF - 1))
                ct_.inc(_tmm)
                w_used[("d", buf)] = ct_.n
                nslab += 1
            ct_.wait(V)
            V.tensor_sub(tE[:, :], pb[:, 0:512], rs2rep[:, :])
            V.tensor_scalar(draw[:, dt, :], tE[:, :], sd_sb[:, dt:dt + 1],
                                   None, ALU.mult)
            evac_done[bank] = (ca.n, cv.n)
        ca.wait(T)
        cv.wait(T)
        cv.wait(A)
        for k in range(DT_FF):
            db = k % 2
            if k >= 2:
                ct_.wait(A, at=sq_used[db])
            A.activation(dsq3[:, db, :], draw[:, k, :], AF.Square, bias=0.0, scale=1.0)
            ca.wait(T)
            _tmm = T.matmul(psA[0:1, 0:512], ones128_bf[:, :], draw[:, k, :],
                     start=(k == 0), stop=(k == DT_FF - 1))
            _tmm = T.matmul(psA[32:33, 0:512], ones128_bf[:, :], dsq3[:, db, :],
                     start=(k == 0), stop=(k == DT_FF - 1))
            ct_.inc(_tmm)
            if k == 0:
                sq_used = {}
            sq_used[db] = ct_.n
        stats_epilogue(psA[0:1, 0:512], psA[32:33, 0:512],
                       mrepE[:, :], rrepE[:, :], 1e-5)
        cspill.wait(P)
        xmid_view = xmid_dram.ap().rearrange("(k p) t -> p k t", p=128)
        cxm_s[0].inc(P.dma_start(out=xmid_sb[:, 0, :], in_=xmid_view[:, 0, g0:g0 + 512]), 16)
        xm_loaded = {0: cxm_s[0].n}
        for k in range(KCH):
            bo = k % 2
            if k + 1 < KCH:
                nb = (k + 1) % 2
                if nb in xmid_used:
                    cv.wait(P, at=xmid_used[nb])
                cxm_s[nb].inc(P.dma_start(out=xmid_sb[:, nb, :],
                                    in_=xmid_view[:, k + 1, g0:g0 + 512]), 16)
                xm_loaded[nb] = cxm_s[nb].n
            ca.wait(V)
            cxm_s[bo].wait(V, at=xm_loaded[bo])
            if bo in out_stored:
                cyb[bo].wait(V)
            V.tensor_sub(tE[:, :], draw[:, k, :], mrepE[:, :])
            V.tensor_mul(tE[:, :], tE[:, :], rrepE[:, :])
            V.tensor_add(outb[:, bo, :], xmid_sb[:, bo, :], tE[:, :])
            V.engine_nop()
            xmid_used[bo] = cv.n
            cv.wait(S)
            cyb[bo].inc(S.dma_start(out=y_out.rearrange("(k p) t -> p k t", p=128)[:, k, g0:g0 + 512],
                                    in_=outb[:, bo, :]), 16)
            out_stored[bo] = cyb[bo].n
    cyb[0].wait(S)
    cyb[1].wait(S)
    cdbg.wait(S)
    _scE.__exit__(None, None, None)
    return nc, dbg_out


def _prep_inputs(inputs, c):
    b, hg = c // 2, c % 2
    x = inputs["L0_attn_x"]
    Wqkv = np.asarray(inputs["L0_attn_Wqkv"], np.float32)
    Wz = np.asarray(inputs["L0_attn_Wz"], np.float32)
    Wb = np.asarray(inputs["L0_attn_Wb"], np.float32)
    Wa = np.asarray(inputs["L0_attn_Wa"], np.float32)
    conv_w = np.asarray(inputs["L0_attn_conv_w"], np.float32)
    Wout = np.asarray(inputs["L0_attn_Wout"], np.float32)
    hsl = slice(hg * 1024, hg * 1024 + 1024)
    qs, ks, vs = Wqkv[:, 0:2048][:, hsl], Wqkv[:, 2048:4096][:, hsl], Wqkv[:, 4096:6144][:, hsl]
    Wslice = np.concatenate([qs, ks, vs], axis=1)
    wqkv_t = np.ascontiguousarray(Wslice.reshape(16, 128, 24, 128).transpose(2, 1, 0, 3))
    wz_t = np.ascontiguousarray(Wz[:, hsl].reshape(16, 128, 8, 128).transpose(2, 1, 0, 3))
    hh = slice(hg * 8, hg * 8 + 8)
    wba_t = np.ascontiguousarray(
        np.stack([Wb[:, hh].reshape(16, 128, 8), Wa[:, hh].reshape(16, 128, 8)])
        .transpose(2, 0, 1, 3))
    cw = conv_w[:, 0, :]
    cws = np.concatenate([cw[0:2048][hsl], cw[2048:4096][hsl], cw[4096:6144][hsl]], axis=0)
    convw_t = np.ascontiguousarray(cws.reshape(24, 128, 4).transpose(1, 0, 2))
    Wp = np.concatenate([Wout[hg * 1024:hg * 1024 + 1024],
                         Wout[(1 - hg) * 1024:(1 - hg) * 1024 + 1024]], axis=0)
    wout_t = np.ascontiguousarray(Wp.reshape(16, 128, 16, 128).transpose(2, 1, 0, 3))
    return {
        "xT": np.ascontiguousarray(np.asarray(x, np.float32)[b].T),
        "wqkv": wqkv_t, "wz": wz_t, "wba": wba_t, "convw": convw_t,
        "alog": np.ascontiguousarray(
            np.asarray(inputs["L0_attn_A_log"], np.float32)[hh].reshape(8, 1)),
        "dtb": np.ascontiguousarray(
            np.asarray(inputs["L0_attn_dt_bias"], np.float32)[hh].reshape(8, 1)),
        "normw": np.ascontiguousarray(
            np.asarray(inputs["L0_attn_norm_w"], np.float32).reshape(128, 1)),
        "wout": wout_t,
    }


def _prep_shared(inputs):
    def u8view(a):
        a = np.ascontiguousarray(a)
        assert a.dtype == np.int32
        return a.view(np.uint8).reshape(*a.shape, 4)[..., 0]

    qg = u8view(inputs["L0_ffn_gate_w"])
    qu = u8view(inputs["L0_ffn_up_w"])
    qd = u8view(inputs["L0_ffn_down_w"])
    wg_t = np.ascontiguousarray(qg.reshape(64, 128, 16, 128).transpose(0, 3, 2, 1))
    wu_t = np.ascontiguousarray(qu.reshape(64, 128, 16, 128).transpose(0, 3, 2, 1))
    wd_t = np.ascontiguousarray(qd.reshape(16, 128, 64, 128).transpose(0, 3, 2, 1))
    return {
        "wg": wg_t, "wu": wu_t, "wd": wd_t,
        "sg": np.ascontiguousarray(
            np.asarray(inputs["L0_ffn_gate_scale"], np.float32).reshape(64, 128).T),
        "su": np.ascontiguousarray(
            np.asarray(inputs["L0_ffn_up_scale"], np.float32).reshape(64, 128).T),
        "sd": np.ascontiguousarray(
            np.asarray(inputs["L0_ffn_down_scale"], np.float32).reshape(16, 128).T),
        "id128": np.eye(128, dtype=BF),
        "id8": np.eye(8, dtype=np.float32),
    }


def make_in_maps(inputs):
    shared = _prep_shared(inputs)
    in_maps = []
    for c in range(8):
        m = _prep_inputs(inputs, c)
        m.update(shared)
        in_maps.append(m)
    return in_maps


def kernel(**inputs):
    in_maps = make_in_maps(inputs)
    nc, _ = build_nc()
    res = run_bass_kernel_spmd(nc, in_maps, core_ids=list(range(8)))
    out = np.empty((4, 2048, 2048), np.float32)
    for c in range(8):
        b, hg = c // 2, c % 2
        out[b, hg * 1024:(hg + 1) * 1024, :] = res.results[c]["y"].T
    return out


# version 45
# speedup vs baseline: 1.1718x; 1.1718x over previous
import dataclasses
import numpy as np
import ml_dtypes
import concourse.bass as bass
import concourse.mybir as mybir
from concourse.bass import ds
from concourse.bass_utils import run_bass_kernel_spmd

F32 = mybir.dt.float32
BF16 = mybir.dt.bfloat16
U8 = mybir.dt.uint8
AF = mybir.ActivationFunctionType
ALU = mybir.AluOpType
BF = ml_dtypes.bfloat16

L = 2048
D = 2048
NH = 8
C = 128
SC = 512
NSC = L // SC
NCH = SC // C
KCH = D // 128
QK_CT = 16
N_CT = 24
Z_CT = 8
TOKH = 1024
IT_FF = 64
DT_FF = 16
FGRP = 512


class EngWrap:

    _PASS = {"wait_ge", "nop", "dma_start", "collective_compute", "partition_id",
             "indirect_dma_start", "sem_inc", "drain"}

    def __init__(self, eng, ctr):
        self._e = eng
        self._c = ctr

    def __getattr__(self, name):
        f = getattr(self._e, name)
        if name in self._PASS or name.startswith("_"):
            return f

        def g(*a, **k):
            self._e.wait_ge(self._c.h, self._c.n)
            r = f(*a, **k)
            self._c.inc(r)
            return r
        return g


class Ctr:
    def __init__(self, nc, name):
        self.h = nc.alloc_semaphore(name)
        self.n = 0
        self._w = {}

    def inc(self, instr, k=1):
        instr.then_inc(self.h, k)
        self.n += k
        return instr

    def pre(self, eng):
        eng.wait_ge(self.h, self.n)

    def wait(self, eng, at=None):
        n = self.n if at is None else at
        e = getattr(eng, "_e", eng)
        k = id(e)
        if self._w.get(k, 0) >= n:
            return
        e.wait_ge(self.h, n)
        self._w[k] = n


def _bview(ap, count):
    return dataclasses.replace(ap, ap=[ap.ap[0], [0, count], ap.ap[1]])


def _bview3(ap, inner=128):
    return dataclasses.replace(ap, ap=[ap.ap[0], ap.ap[1], [0, inner]])


def _as3(ap, h=8, inner=128):
    return dataclasses.replace(ap, ap=[ap.ap[0], [inner, h], [1, inner]])


def build_nc(dbg=(), stop_after=None):
    nc = bass.Bass("TRN2", target_bir_lowering=False)
    T, V, A, P, S = nc.tensor, nc.vector, nc.scalar, nc.gpsimd, nc.sync
    _lp = nc.allow_low_precision("bf16 operands with fp32 PSUM accumulation per problem tolerance")
    _lp.__enter__()

    xT = nc.dram_tensor("xT", [D, L], F32, kind="ExternalInput").ap()
    wqkv = nc.dram_tensor("wqkv", [N_CT, 128, KCH, 128], F32, kind="ExternalInput").ap()
    wz = nc.dram_tensor("wz", [Z_CT, 128, KCH, 128], F32, kind="ExternalInput").ap()
    wba = nc.dram_tensor("wba", [128, 2, KCH, 8], F32, kind="ExternalInput").ap()
    convw = nc.dram_tensor("convw", [128, N_CT, 4], F32, kind="ExternalInput").ap()
    alog = nc.dram_tensor("alog", [8, 1], F32, kind="ExternalInput").ap()
    dtb = nc.dram_tensor("dtb", [8, 1], F32, kind="ExternalInput").ap()
    normw = nc.dram_tensor("normw", [128, 1], F32, kind="ExternalInput").ap()
    wout = nc.dram_tensor("wout", [DT_FF, 128, 16, 128], F32, kind="ExternalInput").ap()
    wg_in = nc.dram_tensor("wg", [IT_FF, 128, KCH, 128], U8, kind="ExternalInput").ap()
    wu_in = nc.dram_tensor("wu", [IT_FF, 128, KCH, 128], U8, kind="ExternalInput").ap()
    wd_in = nc.dram_tensor("wd", [DT_FF, 128, IT_FF, 128], U8, kind="ExternalInput").ap()
    sg_in = nc.dram_tensor("sg", [128, IT_FF], F32, kind="ExternalInput").ap()
    su_in = nc.dram_tensor("su", [128, IT_FF], F32, kind="ExternalInput").ap()
    sd_in = nc.dram_tensor("sd", [128, DT_FF], F32, kind="ExternalInput").ap()
    id128_in = nc.dram_tensor("id128", [128, 128], BF16, kind="ExternalInput").ap()
    id8_in = nc.dram_tensor("id8", [8, 8], F32, kind="ExternalInput").ap()
    y_out = nc.dram_tensor("y", [D, TOKH], F32, kind="ExternalOutput").ap()
    dbg_out = {}
    if "proj" in dbg:
        dbg_out["qhat"] = nc.dram_tensor("dbg_qhat", [128, N_CT, SC], BF16, kind="ExternalOutput").ap()
        dbg_out["zsc"] = nc.dram_tensor("dbg_zsc", [128, Z_CT, SC], BF16, kind="ExternalOutput").ap()
        dbg_out["bg"] = nc.dram_tensor("dbg_bg", [8, 3, SC], F32, kind="ExternalOutput").ap()
    if "og" in dbg:
        dbg_out["og"] = nc.dram_tensor("dbg_og", [128, NH, L], BF16, kind="ExternalOutput").ap()
    if "mid" in dbg:
        dbg_out["araw"] = nc.dram_tensor("dbg_araw", [128, KCH, TOKH], BF16, kind="ExternalOutput").ap()
        dbg_out["xmid"] = nc.dram_tensor("dbg_xmid", [128, KCH, TOKH], F32, kind="ExternalOutput").ap()
    if "ln2" in dbg:
        dbg_out["ln2"] = nc.dram_tensor("dbg_ln2", [128, KCH, TOKH], BF16, kind="ExternalOutput").ap()

    dbgb = nc.dram_tensor("dbgb", [4, 128, 1024], BF16, kind="ExternalOutput").ap()
    dbgf = nc.dram_tensor("dbgf", [4, 128, 1024], F32, kind="ExternalOutput").ap()
    cc_in = nc.dram_tensor("cc_in", [NSC, NH * 128, SC], BF16, kind="Internal")
    cc_out = nc.dram_tensor("cc_out", [NSC, 2 * NH * 128, SC], BF16, kind="Internal")
    xmid_dram = nc.dram_tensor("xmid_dram", [D, TOKH], F32, kind="Internal")
    wbf_dram = nc.dram_tensor("wbf_dram", [N_CT + Z_CT, 128, KCH * 128], BF16, kind="Internal")

    import contextlib
    sbctx_ab = contextlib.ExitStack()
    sbctx_2 = contextlib.ExitStack()
    sbctx_3 = contextlib.ExitStack()

    def sb(name, shape, dtype):
        return nc.alloc_sbuf_tensor(name, shape, dtype)

    def sbA(name, shape, dtype):
        return sbctx_ab.enter_context(nc.sbuf_tensor(name, shape, dtype))

    def sb2(name, shape, dtype):
        return sbctx_2.enter_context(nc.sbuf_tensor(name, shape, dtype))

    def sb3(name, shape, dtype):
        return sbctx_3.enter_context(nc.sbuf_tensor(name, shape, dtype))

    ones128_bf = sb("ones128_bf", [128, 1], BF16)
    ones1_bf = sb("ones1_bf", [1, 128], BF16)
    ones8_f = sb("ones8_f", [8, 128], F32)
    tri_ji = sb("tri_ji", [128, 128], F32)
    blockid = sb("blockid", [8, 8 * 128], F32)
    m128row = sb("m128row", [1, 128], BF16)
    id128 = sb("id128s", [128, 128], BF16)
    id8 = sb("id8s", [8, 8], F32)
    convw_sb = sb("convw_sb", [128, N_CT, 4], F32)
    wba_sb = sb("wba_sb", [128, 2, KCH, 8], BF16)
    alog_sb = sb("alog_sb", [8, 1], F32)
    dtb_sb = sb("dtb_sb", [8, 1], F32)
    negA = sb("negA", [8, 1], F32)
    normw_sb = sb("normw_sb", [128, 1], F32)
    sg_sb = sb("sg_sb", [128, IT_FF], F32)
    su_sb = sb("su_sb", [128, IT_FF], F32)
    sd_sb = sb("sd_sb", [128, DT_FF], F32)
    constt = sb("constt", [128, 8], F32)
    stat_f = sb("stat_f", [1, 2 * SC], F32)
    stat_b = sb("stat_b", [1, 2 * SC], BF16)
    ln2 = sb("ln2", [128, KCH, TOKH], BF16)

    ln1 = sbA("ln1", [128, KCH, SC], BF16)
    mrep = sbA("mrep", [128, SC], BF16)
    rrep = sbA("rrep", [128, SC], BF16)
    wslab = sbA("wslab", [128, 3, KCH * 128], BF16)
    qraw = sbA("qraw", [128, N_CT, 520], BF16)
    tails = sbA("tails", [128, N_CT, 4], BF16)
    qhat = sbA("qhat", [128, N_CT, SC], BF16)
    z_sc = sbA("z_sc", [128, Z_CT, SC], BF16)
    bfm = sbA("bfm", [8, SC], F32)
    lnbfm = sbA("lnbfm", [8, SC], F32)
    gfm = sbA("gfm", [8, SC], F32)
    sp8a = sbA("sp8a", [8, SC], F32)
    sp8b = sbA("sp8b", [8, SC], F32)

    og_sc = sbA("og_sc", [128, NH, SC], BF16)
    M_f32 = sbA("M_f32", [128, NH, 128], F32)
    Mbf = sbA("Mbf", [128, NH, 128], BF16)

    g_tm = sbA("g_tm", [128, 8], F32)
    lnb_tm = sbA("lnb_tm", [128, 8], F32)
    G_tm = sbA("G_tm", [128, 8], F32)
    G8 = sbA("G8", [8, 128], F32)
    colA = sbA("colA", [128, 8], F32)
    rowB = sbA("rowB", [8, 128], F32)
    bdG = sbA("bdG", [8, 8 * 128], F32)
    bdB = sbA("bdB", [8, 8 * 128], F32)
    btm3 = sbA("btm3", [128, 3, 8], F32)
    etm3 = sbA("etm3", [128, 3, 8], F32)
    gam3 = sbA("gam3", [128, 3, 8], F32)
    t8a = sbA("t8a", [128, 8], F32)
    eG = sbA("eG", [128, NH, 128], BF16)
    nW = sbA("nW", [128, NH, 128], F32)
    nWy = sbA("nWy", [128, NH, 128], F32)
    eEb = sbA("eEb", [128, NH, 128], BF16)
    kk_sb = sbA("kk_sb", [128, NH, 128], BF16)
    qk_sb = sbA("qk_sb", [128, NH, 128], BF16)
    A_sb2 = sbA("A_sb2", [128, 2, NH, 128], BF16)
    AT_sb2 = sbA("AT_sb2", [128, 2, NH, 128], BF16)
    V6T2 = sbA("V6T2", [128, 2, NH, 128], BF16)
    Pw = [sbA("Pw0", [128, NH, 128], BF16), sbA("Pw1", [128, NH, 128], BF16)]
    PTw = [sbA("PTw0", [128, NH, 128], BF16), sbA("PTw1", [128, NH, 128], BF16)]
    Vcp = [sbA("Vc0", [128, NH, 128], BF16), sbA("Vc1", [128, NH, 128], BF16)]

    def _carve3(base, off, n0):
        b = base[:, :, :]
        return dataclasses.replace(b, offset=b.offset + off,
                                   ap=[b.ap[0], [8 * 128, n0], [128, 8], [1, 128]])

    def _carve1(base, off):
        b = base[:, :, :]
        return dataclasses.replace(b, offset=b.offset + off,
                                   ap=[b.ap[0], [128, 8], [1, 128]])

    vtm3 = _carve3(ln1, 0, 3)
    ktm3 = _carve3(ln1, 3072, 3)
    eEc_t = _carve1(ln1, 6144)
    ST3 = _carve3(qraw, 0, 3)
    qtil3 = _carve3(qraw, 3072, 3)
    ktil3 = _carve3(qraw, 6144, 3)
    R0t = _carve1(qraw, 9216)
    Ut = _carve1(qraw, 10240)
    eEa_t = _carve1(qraw, 11264)

    import contextlib
    pctx = contextlib.ExitStack()
    psA = pctx.enter_context(nc.psum_tensor("psA", [128, 512], F32))
    ps_b1 = pctx.enter_context(nc.psum_tensor("ps_b1", [128, 1024], F32))
    ps_b2 = pctx.enter_context(nc.psum_tensor("ps_b2", [128, 1024], F32))
    ps_m1 = pctx.enter_context(nc.psum_tensor("ps_m1", [128, 1024], F32))
    ps_tr = pctx.enter_context(nc.psum_tensor("ps_tr", [128, 1024], BF16))

    csd = Ctr(nc, "csd")
    cx = Ctr(nc, "cx")
    cw_s = [Ctr(nc, f"cw{i}") for i in range(4)]
    cwo_s = [Ctr(nc, "cwo0"), Ctr(nc, "cwo1")]
    cgu_s = [Ctr(nc, f"cgu{i}") for i in range(4)]
    cwd_s = [Ctr(nc, f"cwd{i}") for i in range(4)]
    cog = Ctr(nc, "cog")
    cph = Ctr(nc, "cph")
    cxm_s = [Ctr(nc, "cxm0"), Ctr(nc, "cxm1")]
    cspill = Ctr(nc, "cspill")
    cwb = Ctr(nc, "cwb")
    cyb = [Ctr(nc, "cy0"), Ctr(nc, "cy1")]
    cdbg = Ctr(nc, "cdbg")
    ct_ = Ctr(nc, "ct")
    cv = Ctr(nc, "cv")
    ca = Ctr(nc, "ca")
    cp = Ctr(nc, "cp")
    ccc = Ctr(nc, "ccc")

    V = EngWrap(V, cv)
    A = EngWrap(A, ca)
    P_raw = P
    P = EngWrap(P, cp)

    pid = P_raw.partition_id()
    half_off = (pid % 2) * TOKH
    peer_row = (pid + 1 - (pid % 2) * 2) * (NH * 128)

    def stats_epilogue(ms_ap, qs_ap, m_dst, r_dst, eps, repl_ps=None):
        ct_.wait(V)
        V.tensor_scalar_mul(stat_f[0:1, 0:512], ms_ap, 1.0 / D)
        V.tensor_copy(stat_b[0:1, 0:512], stat_f[0:1, 0:512])
        V.tensor_mul(stat_f[0:1, 512:1024], stat_f[0:1, 0:512], stat_f[0:1, 0:512])
        V.scalar_tensor_tensor(stat_f[0:1, 0:512], qs_ap, 1.0 / D,
                               stat_f[0:1, 512:1024], ALU.mult, ALU.subtract)
        V.engine_nop()
        cv.wait(A)
        A.activation(stat_f[0:1, 512:1024], stat_f[0:1, 0:512], AF.Sqrt,
                            bias=eps, scale=1.0)
        ca.wait(V)
        V.reciprocal(stat_b[0:1, 512:1024], stat_f[0:1, 512:1024])
        cv.wait(T)
        ca.wait(T)
        rp_ = ps_b2 if repl_ps is None else repl_ps
        _tmm = T.matmul(rp_[:, 0:512], ones1_bf[:, :], stat_b[0:1, 0:512], start=True, stop=True)
        ct_.inc(T.matmul(rp_[:, 512:1024], ones1_bf[:, :], stat_b[0:1, 512:1024],
                         start=True, stop=True))
        ct_.wait(A)
        A.activation(m_dst, rp_[:, 0:512], AF.Copy, bias=0.0, scale=1.0)
        A.activation(r_dst, rp_[:, 512:1024], AF.Copy, bias=0.0, scale=1.0)

    csd.inc(S.dma_start(out=convw_sb[:, :, :], in_=convw), 16)
    cx.inc(P.dma_start(out=wba_sb[:, :, :, :], in_=wba), 16)
    csd.inc(S.dma_start(out=alog_sb[:, :], in_=alog), 16)
    csd.inc(S.dma_start(out=dtb_sb[:, :], in_=dtb), 16)
    csd.inc(S.dma_start(out=normw_sb[:, :], in_=normw), 16)
    csd.inc(S.dma_start(out=sg_sb[:, :], in_=sg_in), 16)
    csd.inc(S.dma_start(out=su_sb[:, :], in_=su_in), 16)
    csd.inc(S.dma_start(out=sd_sb[:, :], in_=sd_in), 16)
    csd.inc(S.dma_start(out=id128[:, :], in_=id128_in), 16)
    csd.inc(S.dma_start(out=id8[:, :], in_=id8_in), 16)

    _cvals = [0.0, 1.0, -1.0, 1e-5, 1e-6, 128.0 * 1e-6, 128.0, 1.0 / 128.0]
    for _i, _v in enumerate(_cvals):
        V.memset(constt[:, _i:_i + 1], _v)
        nc.const_aps.register(F32, _v, constt[:, _i:_i + 1])
    V.memset(ones128_bf[:, :], 1.0)
    V.memset(ones1_bf[:, :], 1.0)
    V.memset(ones8_f[:, :], 1.0)
    V.memset(m128row[:, :], -128.0)
    V.memset(tails[:, :, :], 0.0)
    V.memset(M_f32[:, :, :], 0.0)
    V.memset(tri_ji[:, :], 1.0)
    V.memset(blockid[:, :], 1.0)
    cv.wait(P)
    P.affine_select(tri_ji[:, :], tri_ji[:, :], [[1, 128]], ALU.is_ge, 0.0,
                           channel_multiplier=-1)
    P.affine_select(blockid[:, :], blockid[:, :], [[-1, 8], [0, 128]],
                           ALU.is_equal, 0.0, channel_multiplier=1)
    csd.wait(A)
    cv.wait(A)
    A.activation(negA[:, :], alog_sb[:, :], AF.Exp, bias=0.0, scale=1.0)
    ca.wait(V)
    V.tensor_scalar_mul(negA[:, :], negA[:, :], -1.0)
    cp.wait(T)
    cp.wait(V)
    csd.wait(T)
    csd.wait(V)
    csd.wait(P)

    PAIRS = [[0, 1], [2, 3], [4, 5], [6, 7]]
    w_used = {}
    wb_at = {}
    og_stored = 0
    cc_issued = 0

    def scope(name):
        return nc.named_scope(name)

    for sc in range(NSC):
        _scA = scope(f"A{sc}"); _scA.__enter__()
        tok0 = sc * SC
        if sc > 0:
            ct_.wait(P)
            cv.wait(P)
            ca.wait(P)
            cdbg.wait(P)
        if sc == 0:
            cx.inc(P.dma_start(out=qhat[:, 0:16, :],
                               in_=xT[:, tok0:tok0 + SC].rearrange("(k p) t -> p k t", p=128)), 16)
        if sc == 0:
            l2done = {}
            cviews = {}
            tas = {}
            eg_at = {}
            etm_at = {}
            eea_at = {}
            eeb_at = {}
            eec_at = {}
            amask_at = {}
            stmask_at = {}
            vtm_at = {}
            ktm_at = {}
            ktil_at = {}
            mbf_at = {}
            y_r0 = {}
            u_at = {}
            bank_free = {}
            buf_read = {}
            buf_rdy = {}
            psm1_s = {"a": 0, "v": 0}
            pstr_s = {"a": 0, "v": 0}
            psa_s = {"a": 0, "v": 0}
            rot = [ps_b2, ps_b1]
            vread = {}

        def psm1_take():
            ca.wait(T, at=psm1_s["a"])
            cv.wait(T, at=psm1_s["v"])

        def bwait(eng, key):
            r = buf_rdy.get(key)
            if r is not None:
                (ca if r[0] == "A" else cv if r[0] == "V" else cp).wait(eng, at=r[1])

        def chain_mm(bank_i, lhs, lkey, rhs, rkey):
            if bank_i in bank_free:
                ca.wait(T, at=bank_free[bank_i][0])
                cv.wait(T, at=bank_free[bank_i][1])
            bwait(T, lkey)
            bwait(T, rkey)
            pb = rot[bank_i]
            for h in range(8):
                mm = T.matmul(pb[:, h * 128:(h + 1) * 128], lhs[:, h, :], rhs[:, h, :],
                              start=True, stop=True)
            ct_.inc(mm)
            buf_read[lkey] = ct_.n
            buf_read[rkey] = ct_.n
            return ct_.n

        def evac_copy(bank_i, dst, dkey, prod_at, eng="A"):
            if eng == "A":
                ct_.wait(A, at=max(prod_at, buf_read.get(dkey, 0)))
                A.activation(dst[:, :, :], rot[bank_i][:, 0:1024], AF.Copy, bias=0.0, scale=1.0)
                bank_free[bank_i] = (ca.n, bank_free.get(bank_i, (0, 0))[1])
                buf_rdy[dkey] = ("A", ca.n)
            else:
                ct_.wait(V, at=max(prod_at, buf_read.get(dkey, 0)))
                V.tensor_copy(dst[:, :, :], rot[bank_i][:, 0:1024])
                bank_free[bank_i] = (bank_free.get(bank_i, (0, 0))[0], cv.n)
                buf_rdy[dkey] = ("V", cv.n)

        def evac_add(bank_i, dst, dkey, a_src, prod_at):
            ct_.wait(V, at=max(prod_at, buf_read.get(dkey, 0)))
            V.tensor_add(dst[:, :, :], a_src[:, :, :], _as3(rot[bank_i][:, 0:1024]))
            bank_free[bank_i] = (bank_free.get(bank_i, (0, 0))[0], cv.n)
            buf_rdy[dkey] = ("V", cv.n)

        def u_prep_open(c):
            d2, d3 = c % 2, c % 3
            tas[c] = slice((c % NCH) * C, (c % NCH) * C + C)
            cviews[c] = dict(a=A_sb2[:, d2], at=AT_sb2[:, d2], st=ST3[:, d3],
                             kt=ktil3[:, d3], qt=qtil3[:, d3], vt=vtm3[:, d3],
                             km=ktm3[:, d3], v6=V6T2[:, d2], b=btm3[:, d3, :],
                             e=etm3[:, d3, :], g=gam3[:, d3, :])

        def u_p1(c):
            vw = cviews[c]; ta = tas[c]
            cv.wait(T, at=psa_s["v"])
            ca.wait(T, at=psa_s["a"])
            _tmm = T.transpose(psA[:, 0:8], gfm[:, ta], id8[:, :])
            _tmm = T.transpose(psA[:, 8:16], bfm[:, ta], id8[:, :])
            ct_.inc(T.transpose(psA[:, 16:24], lnbfm[:, ta], id8[:, :]))
            ct_.wait(V)
            V.tensor_copy(g_tm[:, :], psA[:, 0:8])
            V.tensor_copy(vw["b"], psA[:, 8:16])
            V.tensor_copy(lnb_tm[:, :], psA[:, 16:24])

        def u_p2(c):
            ta = tas[c]
            cv.wait(T)
            _tmm = T.matmul(psA[:, 24:32], tri_ji[:, :], g_tm[:, :], start=True, stop=True)
            ct_.inc(T.matmul(psA[0:8, 32:160], g_tm[:, :], tri_ji[:, :], start=True, stop=True))
            ct_.wait(V)
            V.tensor_copy(G_tm[:, :], psA[:, 24:32])
            V.tensor_copy(G8[:, :], psA[0:8, 32:160])
            V.tensor_add(colA[:, :], G_tm[:, :], lnb_tm[:, :])
            V.tensor_add(rowB[:, :], G8[:, :], lnbfm[:, ta])
            V.tensor_mul(bdG[:, :], _bview(G8[:, :], 8), blockid[:, :])
            V.tensor_mul(bdB[:, :], _bview(rowB[:, :], 8), blockid[:, :])
            psa_s["v"] = cv.n

        def u_p3a(c):
            vw = cviews[c]
            psm1_take()
            cv.wait(T)
            _tmm = T.matmul(ps_m1[:, 0:512], ones8_f[:, :], bdG[:, 0:512], start=True, stop=True)
            ct_.inc(T.matmul(ps_m1[:, 512:1024], ones8_f[:, :], bdG[:, 512:1024],
                             start=True, stop=True))
            ct_.wait(A)
            cv.wait(A)
            A.activation(eG[:, :, :], ps_m1[:, 0:1024], AF.Exp, bias=0.0, scale=1.0)
            eg_at[c] = ca.n
            pm1 = ps_m1[:, 0:1024]
            gb_last = dataclasses.replace(pm1, ap=[pm1.ap[0], [128, 8]],
                                          offset=pm1.offset + 127)
            ct_.wait(V)
            ca.wait(V)
            V.tensor_sub(t8a[:, :], gb_last, G_tm[:, :])
            cv.wait(A)
            A.activation(vw["e"], t8a[:, :], AF.Exp, bias=0.0, scale=1.0)
            A.activation(vw["g"], gb_last, AF.Exp, bias=0.0, scale=1.0)
            etm_at[c] = ca.n
            ca.wait(V)
            V.tensor_sub(nW[:, :, :], _as3(pm1), _bview3(colA[:, :]))
            V.tensor_scalar_max(nW[:, :, :], nW[:, :, :], 0.0)
            cv.wait(A)
            A.activation(eEa_t[:, :, :], nW[:, :, :], AF.Exp, bias=0.0, scale=-1.0)
            eea_at[c] = ca.n
            ca.wait(V)
            V.tensor_sub(nW[:, :, :], _as3(pm1), _bview3(G_tm[:, :]))
            V.tensor_scalar_min(nW[:, :, :], nW[:, :, :], 0.0)
            cv.wait(A)
            A.activation(eEc_t[:, :, :], nW[:, :, :], AF.Exp, bias=0.0, scale=1.0)
            eec_at[c] = ca.n
            psm1_s["a"] = ca.n
            psm1_s["v"] = cv.n

        def u_p3b(c):
            psm1_take()
            _tmm = T.matmul(ps_m1[:, 0:512], ones8_f[:, :], bdB[:, 0:512], start=True, stop=True)
            ct_.inc(T.matmul(ps_m1[:, 512:1024], ones8_f[:, :], bdB[:, 512:1024],
                             start=True, stop=True))
            ct_.wait(V)
            ca.wait(V, at=eec_at[c])
            V.tensor_sub(nW[:, :, :], _as3(ps_m1[:, 0:1024]), _bview3(G_tm[:, :]))
            V.tensor_scalar_min(nW[:, :, :], nW[:, :, :], 0.0)
            cv.wait(A)
            A.activation(eEb[:, :, :], nW[:, :, :], AF.Exp, bias=0.0, scale=1.0)
            eeb_at[c] = ca.n
            psm1_s["v"] = cv.n

        def u_p4(c):
            vw = cviews[c]; ta = tas[c]; d2 = c % 2
            psm1_take()
            cv.wait(T, at=l2done[c // NCH])
            for h in range(8):
                _tmm = T.matmul(ps_m1[:, h * 128:(h + 1) * 128], qhat[:, 8 + h, ta],
                                qhat[:, 8 + h, ta], start=True, stop=True)
            ct_.inc(_tmm)
            ct_.wait(A)
            cv.wait(A)
            A.activation(kk_sb[:, :, :], ps_m1[:, 0:1024], AF.Copy, bias=0.0, scale=1.0)
            kk_at = ca.n
            ca.wait(V, at=kk_at)
            ct_.wait(V, at=buf_read.get(f"A2_{d2}", 0))
            bwait(V, f"A2_{d2}")
            V.tensor_mul(vw["a"], eEa_t[:, :, :], kk_sb[:, :, :])
            cv.wait(P)
            P.affine_select(vw["a"], vw["a"], [[0, 8], [-1, 128]],
                            ALU.is_gt, 0.0, channel_multiplier=1)
            ca.wait(V, at=eeb_at[c])
            ct_.wait(V, at=buf_read.get(f"AT2_{d2}", 0))
            bwait(V, f"AT2_{d2}")
            V.tensor_mul(vw["at"], eEb[:, :, :], kk_sb[:, :, :])
            cv.wait(P)
            P.affine_select(vw["at"], vw["at"], [[0, 8], [1, 128]],
                            ALU.is_gt, 0.0, channel_multiplier=-1)
            amask_at[c] = cp.n
            buf_rdy[f"A2_{d2}"] = ("P", cp.n)
            buf_rdy[f"AT2_{d2}"] = ("P", cp.n)
            psm1_s["a"] = kk_at

        def u_p5(c):
            vw = cviews[c]; ta = tas[c]; d3 = c % 3
            psm1_take()
            cv.wait(T, at=l2done[c // NCH])
            for h in range(8):
                _tmm = T.matmul(ps_m1[:, h * 128:(h + 1) * 128], qhat[:, 8 + h, ta],
                                qhat[:, h, ta], start=True, stop=True)
            ct_.inc(_tmm)
            ct_.wait(A)
            cv.wait(A)
            A.activation(qk_sb[:, :, :], ps_m1[:, 0:1024], AF.Copy, bias=0.0, scale=1.0)
            qk_at = ca.n
            ca.wait(V, at=qk_at)
            ct_.wait(V, at=buf_read.get(f"ST_{d3}", 0))
            bwait(V, f"ST_{d3}")
            V.tensor_mul(vw["st"], eEc_t[:, :, :], qk_sb[:, :, :])
            cv.wait(P)
            P.affine_select(vw["st"], vw["st"], [[0, 8], [1, 128]],
                            ALU.is_ge, 0.0, channel_multiplier=-1)
            stmask_at[c] = cp.n
            psm1_s["a"] = qk_at

        def u_p6a(c):
            vw = cviews[c]; ta = tas[c]; d3 = c % 3
            ca.wait(T, at=pstr_s["a"])
            for h in range(8):
                _tmm = T.transpose(ps_tr[:, h * 128:(h + 1) * 128], qhat[:, 16 + h, ta],
                                   id128[:, :])
            ct_.inc(_tmm)
            ct_.wait(A)
            ct_.wait(A, at=buf_read.get(f"VT_{d3}", 0))
            cv.wait(A, at=vread.get(f"VT_{d3}", 0))
            A.activation(vw["vt"], ps_tr[:, :], AF.Copy, bias=0.0, scale=1.0)
            vtm_at[c] = ca.n
            pstr_s["a"] = ca.n

        def u_p6b(c):
            vw = cviews[c]; ta = tas[c]; d3 = c % 3
            ca.wait(T, at=pstr_s["a"])
            cv.wait(T, at=l2done[c // NCH])
            for h in range(8):
                _tmm = T.transpose(ps_tr[:, h * 128:(h + 1) * 128], qhat[:, 8 + h, ta],
                                   id128[:, :])
            ct_.inc(_tmm)
            ct_.wait(A)
            ct_.wait(A, at=buf_read.get(f"KM_{d3}", 0))
            A.activation(vw["km"], ps_tr[:, :], AF.Copy, bias=0.0, scale=1.0)
            ktm_at[c] = ca.n
            pstr_s["a"] = ca.n

        def u_p7(c):
            vw = cviews[c]; ta = tas[c]; d3 = c % 3
            ca.wait(V, at=eg_at[c])
            ct_.wait(V, at=max(buf_read.get(f"KT_{d3}", 0), buf_read.get(f"QT_{d3}", 0)))
            V.tensor_mul(vw["kt"], qhat[:, 8:16, ta], eG[:, :, :])
            V.tensor_mul(vw["qt"], qhat[:, 0:8, ta], eG[:, :, :])
            ktil_at[c] = cv.n

        def build_chain(c):
            vw = cviews[c]; d2 = c % 2
            units = []
            rp = [0]

            def c0():
                cp.wait(V, at=amask_at[c])
                ct_.wait(V, at=buf_read.get("VcA", 0))
                V.tensor_sub(Vcp[0][:, :, :], _bview(id128[:, :], 8), vw["a"])
                buf_rdy["VcA"] = ("V", cv.n)
            units.append(c0)
            cur, nxt = "VcA", "VcB"
            tl = {"VcA": Vcp[0], "VcB": Vcp[1], "Pw0": Pw[0], "Pw1": Pw[1],
                  "PTw0": PTw[0], "PTw1": PTw[1]}
            pk, ptk = f"A2_{d2}", f"AT2_{d2}"
            pt_, p_ = vw["at"], vw["a"]
            pend_v = []
            for lev in range(1, 7):
                npk, nptk = f"Pw{lev % 2}", f"PTw{lev % 2}"

                def s_unit(pk=pk, ptk=ptk, p_=p_, pt_=pt_, nptk=nptk, lev=lev):
                    b = rp[0] % 2
                    at = chain_mm(b, p_, pk, pt_, ptk)
                    evac_copy(b, tl[nptk], nptk, at)
                    rp[0] += 1
                units.append(s_unit)

                def v_unit(nptk=nptk, cur=cur, nxt=nxt):
                    b = rp[0] % 2
                    at = chain_mm(b, tl[nptk], nptk, tl[cur], cur)
                    evac_add(b, tl[nxt], nxt, tl[cur], at)
                    rp[0] += 1
                pend_v.append(v_unit)
                if lev < 6:
                    def p_unit(pk=pk, ptk=ptk, p_=p_, pt_=pt_, npk=npk):
                        b = rp[0] % 2
                        at = chain_mm(b, pt_, ptk, p_, pk)
                        evac_copy(b, tl[npk], npk, at, eng="V")
                        rp[0] += 1
                    units.append(p_unit)
                if len(pend_v) > 1:
                    units.append(pend_v.pop(0))
                cur, nxt = nxt, cur
                pk, ptk, p_, pt_ = npk, nptk, tl[npk], tl[nptk]
            while pend_v:
                units.append(pend_v.pop(0))

            def v6t(cur=cur):
                ca.wait(T, at=pstr_s["a"])
                bwait(T, cur)
                for h in range(8):
                    _tmm = T.transpose(ps_tr[:, h * 128:(h + 1) * 128], tl[cur][:, h, :],
                                       id128[:, :])
                ct_.inc(_tmm)
                buf_read[cur] = ct_.n
                ct_.wait(A)
                ct_.wait(A, at=buf_read.get(f"V6_{d2}", 0))
                A.activation(vw["v6"], ps_tr[:, :], AF.Copy, bias=0.0, scale=1.0)
                buf_rdy[f"V6_{d2}"] = ("A", ca.n)
                pstr_s["a"] = ca.n
            units.append(v6t)
            return units

        def u_y0(c):
            ct_.wait(V, at=buf_read.get("Mbf", 0))
            V.tensor_copy(Mbf[:, :, :], M_f32[:, :, :])
            mbf_at[c] = cv.n

        def u_y1(c):
            vw = cviews[c]; d3 = c % 3
            psm1_take()
            cv.wait(T, at=mbf_at[c])
            cv.wait(T, at=ktil_at[c])
            for h in range(8):
                _tmm = T.matmul(ps_m1[:, h * 128:(h + 1) * 128], vw["kt"][:, h, :],
                                Mbf[:, h, :], start=True, stop=True)
            ct_.inc(_tmm)
            buf_read[f"KT_{d3}"] = ct_.n
            buf_read["Mbf"] = ct_.n
            ct_.wait(V)
            ca.wait(V, at=vtm_at[c])
            ct_.wait(V, at=buf_read.get("R0", 0))
            V.tensor_sub(nWy[:, :, :], vw["vt"], _as3(ps_m1[:, 0:1024]))
            V.tensor_mul(R0t[:, :, :], nWy[:, :, :], _bview3(vw["b"]))
            vread[f"VT_{d3}"] = cv.n
            y_r0[c] = cv.n
            psm1_s["v"] = cv.n

        def u_y2(c):
            d2 = c % 2
            psm1_take()
            bwait(T, f"V6_{d2}")
            cv.wait(T, at=y_r0[c])
            for h in range(8):
                _tmm = T.matmul(ps_m1[:, h * 128:(h + 1) * 128], V6T2[:, d2][:, h, :],
                                R0t[:, h, :], start=True, stop=True)
            ct_.inc(_tmm)
            buf_read[f"V6_{d2}"] = ct_.n
            buf_read["R0"] = ct_.n
            ct_.wait(A)
            ct_.wait(A, at=buf_read.get("Ut", 0))
            cv.wait(A, at=vread.get("Ut", 0))
            A.activation(Ut[:, :, :], ps_m1[:, 0:1024], AF.Copy, bias=0.0, scale=1.0)
            u_at[c] = ca.n
            psm1_s["a"] = ca.n

        def u_y3(c):
            vw = cviews[c]; ta = tas[c]; d3 = c % 3
            psm1_take()
            ca.wait(T, at=u_at[c])
            cv.wait(T, at=ktil_at[c])
            cp.wait(T, at=stmask_at[c])
            for h in range(8):
                _tmm = T.matmul(ps_m1[:, h * 128:(h + 1) * 128], Mbf[:, h, :],
                                vw["qt"][:, h, :], start=True, stop=False)
                _tmm = T.matmul(ps_m1[:, h * 128:(h + 1) * 128], Ut[:, h, :],
                                vw["st"][:, h, :], start=False, stop=True)
            ct_.inc(_tmm)
            buf_read[f"QT_{d3}"] = ct_.n
            buf_read[f"ST_{d3}"] = ct_.n
            buf_read["Ut"] = ct_.n
            buf_read["Mbf"] = ct_.n
            ct_.wait(A)
            if c == 0 and sc > 0:
                cog.wait(A, at=og_stored)
            A.activation(og_sc[:, :, ta], ps_m1[:, 0:1024], AF.Copy, bias=0.0, scale=1.0)
            psm1_s["a"] = ca.n

        def u_y4(c):
            vw = cviews[c]; d3 = c % 3
            ca.wait(V, at=etm_at[c])
            ct_.wait(V, at=buf_read.get("R0", 0))
            ca.wait(V, at=u_at[c])
            V.tensor_mul(R0t[:, :, :], Ut[:, :, :], _bview3(vw["e"]))
            vread["Ut"] = cv.n
            ee2_at = cv.n
            psm1_take()
            ca.wait(T, at=ktm_at[c])
            cv.wait(T, at=ee2_at)
            for h in range(8):
                _tmm = T.matmul(ps_m1[:, h * 128:(h + 1) * 128], vw["km"][:, h, :],
                                R0t[:, h, :], start=True, stop=True)
            ct_.inc(_tmm)
            buf_read[f"KM_{d3}"] = ct_.n
            buf_read["R0"] = ct_.n
            ct_.wait(V)
            V.tensor_mul(nWy[:, :, :], M_f32[:, :, :], _bview3(vw["g"]))
            V.tensor_add(M_f32[:, :, :], nWy[:, :, :], _as3(ps_m1[:, 0:1024]))
            psm1_s["v"] = cv.n

        chq = []
        lfq = []

        def cpop():
            if lfq:
                lfq.pop(0)()
            elif chq:
                chq.pop(0)()

        cx.wait(T)
        cv.wait(T)
        ca.wait(T)
        for k in range(KCH):
            _tmm = T.matmul(psA[0:1, 0:512], ones128_bf[:, :], qhat[:, k, :],
                     start=(k == 0), stop=(k == KCH - 1))
        ct_.inc(_tmm)
        cx.wait(A)
        if sc > 0:
            ct_.wait(A)
        for k in range(KCH):
            A.activation(ln1[:, k, :], qhat[:, k, :], AF.Square, bias=0.0, scale=1.0)
            _sq = ca.n
            ca.wait(T, at=_sq)
            _tmm = T.matmul(psA[32:33, 0:512], ones128_bf[:, :], ln1[:, k, :],
                     start=(k == 0), stop=(k == KCH - 1))
        ct_.inc(_tmm)
        A.nop()
        stats_epilogue(psA[0:1, 0:512], psA[32:33, 0:512], mrep[:, :], rrep[:, :], 1e-5)
        ca.wait(V)

        w_loaded = {}

        def w_dma(ci, buf):
            if sc == 0:
                srcw = (wqkv[ci] if ci < N_CT else wz[ci - N_CT]).rearrange("p k c -> p (k c)")
                qe = P
            else:
                srcw = wbf_dram.ap()[ci]
                qe = S
            if ("w", buf) in w_used:
                ct_.wait(qe, at=w_used[("w", buf)])
            if buf in wb_at:
                cwb.wait(qe, at=wb_at[buf])
            cw_s[buf].inc(qe.dma_start(out=wslab[:, buf, :], in_=srcw), 16)
            w_loaded[buf] = cw_s[buf].n
            if sc == 0:
                cw_s[buf].wait(S, at=w_loaded[buf])
                cwb.inc(S.dma_start(out=wbf_dram.ap()[ci], in_=wslab[:, buf, :]), 16)
                wb_at[buf] = cwb.n

        for _pf in range(2):
            w_dma(_pf, _pf)

        cv.wait(T)
        ca.wait(T)
        cx.wait(T)
        for k in range(KCH):
            V.tensor_sub(ln1[:, k, :], qhat[:, k, :], mrep[:, :])
            V.tensor_mul(ln1[:, k, :], ln1[:, k, :], rrep[:, :])
            _lnk = cv.n
            cv.wait(T, at=_lnk)
            _tmm = T.matmul(psA[0:8, 0:512], wba_sb[:, 0, k, :], ln1[:, k, :],
                     start=(k == 0), stop=(k == KCH - 1))
            _tmm = T.matmul(psA[32:40, 0:512], wba_sb[:, 1, k, :], ln1[:, k, :],
                     start=(k == 0), stop=(k == KCH - 1))
        ct_.inc(_tmm)
        lnv = cv.n
        ct_.wait(V)
        ct_.wait(A)
        A.activation(bfm[:, :], psA[0:8, 0:512], AF.Sigmoid, bias=0.0, scale=1.0)
        V.tensor_scalar_mul(sp8a[:, :], psA[0:8, 0:512], -1.0)
        cv.wait(A)
        A.activation(sp8b[:, :], sp8a[:, :], AF.Abs, bias=0.0, scale=1.0)
        ca.wait(V)
        V.tensor_scalar_max(sp8a[:, :], sp8a[:, :], 0.0)
        cv.wait(A)
        A.activation(sp8b[:, :], sp8b[:, :], AF.Exp, bias=0.0, scale=-1.0)
        A.activation(sp8b[:, :], sp8b[:, :], AF.Ln, bias=1.0, scale=1.0)
        ca.wait(V)
        V.tensor_add(sp8a[:, :], sp8a[:, :], sp8b[:, :])
        V.tensor_scalar_mul(lnbfm[:, :], sp8a[:, :], -1.0)
        V.tensor_scalar_add(sp8a[:, :], psA[32:40, 0:512], dtb_sb[:, :])
        ba_free = (ca.n, cv.n)
        cv.wait(A)
        A.activation(sp8b[:, :], sp8a[:, :], AF.Abs, bias=0.0, scale=1.0)
        ca.wait(V)
        V.tensor_scalar_max(sp8a[:, :], sp8a[:, :], 0.0)
        cv.wait(A)
        A.activation(sp8b[:, :], sp8b[:, :], AF.Exp, bias=0.0, scale=-1.0)
        A.activation(sp8b[:, :], sp8b[:, :], AF.Ln, bias=1.0, scale=1.0)
        ca.wait(V)
        V.tensor_add(sp8a[:, :], sp8a[:, :], sp8b[:, :])
        V.tensor_scalar_mul(gfm[:, :], sp8a[:, :], negA[:, :])


        projb = [ps_m1[:, 0:512], ps_m1[:, 512:1024], ps_b1[:, 0:512]]
        pevac = {}
        praw_evac = {}
        ca.wait(T)
        cv.wait(T)
        cp.wait(V)
        V.tensor_copy(qraw[:, :, 0:3], tails[:, :, 0:3])
        tails_in_v = cv.n
        cv.wait(P, at=tails_in_v)
        n_tiles = N_CT + Z_CT
        for cti in range(n_tiles):
            buf = cti % 3
            bank = cti % 3
            if cti + 2 < n_tiles:
                w_dma(cti + 2, (cti + 2) % 3)
            cw_s[buf].wait(T, at=w_loaded[buf])
            cv.wait(T, at=lnv)
            if bank in pevac:
                ca.wait(T, at=pevac[bank])
            ph = projb[bank]
            for k in range(KCH):
                _tmm = T.matmul(ph, wslab[:, buf, k * 128:(k + 1) * 128], ln1[:, k, :],
                                start=(k == 0), stop=(k == KCH - 1))
            ct_.inc(_tmm)
            w_used[("w", buf)] = ct_.n
            ct_.wait(A)
            if cti < N_CT:
                A.activation(qraw[:, cti, 3:515], ph, AF.Copy, bias=0.0, scale=1.0)
                praw_evac[cti] = ca.n
                eng = V
                ca.wait(eng, at=praw_evac[cti])
                w4 = [convw_sb[:, cti, j:j + 1] for j in range(4)]
                eng.tensor_scalar_mul(qhat[:, cti, :], qraw[:, cti, 0:512], w4[0])
                for j in range(1, 4):
                    eng.scalar_tensor_tensor(qhat[:, cti, :], qraw[:, cti, j:j + 512], w4[j],
                                             qhat[:, cti, :], ALU.mult, ALU.add)
            else:
                A.activation(z_sc[:, cti - N_CT, :], ph, AF.Copy, bias=0.0, scale=1.0)
            pevac[bank] = ca.n
        cv.wait(A)
        A.activation(qhat[:, 0:12, :], qhat[:, 0:12, :], AF.Silu, bias=0.0, scale=1.0)
        A.activation(qhat[:, 12:24, :], qhat[:, 12:24, :], AF.Silu, bias=0.0, scale=1.0)
        cv.wait(V, at=tails_in_v)
        V.tensor_copy(tails[:, :, 0:3], qraw[:, :, 512:515])

        ss_slots = [psA[0:1, 0:512], psA[32:33, 0:512], psA[64:65, 0:512],
                    ps_b2[0:1, 0:512], ps_b2[32:33, 0:512], ps_b2[64:65, 0:512],
                    ps_b2[0:1, 512:1024], ps_b2[32:33, 512:1024], ps_b2[64:65, 512:1024]]
        sq_at = {}
        ssum_at = {}
        sqrt_at = {}
        rec_at = {}
        rep_at = {}
        mul_at = {}
        ct_.wait(A)
        ca.wait(T)
        cv.wait(T)
        A.activation(ln1[:, 0:8, :], qhat[:, 0:8, :], AF.Square, bias=0.0, scale=1.0)
        sq_at[0] = ca.n
        A.activation(ln1[:, 8:16, :], qhat[:, 8:16, :], AF.Square, bias=0.0, scale=1.0)
        sq_at[1] = ca.n
        stat_bx = [stat_b[0:1, 0:512], stat_b[0:1, 512:1024],
                   qraw[0:1, 2, 3:515], qraw[0:1, 3, 3:515]]
        l2slots = [ps_b2[0:1, 0:512], ps_b2[32:33, 0:512], ps_b2[64:65, 0:512],
                   ps_b2[0:1, 512:1024], ps_b2[32:33, 512:1024], ps_b2[64:65, 512:1024]]
        q_repl = [ps_b1[:, 0:512], ps_b1[:, 512:1024]]
        psm1_s["a"] = ca.n; psm1_s["v"] = cv.n
        pstr_s["a"] = ca.n
        psa_s["a"] = ca.n; psa_s["v"] = cv.n
        c0_ = sc * NCH
        for i in range(QK_CT + 8):
            if i < QK_CT:
                cti = i
                sl = cti % 6
                ca.wait(T, at=sq_at[cti // 8])
                if cti >= 6:
                    ca.wait(T, at=sqrt_at[cti - 6])
                if cti < 3:
                    ca.wait(T, at=ba_free[0])
                    cv.wait(T, at=ba_free[1])
                ct_.inc(T.matmul(l2slots[sl], ones128_bf[:, :], ln1[:, cti, :],
                                 start=True, stop=True))
                ssum_at[cti] = ct_.n
            if i >= 2 and i - 2 < QK_CT:
                cti = i - 2
                sl = cti % 6
                f2 = cti % 2
                ct_.wait(A, at=ssum_at[cti])
                if cti >= 2:
                    cv.wait(A, at=rec_at[cti - 2])
                if cti < 8:
                    A.activation(stat_f[0:1, f2 * 512:(f2 + 1) * 512], l2slots[sl],
                                 AF.Sqrt, bias=128.0 * 1e-6, scale=128.0)
                else:
                    A.activation(stat_f[0:1, f2 * 512:(f2 + 1) * 512], l2slots[sl],
                                 AF.Sqrt, bias=1e-6, scale=1.0)
                sqrt_at[cti] = ca.n
            if i >= 3 and i - 3 < QK_CT:
                cti = i - 3
                f2 = cti % 2
                ca.wait(V, at=sqrt_at[cti])
                if cti >= 4:
                    ct_.wait(V, at=rep_at[cti - 4])
                V.reciprocal(stat_bx[cti % 4], stat_f[0:1, f2 * 512:(f2 + 1) * 512])
                rec_at[cti] = cv.n
            if i >= 8 and i - 8 < QK_CT:
                cti = i - 8
                ct_.wait(V, at=rep_at[cti])
                V.tensor_mul(qhat[:, cti, :], qhat[:, cti, :], q_repl[cti % 2])
                mul_at[cti] = cv.n
            if i >= 6 and i - 6 < QK_CT:
                cti = i - 6
                cv.wait(T, at=rec_at[cti])
                if cti >= 2:
                    cv.wait(T, at=mul_at[cti - 2])
                ct_.inc(T.matmul(q_repl[cti % 2], ones1_bf[:, :],
                                 stat_bx[cti % 4], start=True, stop=True))
                rep_at[cti] = ct_.n
            if i == 2:
                u_prep_open(c0_)
                u_p1(c0_)
            elif i == 5:
                u_p2(c0_)
            elif i == 14:
                u_p3a(c0_)
            elif i == 18:
                u_p3b(c0_)
            elif i == 20:
                u_p6a(c0_)

        l2done[sc] = cv.n

        if "proj" in dbg and sc == 0:
            cv.wait(S)
            ca.wait(S)
            cdbg.inc(S.dma_start(out=dbg_out["qhat"], in_=qhat[:, :, :]), 16)
            cdbg.inc(S.dma_start(out=dbg_out["zsc"], in_=z_sc[:, :, :]), 16)
            cdbg.inc(S.dma_start(out=dbg_out["bg"][:, 0, :], in_=bfm[:, :]), 16)
            cdbg.inc(S.dma_start(out=dbg_out["bg"][:, 1, :], in_=lnbfm[:, :]), 16)
            cdbg.inc(S.dma_start(out=dbg_out["bg"][:, 2, :], in_=gfm[:, :]), 16)

        if stop_after == "A0" and sc == 0:
            cdbg.wait(S)
            return nc, dbg_out

        _scA.__exit__(None, None, None)
        _scB = scope(f"B{sc}"); _scB.__enter__()
        bank_free[0] = (ca.n, cv.n)
        bank_free[1] = (ca.n, cv.n)
        for br in range(NCH + 2):
            pre0 = (br == 0)
            if br == 1 and cc_issued < sc:
                cog.wait(P, at=og_stored)
                ccc.inc(P.collective_compute(
                    "AllGather", ALU.bypass, replica_groups=PAIRS,
                    ins=[cc_in.ap()[sc - 1].opt()], outs=[cc_out.ap()[sc - 1].opt()]))
                cc_issued = sc
            base = sc * NCH
            has_p = br < NCH
            has_c = 1 <= br <= NCH
            has_y = 2 <= br
            cP, cC, cY = base + br, base + br - 1, base + br - 2
            lfq.extend(chq)
            chq = build_chain(cC) if has_c else []
            if has_y:
                u_y0(cY)
            cpop()
            if has_y:
                u_y1(cY)
            cpop()
            if has_p and not pre0:
                u_prep_open(cP)
                u_p1(cP)
            cpop()
            if has_p and not pre0:
                u_p2(cP)
            cpop()
            cpop()
            if has_p and not pre0:
                u_p3a(cP)
            while lfq:
                lfq.pop(0)()
            cpop()
            if has_y:
                u_y2(cY)
            cpop()
            if has_p and not pre0:
                u_p3b(cP)
            cpop()
            if has_y:
                u_y3(cY)
            cpop()
            if has_y:
                u_y4(cY)
            if has_p:
                u_p4(cP)
            cpop()
            cpop()
            if has_p:
                u_p5(cP)
            cpop()
            cpop()
            if has_p and not pre0:
                u_p6a(cP)
            cpop()
            if has_p:
                u_p6b(cP)
            cpop()
            cpop()
            if has_p:
                u_p7(cP)
            cpop()
            cpop()
            if br >= NCH:
                while lfq or chq:
                    cpop()
            if br == NCH - 1 and sc + 1 < NSC:
                ct_.wait(P)
                cv.wait(P)
                ca.wait(P)
                cx.inc(P.dma_start(
                    out=qhat[:, 0:16, :],
                    in_=xT[:, (sc + 1) * SC:(sc + 2) * SC]
                    .rearrange("(k p) t -> p k t", p=128)), 16)

        osq = ln1[:, 0:8, :]
        zsil = ln1[:, 8:16, :]
        ca.wait(T)
        cv.wait(T)
        ct_.wait(A)
        A.activation(osq[:, :, :], og_sc[:, :, :], AF.Square, bias=0.0, scale=1.0)
        osq_at = ca.n
        A.activation(zsil[:, :, :], z_sc[:, :, :], AF.Silu, bias=0.0, scale=1.0)
        zsil_at = ca.n
        ssum_at = {}
        sqrt_at = {}
        rec_at = {}
        rep_at = {}
        mul_at = {}
        o_repl = [ps_b1[:, 0:512], ps_b1[:, 512:1024], ps_m1[:, 0:512], ps_m1[:, 512:1024]]
        ca.wait(T, at=osq_at)
        for i in range(16):
            if i < 8:
                h = i
                ct_.inc(T.matmul(ss_slots[h], ones128_bf[:, :], osq[:, h, :],
                                 start=True, stop=True))
                ssum_at[h] = ct_.n
                if h == 7:
                    ct_.wait(V)
                    ca.wait(V, at=zsil_at)
                    V.scalar_tensor_tensor(osq[:, :, :], og_sc[:, :, :], normw_sb[:, :],
                                           zsil[:, :, :], ALU.mult, ALU.mult)
            if i >= 2 and i - 2 < 8:
                h = i - 2
                f2 = h % 2
                ct_.wait(A, at=ssum_at[h])
                if h >= 2:
                    cv.wait(A, at=rec_at[h - 2])
                A.activation(stat_f[0:1, f2 * 512:(f2 + 1) * 512], ss_slots[h],
                             AF.Sqrt, bias=1e-6, scale=1.0 / 128.0)
                sqrt_at[h] = ca.n
            if i >= 3 and i - 3 < 8:
                h = i - 3
                f2 = h % 2
                ca.wait(V, at=sqrt_at[h])
                if h >= 2:
                    ct_.wait(V, at=rep_at[h - 2])
                V.reciprocal(stat_b[0:1, f2 * 512:(f2 + 1) * 512],
                             stat_f[0:1, f2 * 512:(f2 + 1) * 512])
                rec_at[h] = cv.n
            if i >= 8 and i - 8 < 8:
                h = i - 8
                ct_.wait(V, at=rep_at[h])
                V.tensor_mul(og_sc[:, h, :], osq[:, h, :], o_repl[h % 4])
                mul_at[h] = cv.n
            if i >= 4 and i - 4 < 8:
                h = i - 4
                f2 = h % 2
                cv.wait(T, at=rec_at[h])
                if h >= 4:
                    cv.wait(T, at=mul_at[h - 4])
                ct_.inc(T.matmul(o_repl[h % 4], ones1_bf[:, :],
                                 stat_b[0:1, f2 * 512:(f2 + 1) * 512], start=True, stop=True))
                rep_at[h] = ct_.n

        cv.wait(S)
        cog.inc(S.dma_start(
            out=cc_in.ap()[sc].rearrange("(h p) t -> p h t", p=128),
            in_=og_sc[:, :, :]), 16)
        og_stored = cog.n
        _scB.__exit__(None, None, None)

    cog.wait(P, at=og_stored)
    ccc.inc(P.collective_compute(
        "AllGather", ALU.bypass, replica_groups=PAIRS,
        ins=[cc_in.ap()[NSC - 1].opt()], outs=[cc_out.ap()[NSC - 1].opt()]))
    for ctr in (ct_, cv, ca, cp, cx, *cw_s, cog, csd):
        for eng in (T, V, A, P, S):
            ctr.wait(eng)
    sbctx_ab.close()
    sbctx_w = contextlib.ExitStack()
    wgsl = sbctx_w.enter_context(nc.sbuf_tensor("wgsl", [128, 2, KCH * 128], BF16))
    wusl = sbctx_w.enter_context(nc.sbuf_tensor("wusl", [128, 2, KCH * 128], BF16))
    wg8 = sbctx_w.enter_context(nc.sbuf_tensor("wg8", [128, 2, KCH * 128], U8))
    wu8 = sbctx_w.enter_context(nc.sbuf_tensor("wu8", [128, 2, KCH * 128], U8))
    gu_loaded = {}
    cast_a = {}
    cast_v = {}

    def gu_dma(it, buf):
        sl8 = it % 2
        qe = S if sl8 == 0 else P
        if sl8 in cast_a:
            ca.wait(qe, at=cast_a[sl8][1])
            cv.wait(qe, at=cast_v[sl8][1])
        cgu_s[sl8].inc(qe.dma_start(out=wg8[:, sl8, :], in_=wg_in[it].rearrange("p k c -> p (k c)")), 16)
        cgu_s[sl8].inc(qe.dma_start(out=wu8[:, sl8, :], in_=wu_in[it].rearrange("p k c -> p (k c)")), 16)
        gu_loaded[sl8] = cgu_s[sl8].n

    def gu_cast(it, buf):
        sl8 = it % 2
        if ("g", buf) in w_used:
            ct_.wait(A, at=w_used[("g", buf)])
            ct_.wait(V, at=w_used[("g", buf)])
        cgu_s[sl8].wait(A, at=gu_loaded[sl8])
        cgu_s[sl8].wait(V, at=gu_loaded[sl8])
        A.activation(wgsl[:, buf, :], wg8[:, sl8, :], AF.Copy, bias=0.0, scale=1.0)
        cast_a[sl8] = (buf, ca.n)
        V.tensor_copy(wusl[:, buf, :], wu8[:, sl8, :])
        cast_v[sl8] = (buf, cv.n)

    og_my = sb2("og_my", [128, NH, TOKH], BF16)
    og_peer = sb2("og_peer", [128, NH, TOKH], BF16)
    x_half = sb2("x_half", [128, KCH, TOKH], F32)
    araw = sb2("araw", [128, KCH, TOKH], BF16)
    wosl = sb2("wosl", [128, 2, 16 * 128], BF16)
    dsq2 = og_my[:, 2:4, :]
    dcp2 = og_my[:, 0:2, :]
    mrepD = og_my[:, 4, :]
    rrepD = og_my[:, 5, :]
    tD = ps_b1[:, 0:1024]
    hgv = pid % 2
    ccin_r = cc_in.ap().rearrange("s (h p) t -> p (s h) t", p=128)
    ccout_r = cc_out.ap().rearrange("s (b h p) t -> p (s b h) t", p=128, b=2)
    for i in range(2):
        cph.inc(P.dma_start(out=og_my[:, :, i * SC:(i + 1) * SC],
                            in_=ccin_r[:, ds((2 * hgv + i) * NH, NH), :]), 16)

    _scD = scope("D"); _scD.__enter__()
    wo_loaded = {}

    def wout_dma(dt, buf):
        if ("wo", buf) in w_used:
            ct_.wait(P, at=w_used[("wo", buf)])
        cwo_s[buf].inc(P.dma_start(out=wosl[:, buf, :], in_=wout[dt].rearrange("p v d -> p (v d)")), 16)
        wo_loaded[buf] = cwo_s[buf].n

    wout_dma(0, 0)
    wout_dma(1, 1)
    ccc.wait(P)
    for i in range(2):
        cph.inc(P.dma_start(
            out=og_peer[:, :, i * SC:(i + 1) * SC],
            in_=ccout_r[:, ds((2 * hgv + i) * 2 * NH + (1 - hgv) * NH, NH), :]), 16)
    og_done = cph.n
    cph.inc(P.dma_start(out=x_half[:, :, :],
                        in_=xT[:, ds(half_off, TOKH)].rearrange("(k p) t -> p k t", p=128)), 16)
    rotD = [ps_m1, ps_b1]
    sq_at = {}
    stat_at = {}

    def _sqv(dt):
        return araw[:, dt + 1, :] if dt + 1 < DT_FF else og_my[:, 2, :]

    def _araw_stats(dt):
        ca.wait(T, at=sq_at[dt])
        sv = _sqv(dt)
        _tmm = T.matmul(psA[0:1, 0:512], ones128_bf[:, :], araw[:, dt, 0:512],
                        start=(dt == 0), stop=(dt == DT_FF - 1))
        _tmm = T.matmul(psA[64:65, 0:512], ones128_bf[:, :], araw[:, dt, 512:1024],
                        start=(dt == 0), stop=(dt == DT_FF - 1))
        _tmm = T.matmul(psA[32:33, 0:512], ones128_bf[:, :], sv[:, 0:512],
                        start=(dt == 0), stop=(dt == DT_FF - 1))
        _tmm = T.matmul(ps_b2[0:1, 0:512], ones128_bf[:, :], sv[:, 512:1024],
                        start=(dt == 0), stop=(dt == DT_FF - 1))
        ct_.inc(_tmm)
        stat_at[dt % 2] = ct_.n

    for dt in range(DT_FF):
        buf = dt % 2
        if 1 <= dt and dt + 1 < DT_FF:
            wout_dma(dt + 1, (dt + 1) % 2)
        cph.wait(T, at=og_done)
        cwo_s[buf].wait(T, at=wo_loaded[buf])
        if dt >= 2:
            ca.wait(T)
        pb = rotD[dt % 2]
        for vc in range(16):
            og_src = og_my[:, vc, :] if vc < 8 else og_peer[:, vc - 8, :]
            lhs = wosl[:, buf, vc * 128:(vc + 1) * 128]
            _tmm = T.matmul(pb[:, 0:512], lhs, og_src[:, 0:512], start=(vc == 0), stop=(vc == 15))
            _tmm = T.matmul(pb[:, 512:1024], lhs, og_src[:, 512:1024], start=(vc == 0), stop=(vc == 15))
        ct_.inc(_tmm)
        w_used[("wo", buf)] = ct_.n
        if dt >= 1:
            _araw_stats(dt - 1)
        ct_.wait(A)
        A.activation(araw[:, dt, :], pb[:, 0:1024], AF.Copy, bias=0.0, scale=1.0)
        A.activation(_sqv(dt), araw[:, dt, :], AF.Square, bias=0.0, scale=1.0)
        sq_at[dt] = ca.n
    _araw_stats(DT_FF - 1)

    gu_dma(0, 0)
    gu_dma(1, 1)
    stats_epilogue(psA[0:1, 0:512], psA[32:33, 0:512],
                   mrepD[:, 0:512], rrepD[:, 0:512], 1e-5, repl_ps=ps_b1)
    stats_epilogue(psA[64:65, 0:512], ps_b2[0:1, 0:512],
                   mrepD[:, 512:1024], rrepD[:, 512:1024], 1e-5, repl_ps=ps_b1)

    ca.wait(V)
    cph.wait(V)
    ca.wait(T)
    cv.wait(T)
    for k in range(KCH):
        db = k % 2
        V.tensor_sub(tD[:, :], araw[:, k, :], mrepD[:, :])
        V.tensor_mul(tD[:, :], tD[:, :], rrepD[:, :])
        V.tensor_add(x_half[:, k, :], x_half[:, k, :], tD[:, :])
        V.engine_nop()
        mid_at = cv.n
        cv.wait(A, at=mid_at)
        if k >= 2:
            ct_.wait(A, at=stat_at[db])
        A.activation(dcp2[:, db, :], x_half[:, k, :], AF.Copy, bias=0.0, scale=1.0)
        A.activation(dsq2[:, db, :], dcp2[:, db, :], AF.Square, bias=0.0, scale=1.0)
        sq_at[db] = ca.n
        ca.wait(T, at=sq_at[db])
        _tmm = T.matmul(psA[0:1, 0:512], ones128_bf[:, :], dcp2[:, db, 0:512],
                        start=(k == 0), stop=(k == KCH - 1))
        _tmm = T.matmul(psA[64:65, 0:512], ones128_bf[:, :], dcp2[:, db, 512:1024],
                        start=(k == 0), stop=(k == KCH - 1))
        _tmm = T.matmul(psA[32:33, 0:512], ones128_bf[:, :], dsq2[:, db, 0:512],
                        start=(k == 0), stop=(k == KCH - 1))
        _tmm = T.matmul(ps_b2[0:1, 0:512], ones128_bf[:, :], dsq2[:, db, 512:1024],
                        start=(k == 0), stop=(k == KCH - 1))
        ct_.inc(_tmm)
        stat_at[db] = ct_.n
    if "mid" in dbg:
        ca.wait(S)
        cv.wait(S)
        cdbg.inc(S.dma_start(out=dbg_out["araw"], in_=araw[:, :, :]), 16)
        cdbg.inc(S.dma_start(out=dbg_out["xmid"], in_=x_half[:, :, :]), 16)

    stats_epilogue(psA[0:1, 0:512], psA[32:33, 0:512],
                   mrepD[:, 0:512], rrepD[:, 0:512], 1e-5, repl_ps=ps_b1)
    stats_epilogue(psA[64:65, 0:512], ps_b2[0:1, 0:512],
                   mrepD[:, 512:1024], rrepD[:, 512:1024], 1e-5, repl_ps=ps_b1)

    gu_cast(0, 0)
    gu_cast(1, 1)
    tD2 = ps_b2[:, 0:1024]
    ca.wait(V)
    ca.wait(T)
    cv.wait(T)
    ca.wait(T, at=cast_a[0][1])
    cv.wait(T, at=cast_v[0][1])
    ca.wait(T, at=cast_a[1][1])
    cv.wait(T, at=cast_v[1][1])
    pbE2 = [ps_m1, ps_b1]
    for k in range(KCH):
        V.tensor_sub(tD2[:, :], x_half[:, k, :], mrepD[:, :])
        V.tensor_mul(ln2[:, k, :], tD2[:, :], rrepD[:, :])
        ln2_at = cv.n
        cv.wait(T, at=ln2_at)
        _tmm = T.matmul(psA[0:1, 0:512], ones128_bf[:, :], ln2[:, k, 0:512],
                        start=(k == 0), stop=(k == KCH - 1))
        _tmm = T.matmul(psA[32:33, 0:512], ones128_bf[:, :], ln2[:, k, 512:1024],
                        start=(k == 0), stop=(k == KCH - 1))
        for itt in range(2):
            pbE = pbE2[itt]
            _tmm = T.matmul(pbE[:, 0:512], wgsl[:, itt, k * 128:(k + 1) * 128],
                            ln2[:, k, 0:512], start=(k == 0), stop=(k == KCH - 1))
            _tmm = T.matmul(pbE[:, 512:1024], wusl[:, itt, k * 128:(k + 1) * 128],
                            ln2[:, k, 0:512], start=(k == 0), stop=(k == KCH - 1))
        ct_.inc(_tmm)
    V.engine_nop()
    if "ln2" in dbg:
        cv.wait(S)
        cdbg.inc(S.dma_start(out=dbg_out["ln2"], in_=ln2[:, :, :]), 16)
    cv.wait(S)
    cspill.inc(S.dma_start(out=xmid_dram.ap().rearrange("(k p) t -> p k t", p=128),
                         in_=x_half[:, :, :]), 16)

    if stop_after == "D":
        cdbg.wait(S)
        cspill.wait(S)
        return nc, dbg_out

    _scD.__exit__(None, None, None)
    _scE = scope("E"); _scE.__enter__()
    xmid_used = {}
    out_stored = {}
    for ctr in (ct_, cv, ca, cp, cph, cwo_s[0], cwo_s[1], cspill, cdbg):
        for eng in (T, V, A, P, S):
            ctr.wait(eng)
    sbctx_2.close()
    h2 = sb3("h2", [128, IT_FF, FGRP], BF16)
    wdsl = sb3("wdsl", [128, 4, 16 * 128], BF16)
    gact = sb3("gact", [128, 2, FGRP], BF16)
    udeq = sb3("udeq", [128, FGRP], BF16)
    draw = sb3("draw", [128, DT_FF, FGRP], BF16)
    rs2grp = sb3("rs2grp", [1, 2, FGRP], BF16)
    rsrep = sb3("rsrep", [128, FGRP], BF16)
    rs2rep = sb3("rs2rep", [128, FGRP], BF16)
    tEu = sb3("tEu", [128, FGRP], F32)
    rs2_bf = sb3("rs2_bf", [1, FGRP], BF16)
    xmid_sb = sb3("xmid_sb", [128, 2, FGRP], F32)
    outb = sb3("outb", [128, 2, FGRP], F32)
    tE = sb3("tE", [128, FGRP], F32)
    dsq3 = sb3("dsq3", [128, 2, 512], BF16)
    mrepE = sb3("mrepE", [128, 512], BF16)
    rrepE = sb3("rrepE", [128, 512], BF16)
    ct_.wait(V)
    V.tensor_copy(rs2grp[0:1, 0, :], psA[0:1, 0:512])
    V.tensor_copy(rs2grp[0:1, 1, :], psA[32:33, 0:512])
    for grp in range(2):
        g0 = grp * FGRP
        rs_bf = rs2grp[0:1, grp, :]
        cv.wait(T)
        ca.wait(T)
        ct_.inc(T.matmul(psA[:, 0:512], ones1_bf[:, :], rs_bf, start=True, stop=True))
        ct_.wait(A)
        A.activation(rsrep[:, :], psA[:, 0:512], AF.Copy, bias=0.0, scale=128.0)
        rsrep_at = ca.n

        if grp == 1:
            gu_cast(0, 0)
        rotE = [ps_m1, ps_b1, ps_b2]
        evac_done = {}
        gact_used = {}
        cv.wait(T)
        for it in range(IT_FF):
            buf = it % 2
            if it + 2 < IT_FF:
                gu_dma(it + 2, 0)
            if it + 1 < IT_FF and not (grp == 0 and it + 1 < 2):
                gu_cast(it + 1, (it + 1) % 2)
            bank = it % 3
            if bank in evac_done:
                ca.wait(T, at=evac_done[bank][0])
                cv.wait(T, at=evac_done[bank][1])
            pb = rotE[bank]
            if grp == 0 and it < 2:
                w_used[("g", buf)] = ct_.n
            else:
                ca.wait(T, at=cast_a[it % 2][1])
                cv.wait(T, at=cast_v[it % 2][1])
                for k in range(KCH):
                    _tmm = T.matmul(pb[:, 0:512], wgsl[:, buf, k * 128:(k + 1) * 128],
                             ln2[:, k, g0:g0 + 512], start=(k == 0), stop=(k == KCH - 1))
                ct_.inc(_tmm)
                for k in range(KCH):
                    _tmm = T.matmul(pb[:, 512:1024], wusl[:, buf, k * 128:(k + 1) * 128],
                             ln2[:, k, g0:g0 + 512], start=(k == 0), stop=(k == KCH - 1))
                ct_.inc(_tmm)
                w_used[("g", buf)] = ct_.n
            ct_.wait(V)
            ca.wait(V, at=silu_at if it >= 1 else rsrep_at)
            V.tensor_sub(tE[:, :], pb[:, 0:512], rsrep[:, :])
            V.tensor_sub(tEu[:, :], pb[:, 512:1024], rsrep[:, :])
            sub_at = cv.n
            gb = it % 2
            if gb in gact_used:
                cv.wait(A, at=gact_used[gb])
            cv.wait(A, at=sub_at)
            A.activation(gact[:, gb, :], tE[:, :], AF.Silu, bias=0.0,
                                scale=sg_sb[:, it:it + 1])
            silu_at = ca.n
            ca.wait(V)
            V.scalar_tensor_tensor(h2[:, it, :], tEu[:, :], su_sb[:, it:it + 1],
                                   gact[:, gb, :], ALU.mult, ALU.mult)
            gact_used[gb] = cv.n
            evac_done[bank] = (ca.n, sub_at)
        cv.wait(T)
        for ic in range(IT_FF):
            _tmm = T.matmul(psA[0:1, 0:512], ones128_bf[:, :], h2[:, ic, :],
                     start=(ic == 0), stop=(ic == IT_FF - 1))
        ct_.inc(_tmm)
        ct_.wait(V)
        V.tensor_copy(rs2_bf[0:1, :], psA[0:1, 0:512])
        cv.wait(T)
        ct_.inc(T.matmul(psA[:, 0:512], ones1_bf[:, :], rs2_bf[0:1, :], start=True, stop=True))
        ct_.wait(A)
        A.activation(rs2rep[:, :], psA[:, 0:512], AF.Copy, bias=0.0, scale=128.0)
        ca.wait(V)

        wd_loaded = {}

        def wd_dma(dt, icg, buf):
            if ("d", buf) in w_used:
                ct_.wait(P, at=w_used[("d", buf)])
            cwd_s[buf].inc(P.dma_start(out=wdsl[:, buf, :],
                                in_=wd_in[dt, :, icg * 16:(icg + 1) * 16, :]
                                .rearrange("p c d -> p (c d)")), 16)
            wd_loaded[buf] = cwd_s[buf].n

        for _pf in range(3):
            wd_dma(0, _pf, _pf)
        if grp == 0:
            gu_dma(0, 0)
            gu_dma(1, 1)
        nslab = 0
        for dt in range(DT_FF):
            bank = dt % 3
            pb = rotE[bank]
            if bank in evac_done:
                ca.wait(T, at=evac_done[bank][0])
                cv.wait(T, at=evac_done[bank][1])
            for icg in range(4):
                buf = nslab % 4
                if nslab + 3 < DT_FF * 4:
                    nxt = nslab + 3
                    wd_dma(nxt // 4, nxt % 4, nxt % 4)
                cwd_s[buf].wait(T, at=wd_loaded[buf])
                for ici in range(16):
                    ic = icg * 16 + ici
                    _tmm = T.matmul(pb[:, 0:512], wdsl[:, buf, ici * 128:(ici + 1) * 128], h2[:, ic, :],
                             start=(ic == 0), stop=(ic == IT_FF - 1))
                ct_.inc(_tmm)
                w_used[("d", buf)] = ct_.n
                nslab += 1
            ct_.wait(V)
            V.tensor_sub(tE[:, :], pb[:, 0:512], rs2rep[:, :])
            V.tensor_scalar(draw[:, dt, :], tE[:, :], sd_sb[:, dt:dt + 1],
                                   None, ALU.mult)
            evac_done[bank] = (ca.n, cv.n)
        ca.wait(T)
        cv.wait(T)
        cv.wait(A)
        for k in range(DT_FF):
            db = k % 2
            if k >= 2:
                ct_.wait(A, at=sq_used[db])
            A.activation(dsq3[:, db, :], draw[:, k, :], AF.Square, bias=0.0, scale=1.0)
            ca.wait(T)
            _tmm = T.matmul(psA[0:1, 0:512], ones128_bf[:, :], draw[:, k, :],
                     start=(k == 0), stop=(k == DT_FF - 1))
            _tmm = T.matmul(psA[32:33, 0:512], ones128_bf[:, :], dsq3[:, db, :],
                     start=(k == 0), stop=(k == DT_FF - 1))
            ct_.inc(_tmm)
            if k == 0:
                sq_used = {}
            sq_used[db] = ct_.n
        stats_epilogue(psA[0:1, 0:512], psA[32:33, 0:512],
                       mrepE[:, :], rrepE[:, :], 1e-5)
        cspill.wait(P)
        xmid_view = xmid_dram.ap().rearrange("(k p) t -> p k t", p=128)
        cxm_s[0].inc(P.dma_start(out=xmid_sb[:, 0, :], in_=xmid_view[:, 0, g0:g0 + 512]), 16)
        xm_loaded = {0: cxm_s[0].n}
        for k in range(KCH):
            bo = k % 2
            if k + 1 < KCH:
                nb = (k + 1) % 2
                if nb in xmid_used:
                    cv.wait(P, at=xmid_used[nb])
                cxm_s[nb].inc(P.dma_start(out=xmid_sb[:, nb, :],
                                    in_=xmid_view[:, k + 1, g0:g0 + 512]), 16)
                xm_loaded[nb] = cxm_s[nb].n
            ca.wait(V)
            cxm_s[bo].wait(V, at=xm_loaded[bo])
            if bo in out_stored:
                cyb[bo].wait(V)
            V.tensor_sub(tE[:, :], draw[:, k, :], mrepE[:, :])
            V.tensor_mul(tE[:, :], tE[:, :], rrepE[:, :])
            V.tensor_add(outb[:, bo, :], xmid_sb[:, bo, :], tE[:, :])
            V.engine_nop()
            xmid_used[bo] = cv.n
            cv.wait(S)
            cyb[bo].inc(S.dma_start(out=y_out.rearrange("(k p) t -> p k t", p=128)[:, k, g0:g0 + 512],
                                    in_=outb[:, bo, :]), 16)
            out_stored[bo] = cyb[bo].n
    cyb[0].wait(S)
    cyb[1].wait(S)
    cdbg.wait(S)
    _scE.__exit__(None, None, None)
    return nc, dbg_out


def _prep_inputs(inputs, c):
    b, hg = c // 2, c % 2
    x = inputs["L0_attn_x"]
    Wqkv = np.asarray(inputs["L0_attn_Wqkv"], np.float32)
    Wz = np.asarray(inputs["L0_attn_Wz"], np.float32)
    Wb = np.asarray(inputs["L0_attn_Wb"], np.float32)
    Wa = np.asarray(inputs["L0_attn_Wa"], np.float32)
    conv_w = np.asarray(inputs["L0_attn_conv_w"], np.float32)
    Wout = np.asarray(inputs["L0_attn_Wout"], np.float32)
    hsl = slice(hg * 1024, hg * 1024 + 1024)
    qs, ks, vs = Wqkv[:, 0:2048][:, hsl], Wqkv[:, 2048:4096][:, hsl], Wqkv[:, 4096:6144][:, hsl]
    Wslice = np.concatenate([qs, ks, vs], axis=1)
    wqkv_t = np.ascontiguousarray(Wslice.reshape(16, 128, 24, 128).transpose(2, 1, 0, 3))
    wz_t = np.ascontiguousarray(Wz[:, hsl].reshape(16, 128, 8, 128).transpose(2, 1, 0, 3))
    hh = slice(hg * 8, hg * 8 + 8)
    wba_t = np.ascontiguousarray(
        np.stack([Wb[:, hh].reshape(16, 128, 8), Wa[:, hh].reshape(16, 128, 8)])
        .transpose(2, 0, 1, 3))
    cw = conv_w[:, 0, :]
    cws = np.concatenate([cw[0:2048][hsl], cw[2048:4096][hsl], cw[4096:6144][hsl]], axis=0)
    convw_t = np.ascontiguousarray(cws.reshape(24, 128, 4).transpose(1, 0, 2))
    Wp = np.concatenate([Wout[hg * 1024:hg * 1024 + 1024],
                         Wout[(1 - hg) * 1024:(1 - hg) * 1024 + 1024]], axis=0)
    wout_t = np.ascontiguousarray(Wp.reshape(16, 128, 16, 128).transpose(2, 1, 0, 3))
    return {
        "xT": np.ascontiguousarray(np.asarray(x, np.float32)[b].T),
        "wqkv": wqkv_t, "wz": wz_t, "wba": wba_t, "convw": convw_t,
        "alog": np.ascontiguousarray(
            np.asarray(inputs["L0_attn_A_log"], np.float32)[hh].reshape(8, 1)),
        "dtb": np.ascontiguousarray(
            np.asarray(inputs["L0_attn_dt_bias"], np.float32)[hh].reshape(8, 1)),
        "normw": np.ascontiguousarray(
            np.asarray(inputs["L0_attn_norm_w"], np.float32).reshape(128, 1)),
        "wout": wout_t,
    }


def _prep_shared(inputs):
    def u8view(a):
        a = np.ascontiguousarray(a)
        assert a.dtype == np.int32
        return a.view(np.uint8).reshape(*a.shape, 4)[..., 0]

    qg = u8view(inputs["L0_ffn_gate_w"])
    qu = u8view(inputs["L0_ffn_up_w"])
    qd = u8view(inputs["L0_ffn_down_w"])
    wg_t = np.ascontiguousarray(qg.reshape(64, 128, 16, 128).transpose(0, 3, 2, 1))
    wu_t = np.ascontiguousarray(qu.reshape(64, 128, 16, 128).transpose(0, 3, 2, 1))
    wd_t = np.ascontiguousarray(qd.reshape(16, 128, 64, 128).transpose(0, 3, 2, 1))
    return {
        "wg": wg_t, "wu": wu_t, "wd": wd_t,
        "sg": np.ascontiguousarray(
            np.asarray(inputs["L0_ffn_gate_scale"], np.float32).reshape(64, 128).T),
        "su": np.ascontiguousarray(
            np.asarray(inputs["L0_ffn_up_scale"], np.float32).reshape(64, 128).T),
        "sd": np.ascontiguousarray(
            np.asarray(inputs["L0_ffn_down_scale"], np.float32).reshape(16, 128).T),
        "id128": np.eye(128, dtype=BF),
        "id8": np.eye(8, dtype=np.float32),
    }


def make_in_maps(inputs):
    shared = _prep_shared(inputs)
    in_maps = []
    for c in range(8):
        m = _prep_inputs(inputs, c)
        m.update(shared)
        in_maps.append(m)
    return in_maps


def kernel(**inputs):
    in_maps = make_in_maps(inputs)
    nc, _ = build_nc()
    res = run_bass_kernel_spmd(nc, in_maps, core_ids=list(range(8)))
    out = np.empty((4, 2048, 2048), np.float32)
    for c in range(8):
        b, hg = c // 2, c % 2
        out[b, hg * 1024:(hg + 1) * 1024, :] = res.results[c]["y"].T
    return out


# version 46 (final)
# speedup vs baseline: 1.1744x; 1.0022x over previous
import dataclasses
import numpy as np
import ml_dtypes
import concourse.bass as bass
import concourse.mybir as mybir
from concourse.bass import ds
from concourse.bass_utils import run_bass_kernel_spmd

F32 = mybir.dt.float32
BF16 = mybir.dt.bfloat16
U8 = mybir.dt.uint8
AF = mybir.ActivationFunctionType
ALU = mybir.AluOpType
BF = ml_dtypes.bfloat16

L = 2048
D = 2048
NH = 8
C = 128
SC = 512
NSC = L // SC
NCH = SC // C
KCH = D // 128
QK_CT = 16
N_CT = 24
Z_CT = 8
TOKH = 1024
IT_FF = 64
DT_FF = 16
FGRP = 512


class EngWrap:

    _PASS = {"wait_ge", "nop", "dma_start", "collective_compute", "partition_id",
             "indirect_dma_start", "sem_inc", "drain"}

    def __init__(self, eng, ctr):
        self._e = eng
        self._c = ctr

    def __getattr__(self, name):
        f = getattr(self._e, name)
        if name in self._PASS or name.startswith("_"):
            return f

        def g(*a, **k):
            self._e.wait_ge(self._c.h, self._c.n)
            r = f(*a, **k)
            self._c.inc(r)
            return r
        return g


class Ctr:
    def __init__(self, nc, name):
        self.h = nc.alloc_semaphore(name)
        self.n = 0
        self._w = {}

    def inc(self, instr, k=1):
        instr.then_inc(self.h, k)
        self.n += k
        return instr

    def pre(self, eng):
        eng.wait_ge(self.h, self.n)

    def wait(self, eng, at=None):
        n = self.n if at is None else at
        e = getattr(eng, "_e", eng)
        k = id(e)
        if self._w.get(k, 0) >= n:
            return
        e.wait_ge(self.h, n)
        self._w[k] = n


def _bview(ap, count):
    return dataclasses.replace(ap, ap=[ap.ap[0], [0, count], ap.ap[1]])


def _bview3(ap, inner=128):
    return dataclasses.replace(ap, ap=[ap.ap[0], ap.ap[1], [0, inner]])


def _as3(ap, h=8, inner=128):
    return dataclasses.replace(ap, ap=[ap.ap[0], [inner, h], [1, inner]])


def build_nc(dbg=(), stop_after=None):
    nc = bass.Bass("TRN2", target_bir_lowering=False)
    T, V, A, P, S = nc.tensor, nc.vector, nc.scalar, nc.gpsimd, nc.sync
    _lp = nc.allow_low_precision("bf16 operands with fp32 PSUM accumulation per problem tolerance")
    _lp.__enter__()

    xT = nc.dram_tensor("xT", [D, L], F32, kind="ExternalInput").ap()
    wqkv = nc.dram_tensor("wqkv", [N_CT, 128, KCH, 128], F32, kind="ExternalInput").ap()
    wz = nc.dram_tensor("wz", [Z_CT, 128, KCH, 128], F32, kind="ExternalInput").ap()
    wba = nc.dram_tensor("wba", [128, 2, KCH, 8], F32, kind="ExternalInput").ap()
    convw = nc.dram_tensor("convw", [128, N_CT, 4], F32, kind="ExternalInput").ap()
    alog = nc.dram_tensor("alog", [8, 1], F32, kind="ExternalInput").ap()
    dtb = nc.dram_tensor("dtb", [8, 1], F32, kind="ExternalInput").ap()
    normw = nc.dram_tensor("normw", [128, 1], F32, kind="ExternalInput").ap()
    wout = nc.dram_tensor("wout", [DT_FF, 128, 16, 128], F32, kind="ExternalInput").ap()
    wg_in = nc.dram_tensor("wg", [IT_FF, 128, KCH, 128], U8, kind="ExternalInput").ap()
    wu_in = nc.dram_tensor("wu", [IT_FF, 128, KCH, 128], U8, kind="ExternalInput").ap()
    wd_in = nc.dram_tensor("wd", [DT_FF, 128, IT_FF, 128], U8, kind="ExternalInput").ap()
    sg_in = nc.dram_tensor("sg", [128, IT_FF], F32, kind="ExternalInput").ap()
    su_in = nc.dram_tensor("su", [128, IT_FF], F32, kind="ExternalInput").ap()
    sd_in = nc.dram_tensor("sd", [128, DT_FF], F32, kind="ExternalInput").ap()
    id128_in = nc.dram_tensor("id128", [128, 128], BF16, kind="ExternalInput").ap()
    id8_in = nc.dram_tensor("id8", [8, 8], F32, kind="ExternalInput").ap()
    y_out = nc.dram_tensor("y", [D, TOKH], F32, kind="ExternalOutput").ap()
    dbg_out = {}
    if "proj" in dbg:
        dbg_out["qhat"] = nc.dram_tensor("dbg_qhat", [128, N_CT, SC], BF16, kind="ExternalOutput").ap()
        dbg_out["zsc"] = nc.dram_tensor("dbg_zsc", [128, Z_CT, SC], BF16, kind="ExternalOutput").ap()
        dbg_out["bg"] = nc.dram_tensor("dbg_bg", [8, 3, SC], F32, kind="ExternalOutput").ap()
    if "og" in dbg:
        dbg_out["og"] = nc.dram_tensor("dbg_og", [128, NH, L], BF16, kind="ExternalOutput").ap()
    if "mid" in dbg:
        dbg_out["araw"] = nc.dram_tensor("dbg_araw", [128, KCH, TOKH], BF16, kind="ExternalOutput").ap()
        dbg_out["xmid"] = nc.dram_tensor("dbg_xmid", [128, KCH, TOKH], F32, kind="ExternalOutput").ap()
    if "ln2" in dbg:
        dbg_out["ln2"] = nc.dram_tensor("dbg_ln2", [128, KCH, TOKH], BF16, kind="ExternalOutput").ap()

    dbgb = nc.dram_tensor("dbgb", [4, 128, 1024], BF16, kind="ExternalOutput").ap()
    dbgf = nc.dram_tensor("dbgf", [4, 128, 1024], F32, kind="ExternalOutput").ap()
    cc_in = nc.dram_tensor("cc_in", [NSC, NH * 128, SC], BF16, kind="Internal")
    cc_out = nc.dram_tensor("cc_out", [NSC, 2 * NH * 128, SC], BF16, kind="Internal")
    xmid_dram = nc.dram_tensor("xmid_dram", [D, TOKH], F32, kind="Internal")
    wbf_dram = nc.dram_tensor("wbf_dram", [N_CT + Z_CT, 128, KCH * 128], BF16, kind="Internal")

    import contextlib
    sbctx_ab = contextlib.ExitStack()
    sbctx_2 = contextlib.ExitStack()
    sbctx_3 = contextlib.ExitStack()

    def sb(name, shape, dtype):
        return nc.alloc_sbuf_tensor(name, shape, dtype)

    def sbA(name, shape, dtype):
        return sbctx_ab.enter_context(nc.sbuf_tensor(name, shape, dtype))

    def sb2(name, shape, dtype):
        return sbctx_2.enter_context(nc.sbuf_tensor(name, shape, dtype))

    def sb3(name, shape, dtype):
        return sbctx_3.enter_context(nc.sbuf_tensor(name, shape, dtype))

    ones128_bf = sb("ones128_bf", [128, 1], BF16)
    ones1_bf = sb("ones1_bf", [1, 128], BF16)
    ones8_f = sb("ones8_f", [8, 128], F32)
    tri_ji = sb("tri_ji", [128, 128], F32)
    blockid = sb("blockid", [8, 8 * 128], F32)
    m128row = sb("m128row", [1, 128], BF16)
    id128 = sb("id128s", [128, 128], BF16)
    id8 = sb("id8s", [8, 8], F32)
    convw_sb = sb("convw_sb", [128, N_CT, 4], F32)
    wba_sb = sb("wba_sb", [128, 2, KCH, 8], BF16)
    alog_sb = sb("alog_sb", [8, 1], F32)
    dtb_sb = sb("dtb_sb", [8, 1], F32)
    negA = sb("negA", [8, 1], F32)
    normw_sb = sb("normw_sb", [128, 1], F32)
    sg_sb = sb("sg_sb", [128, IT_FF], F32)
    su_sb = sb("su_sb", [128, IT_FF], F32)
    sd_sb = sb("sd_sb", [128, DT_FF], F32)
    constt = sb("constt", [128, 8], F32)
    stat_f = sb("stat_f", [1, 2 * SC], F32)
    stat_b = sb("stat_b", [1, 2 * SC], BF16)
    ln2 = sb("ln2", [128, KCH, TOKH], BF16)

    ln1 = sbA("ln1", [128, KCH, SC], BF16)
    mrep = sbA("mrep", [128, SC], BF16)
    rrep = sbA("rrep", [128, SC], BF16)
    wslab = sbA("wslab", [128, 3, KCH * 128], BF16)
    qraw = sbA("qraw", [128, N_CT, 520], BF16)
    tails = sbA("tails", [128, N_CT, 4], BF16)
    qhat = sbA("qhat", [128, N_CT, SC], BF16)
    z_sc = sbA("z_sc", [128, Z_CT, SC], BF16)
    bfm = sbA("bfm", [8, SC], F32)
    lnbfm = sbA("lnbfm", [8, SC], F32)
    gfm = sbA("gfm", [8, SC], F32)
    sp8a = sbA("sp8a", [8, SC], F32)
    sp8b = sbA("sp8b", [8, SC], F32)

    og_sc = sbA("og_sc", [128, NH, SC], BF16)
    M_f32 = sbA("M_f32", [128, NH, 128], F32)
    Mbf = sbA("Mbf", [128, NH, 128], BF16)

    g_tm = sbA("g_tm", [128, 8], F32)
    lnb_tm = sbA("lnb_tm", [128, 8], F32)
    G_tm = sbA("G_tm", [128, 8], F32)
    G8 = sbA("G8", [8, 128], F32)
    colA = sbA("colA", [128, 8], F32)
    rowB = sbA("rowB", [8, 128], F32)
    bdG = sbA("bdG", [8, 8 * 128], F32)
    bdB = sbA("bdB", [8, 8 * 128], F32)
    btm3 = sbA("btm3", [128, 3, 8], F32)
    etm3 = sbA("etm3", [128, 3, 8], F32)
    gam3 = sbA("gam3", [128, 3, 8], F32)
    t8a = sbA("t8a", [128, 8], F32)
    eG = sbA("eG", [128, NH, 128], BF16)
    nW = sbA("nW", [128, NH, 128], F32)
    nWy = sbA("nWy", [128, NH, 128], F32)
    eEb = sbA("eEb", [128, NH, 128], BF16)
    kk_sb = sbA("kk_sb", [128, NH, 128], BF16)
    qk_sb = sbA("qk_sb", [128, NH, 128], BF16)
    A_sb2 = sbA("A_sb2", [128, 2, NH, 128], BF16)
    AT_sb2 = sbA("AT_sb2", [128, 2, NH, 128], BF16)
    V6T2 = sbA("V6T2", [128, 2, NH, 128], BF16)
    Pw = [sbA("Pw0", [128, NH, 128], BF16), sbA("Pw1", [128, NH, 128], BF16)]
    PTw = [sbA("PTw0", [128, NH, 128], BF16), sbA("PTw1", [128, NH, 128], BF16)]
    Vcp = [sbA("Vc0", [128, NH, 128], BF16), sbA("Vc1", [128, NH, 128], BF16)]

    def _carve3(base, off, n0):
        b = base[:, :, :]
        return dataclasses.replace(b, offset=b.offset + off,
                                   ap=[b.ap[0], [8 * 128, n0], [128, 8], [1, 128]])

    def _carve1(base, off):
        b = base[:, :, :]
        return dataclasses.replace(b, offset=b.offset + off,
                                   ap=[b.ap[0], [128, 8], [1, 128]])

    vtm3 = _carve3(ln1, 0, 3)
    ktm3 = _carve3(ln1, 3072, 3)
    eEc_t = _carve1(ln1, 6144)
    ST3 = _carve3(qraw, 0, 3)
    qtil3 = _carve3(qraw, 3072, 3)
    ktil3 = _carve3(qraw, 6144, 3)
    R0t = _carve1(qraw, 9216)
    Ut = _carve1(qraw, 10240)
    eEa_t = _carve1(qraw, 11264)

    import contextlib
    pctx = contextlib.ExitStack()
    psA = pctx.enter_context(nc.psum_tensor("psA", [128, 512], F32))
    ps_b1 = pctx.enter_context(nc.psum_tensor("ps_b1", [128, 1024], F32))
    ps_b2 = pctx.enter_context(nc.psum_tensor("ps_b2", [128, 1024], F32))
    ps_m1 = pctx.enter_context(nc.psum_tensor("ps_m1", [128, 1024], F32))
    ps_tr = pctx.enter_context(nc.psum_tensor("ps_tr", [128, 1024], BF16))

    csd = Ctr(nc, "csd")
    cx = Ctr(nc, "cx")
    cw_s = [Ctr(nc, f"cw{i}") for i in range(4)]
    cwo_s = [Ctr(nc, "cwo0"), Ctr(nc, "cwo1")]
    cgu_s = [Ctr(nc, f"cgu{i}") for i in range(4)]
    cwd_s = [Ctr(nc, f"cwd{i}") for i in range(4)]
    cog = Ctr(nc, "cog")
    cph = Ctr(nc, "cph")
    cxm_s = [Ctr(nc, "cxm0"), Ctr(nc, "cxm1")]
    cspill = Ctr(nc, "cspill")
    cwb = Ctr(nc, "cwb")
    cyb = [Ctr(nc, "cy0"), Ctr(nc, "cy1")]
    cdbg = Ctr(nc, "cdbg")
    ct_ = Ctr(nc, "ct")
    cv = Ctr(nc, "cv")
    ca = Ctr(nc, "ca")
    cp = Ctr(nc, "cp")
    ccc = Ctr(nc, "ccc")

    V = EngWrap(V, cv)
    A = EngWrap(A, ca)
    P_raw = P
    P = EngWrap(P, cp)

    pid = P_raw.partition_id()
    half_off = (pid % 2) * TOKH
    peer_row = (pid + 1 - (pid % 2) * 2) * (NH * 128)

    def stats_epilogue(ms_ap, qs_ap, m_dst, r_dst, eps, repl_ps=None):
        ct_.wait(V)
        V.tensor_scalar_mul(stat_f[0:1, 0:512], ms_ap, 1.0 / D)
        V.tensor_copy(stat_b[0:1, 0:512], stat_f[0:1, 0:512])
        V.tensor_mul(stat_f[0:1, 512:1024], stat_f[0:1, 0:512], stat_f[0:1, 0:512])
        V.scalar_tensor_tensor(stat_f[0:1, 0:512], qs_ap, 1.0 / D,
                               stat_f[0:1, 512:1024], ALU.mult, ALU.subtract)
        V.engine_nop()
        cv.wait(A)
        A.activation(stat_f[0:1, 512:1024], stat_f[0:1, 0:512], AF.Sqrt,
                            bias=eps, scale=1.0)
        ca.wait(V)
        V.reciprocal(stat_b[0:1, 512:1024], stat_f[0:1, 512:1024])
        cv.wait(T)
        ca.wait(T)
        rp_ = ps_b2 if repl_ps is None else repl_ps
        _tmm = T.matmul(rp_[:, 0:512], ones1_bf[:, :], stat_b[0:1, 0:512], start=True, stop=True)
        ct_.inc(T.matmul(rp_[:, 512:1024], ones1_bf[:, :], stat_b[0:1, 512:1024],
                         start=True, stop=True))
        ct_.wait(A)
        A.activation(m_dst, rp_[:, 0:512], AF.Copy, bias=0.0, scale=1.0)
        A.activation(r_dst, rp_[:, 512:1024], AF.Copy, bias=0.0, scale=1.0)

    csd.inc(S.dma_start(out=convw_sb[:, :, :], in_=convw), 16)
    cx.inc(P.dma_start(out=wba_sb[:, :, :, :], in_=wba), 16)
    csd.inc(S.dma_start(out=alog_sb[:, :], in_=alog), 16)
    csd.inc(S.dma_start(out=dtb_sb[:, :], in_=dtb), 16)
    csd.inc(S.dma_start(out=normw_sb[:, :], in_=normw), 16)
    csd.inc(S.dma_start(out=sg_sb[:, :], in_=sg_in), 16)
    csd.inc(S.dma_start(out=su_sb[:, :], in_=su_in), 16)
    csd.inc(S.dma_start(out=sd_sb[:, :], in_=sd_in), 16)
    csd.inc(S.dma_start(out=id128[:, :], in_=id128_in), 16)
    csd.inc(S.dma_start(out=id8[:, :], in_=id8_in), 16)

    _cvals = [0.0, 1.0, -1.0, 1e-5, 1e-6, 128.0 * 1e-6, 128.0, 1.0 / 128.0]
    for _i, _v in enumerate(_cvals):
        V.memset(constt[:, _i:_i + 1], _v)
        nc.const_aps.register(F32, _v, constt[:, _i:_i + 1])
    V.memset(ones128_bf[:, :], 1.0)
    V.memset(ones1_bf[:, :], 1.0)
    V.memset(ones8_f[:, :], 1.0)
    V.memset(m128row[:, :], -128.0)
    V.memset(tails[:, :, :], 0.0)
    V.memset(M_f32[:, :, :], 0.0)
    V.memset(tri_ji[:, :], 1.0)
    V.memset(blockid[:, :], 1.0)
    cv.wait(P)
    P.affine_select(tri_ji[:, :], tri_ji[:, :], [[1, 128]], ALU.is_ge, 0.0,
                           channel_multiplier=-1)
    P.affine_select(blockid[:, :], blockid[:, :], [[-1, 8], [0, 128]],
                           ALU.is_equal, 0.0, channel_multiplier=1)
    csd.wait(A)
    cv.wait(A)
    A.activation(negA[:, :], alog_sb[:, :], AF.Exp, bias=0.0, scale=1.0)
    ca.wait(V)
    V.tensor_scalar_mul(negA[:, :], negA[:, :], -1.0)
    cp.wait(T)
    cp.wait(V)
    csd.wait(T)
    csd.wait(V)
    csd.wait(P)

    PAIRS = [[0, 1], [2, 3], [4, 5], [6, 7]]
    w_used = {}
    wb_at = {}
    og_stored = 0
    cc_issued = 0

    def scope(name):
        return nc.named_scope(name)

    for sc in range(NSC):
        _scA = scope(f"A{sc}"); _scA.__enter__()
        tok0 = sc * SC
        if sc > 0:
            ct_.wait(P)
            cv.wait(P)
            ca.wait(P)
            cdbg.wait(P)
        if sc == 0:
            cx.inc(P.dma_start(out=qhat[:, 0:16, :],
                               in_=xT[:, tok0:tok0 + SC].rearrange("(k p) t -> p k t", p=128)), 16)
        if sc == 0:
            l2done = {}
            cviews = {}
            tas = {}
            eg_at = {}
            etm_at = {}
            eea_at = {}
            eeb_at = {}
            eec_at = {}
            amask_at = {}
            stmask_at = {}
            vtm_at = {}
            ktm_at = {}
            ktil_at = {}
            mbf_at = {}
            y_r0 = {}
            u_at = {}
            bank_free = {}
            buf_read = {}
            buf_rdy = {}
            psm1_s = {"a": 0, "v": 0}
            pstr_s = {"a": 0, "v": 0}
            psa_s = {"a": 0, "v": 0}
            rot = [ps_b2, ps_b1]
            vread = {}

        def psm1_take():
            ca.wait(T, at=psm1_s["a"])
            cv.wait(T, at=psm1_s["v"])

        def bwait(eng, key):
            r = buf_rdy.get(key)
            if r is not None:
                (ca if r[0] == "A" else cv if r[0] == "V" else cp).wait(eng, at=r[1])

        def chain_mm(bank_i, lhs, lkey, rhs, rkey):
            if bank_i in bank_free:
                ca.wait(T, at=bank_free[bank_i][0])
                cv.wait(T, at=bank_free[bank_i][1])
            bwait(T, lkey)
            bwait(T, rkey)
            pb = rot[bank_i]
            for h in range(8):
                mm = T.matmul(pb[:, h * 128:(h + 1) * 128], lhs[:, h, :], rhs[:, h, :],
                              start=True, stop=True)
            ct_.inc(mm)
            buf_read[lkey] = ct_.n
            buf_read[rkey] = ct_.n
            return ct_.n

        def evac_copy(bank_i, dst, dkey, prod_at, eng="A"):
            if eng == "A":
                ct_.wait(A, at=max(prod_at, buf_read.get(dkey, 0)))
                A.activation(dst[:, :, :], rot[bank_i][:, 0:1024], AF.Copy, bias=0.0, scale=1.0)
                bank_free[bank_i] = (ca.n, bank_free.get(bank_i, (0, 0))[1])
                buf_rdy[dkey] = ("A", ca.n)
            else:
                ct_.wait(V, at=max(prod_at, buf_read.get(dkey, 0)))
                V.tensor_copy(dst[:, :, :], rot[bank_i][:, 0:1024])
                bank_free[bank_i] = (bank_free.get(bank_i, (0, 0))[0], cv.n)
                buf_rdy[dkey] = ("V", cv.n)

        def evac_add(bank_i, dst, dkey, a_src, prod_at):
            ct_.wait(V, at=max(prod_at, buf_read.get(dkey, 0)))
            V.tensor_add(dst[:, :, :], a_src[:, :, :], _as3(rot[bank_i][:, 0:1024]))
            bank_free[bank_i] = (bank_free.get(bank_i, (0, 0))[0], cv.n)
            buf_rdy[dkey] = ("V", cv.n)

        def u_prep_open(c):
            d2, d3 = c % 2, c % 3
            tas[c] = slice((c % NCH) * C, (c % NCH) * C + C)
            cviews[c] = dict(a=A_sb2[:, d2], at=AT_sb2[:, d2], st=ST3[:, d3],
                             kt=ktil3[:, d3], qt=qtil3[:, d3], vt=vtm3[:, d3],
                             km=ktm3[:, d3], v6=V6T2[:, d2], b=btm3[:, d3, :],
                             e=etm3[:, d3, :], g=gam3[:, d3, :])

        def u_p1(c):
            vw = cviews[c]; ta = tas[c]
            cv.wait(T, at=psa_s["v"])
            ca.wait(T, at=psa_s["a"])
            _tmm = T.transpose(psA[:, 0:8], gfm[:, ta], id8[:, :])
            _tmm = T.transpose(psA[:, 8:16], bfm[:, ta], id8[:, :])
            ct_.inc(T.transpose(psA[:, 16:24], lnbfm[:, ta], id8[:, :]))
            ct_.wait(V)
            V.tensor_copy(g_tm[:, :], psA[:, 0:8])
            V.tensor_copy(vw["b"], psA[:, 8:16])
            V.tensor_copy(lnb_tm[:, :], psA[:, 16:24])

        def u_p2(c):
            ta = tas[c]
            cv.wait(T)
            _tmm = T.matmul(psA[:, 24:32], tri_ji[:, :], g_tm[:, :], start=True, stop=True)
            ct_.inc(T.matmul(psA[0:8, 32:160], g_tm[:, :], tri_ji[:, :], start=True, stop=True))
            ct_.wait(V)
            V.tensor_copy(G_tm[:, :], psA[:, 24:32])
            V.tensor_copy(G8[:, :], psA[0:8, 32:160])
            V.tensor_add(colA[:, :], G_tm[:, :], lnb_tm[:, :])
            V.tensor_add(rowB[:, :], G8[:, :], lnbfm[:, ta])
            V.tensor_mul(bdG[:, :], _bview(G8[:, :], 8), blockid[:, :])
            V.tensor_mul(bdB[:, :], _bview(rowB[:, :], 8), blockid[:, :])
            psa_s["v"] = cv.n

        def u_p3a(c):
            vw = cviews[c]
            psm1_take()
            cv.wait(T)
            _tmm = T.matmul(ps_m1[:, 0:512], ones8_f[:, :], bdG[:, 0:512], start=True, stop=True)
            ct_.inc(T.matmul(ps_m1[:, 512:1024], ones8_f[:, :], bdG[:, 512:1024],
                             start=True, stop=True))
            ct_.wait(A)
            cv.wait(A)
            A.activation(eG[:, :, :], ps_m1[:, 0:1024], AF.Exp, bias=0.0, scale=1.0)
            eg_at[c] = ca.n
            pm1 = ps_m1[:, 0:1024]
            gb_last = dataclasses.replace(pm1, ap=[pm1.ap[0], [128, 8]],
                                          offset=pm1.offset + 127)
            ct_.wait(V)
            ca.wait(V)
            V.tensor_sub(t8a[:, :], gb_last, G_tm[:, :])
            cv.wait(A)
            A.activation(vw["e"], t8a[:, :], AF.Exp, bias=0.0, scale=1.0)
            A.activation(vw["g"], gb_last, AF.Exp, bias=0.0, scale=1.0)
            etm_at[c] = ca.n
            ca.wait(V)
            V.tensor_sub(nW[:, :, :], _as3(pm1), _bview3(colA[:, :]))
            V.tensor_scalar_max(nW[:, :, :], nW[:, :, :], 0.0)
            cv.wait(A)
            A.activation(eEa_t[:, :, :], nW[:, :, :], AF.Exp, bias=0.0, scale=-1.0)
            eea_at[c] = ca.n
            ca.wait(V)
            V.tensor_sub(nW[:, :, :], _as3(pm1), _bview3(G_tm[:, :]))
            V.tensor_scalar_min(nW[:, :, :], nW[:, :, :], 0.0)
            cv.wait(A)
            A.activation(eEc_t[:, :, :], nW[:, :, :], AF.Exp, bias=0.0, scale=1.0)
            eec_at[c] = ca.n
            psm1_s["a"] = ca.n
            psm1_s["v"] = cv.n

        def u_p3b(c):
            psm1_take()
            _tmm = T.matmul(ps_m1[:, 0:512], ones8_f[:, :], bdB[:, 0:512], start=True, stop=True)
            ct_.inc(T.matmul(ps_m1[:, 512:1024], ones8_f[:, :], bdB[:, 512:1024],
                             start=True, stop=True))
            ct_.wait(V)
            ca.wait(V, at=eec_at[c])
            V.tensor_sub(nW[:, :, :], _as3(ps_m1[:, 0:1024]), _bview3(G_tm[:, :]))
            V.tensor_scalar_min(nW[:, :, :], nW[:, :, :], 0.0)
            cv.wait(A)
            A.activation(eEb[:, :, :], nW[:, :, :], AF.Exp, bias=0.0, scale=1.0)
            eeb_at[c] = ca.n
            psm1_s["v"] = cv.n

        def u_p4(c):
            vw = cviews[c]; ta = tas[c]; d2 = c % 2
            psm1_take()
            cv.wait(T, at=l2done[c // NCH])
            for h in range(8):
                _tmm = T.matmul(ps_m1[:, h * 128:(h + 1) * 128], qhat[:, 8 + h, ta],
                                qhat[:, 8 + h, ta], start=True, stop=True)
            ct_.inc(_tmm)
            ct_.wait(A)
            cv.wait(A)
            A.activation(kk_sb[:, :, :], ps_m1[:, 0:1024], AF.Copy, bias=0.0, scale=1.0)
            kk_at = ca.n
            ca.wait(V, at=kk_at)
            ct_.wait(V, at=buf_read.get(f"A2_{d2}", 0))
            bwait(V, f"A2_{d2}")
            V.tensor_mul(vw["a"], eEa_t[:, :, :], kk_sb[:, :, :])
            cv.wait(P)
            P.affine_select(vw["a"], vw["a"], [[0, 8], [-1, 128]],
                            ALU.is_gt, 0.0, channel_multiplier=1)
            ca.wait(V, at=eeb_at[c])
            ct_.wait(V, at=buf_read.get(f"AT2_{d2}", 0))
            bwait(V, f"AT2_{d2}")
            V.tensor_mul(vw["at"], eEb[:, :, :], kk_sb[:, :, :])
            cv.wait(P)
            P.affine_select(vw["at"], vw["at"], [[0, 8], [1, 128]],
                            ALU.is_gt, 0.0, channel_multiplier=-1)
            amask_at[c] = cp.n
            buf_rdy[f"A2_{d2}"] = ("P", cp.n)
            buf_rdy[f"AT2_{d2}"] = ("P", cp.n)
            psm1_s["a"] = kk_at

        def u_p5(c):
            vw = cviews[c]; ta = tas[c]; d3 = c % 3
            psm1_take()
            cv.wait(T, at=l2done[c // NCH])
            for h in range(8):
                _tmm = T.matmul(ps_m1[:, h * 128:(h + 1) * 128], qhat[:, 8 + h, ta],
                                qhat[:, h, ta], start=True, stop=True)
            ct_.inc(_tmm)
            ct_.wait(A)
            cv.wait(A)
            A.activation(qk_sb[:, :, :], ps_m1[:, 0:1024], AF.Copy, bias=0.0, scale=1.0)
            qk_at = ca.n
            ca.wait(V, at=qk_at)
            ct_.wait(V, at=buf_read.get(f"ST_{d3}", 0))
            bwait(V, f"ST_{d3}")
            V.tensor_mul(vw["st"], eEc_t[:, :, :], qk_sb[:, :, :])
            cv.wait(P)
            P.affine_select(vw["st"], vw["st"], [[0, 8], [1, 128]],
                            ALU.is_ge, 0.0, channel_multiplier=-1)
            stmask_at[c] = cp.n
            psm1_s["a"] = qk_at

        def u_p6a(c):
            vw = cviews[c]; ta = tas[c]; d3 = c % 3
            ca.wait(T, at=pstr_s["a"])
            for h in range(8):
                _tmm = T.transpose(ps_tr[:, h * 128:(h + 1) * 128], qhat[:, 16 + h, ta],
                                   id128[:, :])
            ct_.inc(_tmm)
            ct_.wait(A)
            ct_.wait(A, at=buf_read.get(f"VT_{d3}", 0))
            cv.wait(A, at=vread.get(f"VT_{d3}", 0))
            A.activation(vw["vt"], ps_tr[:, :], AF.Copy, bias=0.0, scale=1.0)
            vtm_at[c] = ca.n
            pstr_s["a"] = ca.n

        def u_p6b(c):
            vw = cviews[c]; ta = tas[c]; d3 = c % 3
            ca.wait(T, at=pstr_s["a"])
            cv.wait(T, at=l2done[c // NCH])
            for h in range(8):
                _tmm = T.transpose(ps_tr[:, h * 128:(h + 1) * 128], qhat[:, 8 + h, ta],
                                   id128[:, :])
            ct_.inc(_tmm)
            ct_.wait(A)
            ct_.wait(A, at=buf_read.get(f"KM_{d3}", 0))
            A.activation(vw["km"], ps_tr[:, :], AF.Copy, bias=0.0, scale=1.0)
            ktm_at[c] = ca.n
            pstr_s["a"] = ca.n

        def u_p7(c):
            vw = cviews[c]; ta = tas[c]; d3 = c % 3
            ca.wait(V, at=eg_at[c])
            ct_.wait(V, at=max(buf_read.get(f"KT_{d3}", 0), buf_read.get(f"QT_{d3}", 0)))
            V.tensor_mul(vw["kt"], qhat[:, 8:16, ta], eG[:, :, :])
            V.tensor_mul(vw["qt"], qhat[:, 0:8, ta], eG[:, :, :])
            ktil_at[c] = cv.n

        def build_chain(c):
            vw = cviews[c]; d2 = c % 2
            units = []
            rp = [0]

            def c0():
                cp.wait(V, at=amask_at[c])
                ct_.wait(V, at=buf_read.get("VcA", 0))
                V.tensor_sub(Vcp[0][:, :, :], _bview(id128[:, :], 8), vw["a"])
                buf_rdy["VcA"] = ("V", cv.n)
            units.append(c0)
            cur, nxt = "VcA", "VcB"
            tl = {"VcA": Vcp[0], "VcB": Vcp[1], "Pw0": Pw[0], "Pw1": Pw[1],
                  "PTw0": PTw[0], "PTw1": PTw[1]}
            pk, ptk = f"A2_{d2}", f"AT2_{d2}"
            pt_, p_ = vw["at"], vw["a"]
            pend_v = []
            for lev in range(1, 7):
                npk, nptk = f"Pw{lev % 2}", f"PTw{lev % 2}"

                def s_unit(pk=pk, ptk=ptk, p_=p_, pt_=pt_, nptk=nptk, lev=lev):
                    b = rp[0] % 2
                    at = chain_mm(b, p_, pk, pt_, ptk)
                    evac_copy(b, tl[nptk], nptk, at)
                    rp[0] += 1
                units.append(s_unit)

                def v_unit(nptk=nptk, cur=cur, nxt=nxt):
                    b = rp[0] % 2
                    at = chain_mm(b, tl[nptk], nptk, tl[cur], cur)
                    evac_add(b, tl[nxt], nxt, tl[cur], at)
                    rp[0] += 1
                pend_v.append(v_unit)
                if lev < 6:
                    def p_unit(pk=pk, ptk=ptk, p_=p_, pt_=pt_, npk=npk):
                        b = rp[0] % 2
                        at = chain_mm(b, pt_, ptk, p_, pk)
                        evac_copy(b, tl[npk], npk, at, eng="V")
                        rp[0] += 1
                    units.append(p_unit)
                if len(pend_v) > 1:
                    units.append(pend_v.pop(0))
                cur, nxt = nxt, cur
                pk, ptk, p_, pt_ = npk, nptk, tl[npk], tl[nptk]
            while pend_v:
                units.append(pend_v.pop(0))

            def v6t(cur=cur):
                ca.wait(T, at=pstr_s["a"])
                bwait(T, cur)
                for h in range(8):
                    _tmm = T.transpose(ps_tr[:, h * 128:(h + 1) * 128], tl[cur][:, h, :],
                                       id128[:, :])
                ct_.inc(_tmm)
                buf_read[cur] = ct_.n
                ct_.wait(A)
                ct_.wait(A, at=buf_read.get(f"V6_{d2}", 0))
                A.activation(vw["v6"], ps_tr[:, :], AF.Copy, bias=0.0, scale=1.0)
                buf_rdy[f"V6_{d2}"] = ("A", ca.n)
                pstr_s["a"] = ca.n
            units.append(v6t)
            return units

        def u_y0(c):
            ct_.wait(V, at=buf_read.get("Mbf", 0))
            V.tensor_copy(Mbf[:, :, :], M_f32[:, :, :])
            mbf_at[c] = cv.n

        def u_y1(c):
            vw = cviews[c]; d3 = c % 3
            psm1_take()
            cv.wait(T, at=mbf_at[c])
            cv.wait(T, at=ktil_at[c])
            for h in range(8):
                _tmm = T.matmul(ps_m1[:, h * 128:(h + 1) * 128], vw["kt"][:, h, :],
                                Mbf[:, h, :], start=True, stop=True)
            ct_.inc(_tmm)
            buf_read[f"KT_{d3}"] = ct_.n
            buf_read["Mbf"] = ct_.n
            ct_.wait(V)
            ca.wait(V, at=vtm_at[c])
            ct_.wait(V, at=buf_read.get("R0", 0))
            V.tensor_sub(nWy[:, :, :], vw["vt"], _as3(ps_m1[:, 0:1024]))
            V.tensor_mul(R0t[:, :, :], nWy[:, :, :], _bview3(vw["b"]))
            vread[f"VT_{d3}"] = cv.n
            y_r0[c] = cv.n
            psm1_s["v"] = cv.n

        def u_y2(c):
            d2 = c % 2
            psm1_take()
            bwait(T, f"V6_{d2}")
            cv.wait(T, at=y_r0[c])
            for h in range(8):
                _tmm = T.matmul(ps_m1[:, h * 128:(h + 1) * 128], V6T2[:, d2][:, h, :],
                                R0t[:, h, :], start=True, stop=True)
            ct_.inc(_tmm)
            buf_read[f"V6_{d2}"] = ct_.n
            buf_read["R0"] = ct_.n
            ct_.wait(A)
            ct_.wait(A, at=buf_read.get("Ut", 0))
            cv.wait(A, at=vread.get("Ut", 0))
            A.activation(Ut[:, :, :], ps_m1[:, 0:1024], AF.Copy, bias=0.0, scale=1.0)
            u_at[c] = ca.n
            psm1_s["a"] = ca.n

        def u_y3(c):
            vw = cviews[c]; ta = tas[c]; d3 = c % 3
            psm1_take()
            ca.wait(T, at=u_at[c])
            cv.wait(T, at=ktil_at[c])
            cp.wait(T, at=stmask_at[c])
            for h in range(8):
                _tmm = T.matmul(ps_m1[:, h * 128:(h + 1) * 128], Mbf[:, h, :],
                                vw["qt"][:, h, :], start=True, stop=False)
                _tmm = T.matmul(ps_m1[:, h * 128:(h + 1) * 128], Ut[:, h, :],
                                vw["st"][:, h, :], start=False, stop=True)
            ct_.inc(_tmm)
            buf_read[f"QT_{d3}"] = ct_.n
            buf_read[f"ST_{d3}"] = ct_.n
            buf_read["Ut"] = ct_.n
            buf_read["Mbf"] = ct_.n
            ct_.wait(A)
            if c == 0 and sc > 0:
                cog.wait(A, at=og_stored)
            A.activation(og_sc[:, :, ta], ps_m1[:, 0:1024], AF.Copy, bias=0.0, scale=1.0)
            psm1_s["a"] = ca.n

        def u_y4(c):
            vw = cviews[c]; d3 = c % 3
            ca.wait(V, at=etm_at[c])
            ct_.wait(V, at=buf_read.get("R0", 0))
            ca.wait(V, at=u_at[c])
            V.tensor_mul(R0t[:, :, :], Ut[:, :, :], _bview3(vw["e"]))
            vread["Ut"] = cv.n
            ee2_at = cv.n
            psm1_take()
            ca.wait(T, at=ktm_at[c])
            cv.wait(T, at=ee2_at)
            for h in range(8):
                _tmm = T.matmul(ps_m1[:, h * 128:(h + 1) * 128], vw["km"][:, h, :],
                                R0t[:, h, :], start=True, stop=True)
            ct_.inc(_tmm)
            buf_read[f"KM_{d3}"] = ct_.n
            buf_read["R0"] = ct_.n
            ct_.wait(V)
            V.tensor_mul(nWy[:, :, :], M_f32[:, :, :], _bview3(vw["g"]))
            V.tensor_add(M_f32[:, :, :], nWy[:, :, :], _as3(ps_m1[:, 0:1024]))
            psm1_s["v"] = cv.n

        chq = []
        lfq = []

        def cpop():
            if lfq:
                lfq.pop(0)()
            elif chq:
                chq.pop(0)()

        cx.wait(T)
        cv.wait(T)
        ca.wait(T)
        for k in range(KCH):
            _tmm = T.matmul(psA[0:1, 0:512], ones128_bf[:, :], qhat[:, k, :],
                     start=(k == 0), stop=(k == KCH - 1))
        ct_.inc(_tmm)
        cx.wait(A)
        if sc > 0:
            ct_.wait(A)
        for k in range(KCH):
            A.activation(ln1[:, k, :], qhat[:, k, :], AF.Square, bias=0.0, scale=1.0)
            _sq = ca.n
            ca.wait(T, at=_sq)
            _tmm = T.matmul(psA[32:33, 0:512], ones128_bf[:, :], ln1[:, k, :],
                     start=(k == 0), stop=(k == KCH - 1))
        ct_.inc(_tmm)
        A.nop()
        stats_epilogue(psA[0:1, 0:512], psA[32:33, 0:512], mrep[:, :], rrep[:, :], 1e-5)
        ca.wait(V)

        w_loaded = {}

        def w_dma(ci, buf):
            if sc == 0:
                srcw = (wqkv[ci] if ci < N_CT else wz[ci - N_CT]).rearrange("p k c -> p (k c)")
                qe = P
            else:
                srcw = wbf_dram.ap()[ci]
                qe = S
            if ("w", buf) in w_used:
                ct_.wait(qe, at=w_used[("w", buf)])
            if buf in wb_at:
                cwb.wait(qe, at=wb_at[buf])
            cw_s[buf].inc(qe.dma_start(out=wslab[:, buf, :], in_=srcw), 16)
            w_loaded[buf] = cw_s[buf].n
            if sc == 0:
                cw_s[buf].wait(S, at=w_loaded[buf])
                cwb.inc(S.dma_start(out=wbf_dram.ap()[ci], in_=wslab[:, buf, :]), 16)
                wb_at[buf] = cwb.n

        for _pf in range(2):
            w_dma(_pf, _pf)

        cv.wait(T)
        ca.wait(T)
        cx.wait(T)
        for k in range(KCH):
            V.tensor_sub(ln1[:, k, :], qhat[:, k, :], mrep[:, :])
            V.tensor_mul(ln1[:, k, :], ln1[:, k, :], rrep[:, :])
            _lnk = cv.n
            cv.wait(T, at=_lnk)
            _tmm = T.matmul(psA[0:8, 0:512], wba_sb[:, 0, k, :], ln1[:, k, :],
                     start=(k == 0), stop=(k == KCH - 1))
            _tmm = T.matmul(psA[32:40, 0:512], wba_sb[:, 1, k, :], ln1[:, k, :],
                     start=(k == 0), stop=(k == KCH - 1))
        ct_.inc(_tmm)
        lnv = cv.n
        ct_.wait(V)
        ct_.wait(A)
        A.activation(bfm[:, :], psA[0:8, 0:512], AF.Sigmoid, bias=0.0, scale=1.0)
        V.tensor_scalar_mul(sp8a[:, :], psA[0:8, 0:512], -1.0)
        cv.wait(A)
        A.activation(sp8b[:, :], sp8a[:, :], AF.Abs, bias=0.0, scale=1.0)
        ca.wait(V)
        V.tensor_scalar_max(sp8a[:, :], sp8a[:, :], 0.0)
        cv.wait(A)
        A.activation(sp8b[:, :], sp8b[:, :], AF.Exp, bias=0.0, scale=-1.0)
        A.activation(sp8b[:, :], sp8b[:, :], AF.Ln, bias=1.0, scale=1.0)
        ca.wait(V)
        V.tensor_add(sp8a[:, :], sp8a[:, :], sp8b[:, :])
        V.tensor_scalar_mul(lnbfm[:, :], sp8a[:, :], -1.0)
        V.tensor_scalar_add(sp8a[:, :], psA[32:40, 0:512], dtb_sb[:, :])
        ba_free = (ca.n, cv.n)
        cv.wait(A)
        A.activation(sp8b[:, :], sp8a[:, :], AF.Abs, bias=0.0, scale=1.0)
        ca.wait(V)
        V.tensor_scalar_max(sp8a[:, :], sp8a[:, :], 0.0)
        cv.wait(A)
        A.activation(sp8b[:, :], sp8b[:, :], AF.Exp, bias=0.0, scale=-1.0)
        A.activation(sp8b[:, :], sp8b[:, :], AF.Ln, bias=1.0, scale=1.0)
        ca.wait(V)
        V.tensor_add(sp8a[:, :], sp8a[:, :], sp8b[:, :])
        V.tensor_scalar_mul(gfm[:, :], sp8a[:, :], negA[:, :])


        projb = [ps_m1[:, 0:512], ps_m1[:, 512:1024], ps_b1[:, 0:512]]
        pevac = {}
        praw_evac = {}
        ca.wait(T)
        cv.wait(T)
        cp.wait(V)
        V.tensor_copy(qraw[:, :, 0:3], tails[:, :, 0:3])
        tails_in_v = cv.n
        cv.wait(P, at=tails_in_v)
        n_tiles = N_CT + Z_CT
        for cti in range(n_tiles):
            buf = cti % 3
            bank = cti % 3
            if cti + 2 < n_tiles:
                w_dma(cti + 2, (cti + 2) % 3)
            cw_s[buf].wait(T, at=w_loaded[buf])
            cv.wait(T, at=lnv)
            if bank in pevac:
                ca.wait(T, at=pevac[bank])
            ph = projb[bank]
            for k in range(KCH):
                _tmm = T.matmul(ph, wslab[:, buf, k * 128:(k + 1) * 128], ln1[:, k, :],
                                start=(k == 0), stop=(k == KCH - 1))
            ct_.inc(_tmm)
            w_used[("w", buf)] = ct_.n
            ct_.wait(A)
            if cti < N_CT:
                A.activation(qraw[:, cti, 3:515], ph, AF.Copy, bias=0.0, scale=1.0)
                praw_evac[cti] = ca.n
                eng = V
                ca.wait(eng, at=praw_evac[cti])
                w4 = [convw_sb[:, cti, j:j + 1] for j in range(4)]
                eng.tensor_scalar_mul(qhat[:, cti, :], qraw[:, cti, 0:512], w4[0])
                for j in range(1, 4):
                    eng.scalar_tensor_tensor(qhat[:, cti, :], qraw[:, cti, j:j + 512], w4[j],
                                             qhat[:, cti, :], ALU.mult, ALU.add)
            else:
                A.activation(z_sc[:, cti - N_CT, :], ph, AF.Copy, bias=0.0, scale=1.0)
            pevac[bank] = ca.n
        cv.wait(A)
        A.activation(qhat[:, 0:12, :], qhat[:, 0:12, :], AF.Silu, bias=0.0, scale=1.0)
        A.activation(qhat[:, 12:24, :], qhat[:, 12:24, :], AF.Silu, bias=0.0, scale=1.0)
        cv.wait(V, at=tails_in_v)
        V.tensor_copy(tails[:, :, 0:3], qraw[:, :, 512:515])

        ss_slots = [psA[0:1, 0:512], psA[32:33, 0:512], psA[64:65, 0:512],
                    ps_b2[0:1, 0:512], ps_b2[32:33, 0:512], ps_b2[64:65, 0:512],
                    ps_b2[0:1, 512:1024], ps_b2[32:33, 512:1024], ps_b2[64:65, 512:1024]]
        sq_at = {}
        ssum_at = {}
        sqrt_at = {}
        rec_at = {}
        rep_at = {}
        mul_at = {}
        ct_.wait(A)
        ca.wait(T)
        cv.wait(T)
        A.activation(ln1[:, 0:8, :], qhat[:, 0:8, :], AF.Square, bias=0.0, scale=1.0)
        sq_at[0] = ca.n
        A.activation(ln1[:, 8:16, :], qhat[:, 8:16, :], AF.Square, bias=0.0, scale=1.0)
        sq_at[1] = ca.n
        stat_bx = [stat_b[0:1, 0:512], stat_b[0:1, 512:1024],
                   qraw[0:1, 2, 3:515], qraw[0:1, 3, 3:515]]
        l2slots = [ps_b2[0:1, 0:512], ps_b2[32:33, 0:512], ps_b2[64:65, 0:512],
                   ps_b2[0:1, 512:1024], ps_b2[32:33, 512:1024], ps_b2[64:65, 512:1024]]
        q_repl = [ps_b1[:, 0:512], ps_b1[:, 512:1024]]
        psm1_s["a"] = ca.n; psm1_s["v"] = cv.n
        pstr_s["a"] = ca.n
        psa_s["a"] = ca.n; psa_s["v"] = cv.n
        c0_ = sc * NCH
        for i in range(QK_CT + 8):
            if i < QK_CT:
                cti = i
                sl = cti % 6
                ca.wait(T, at=sq_at[cti // 8])
                if cti >= 6:
                    ca.wait(T, at=sqrt_at[cti - 6])
                if cti < 3:
                    ca.wait(T, at=ba_free[0])
                    cv.wait(T, at=ba_free[1])
                ct_.inc(T.matmul(l2slots[sl], ones128_bf[:, :], ln1[:, cti, :],
                                 start=True, stop=True))
                ssum_at[cti] = ct_.n
            if i >= 2 and i - 2 < QK_CT:
                cti = i - 2
                sl = cti % 6
                f2 = cti % 2
                ct_.wait(A, at=ssum_at[cti])
                if cti >= 2:
                    cv.wait(A, at=rec_at[cti - 2])
                if cti < 8:
                    A.activation(stat_f[0:1, f2 * 512:(f2 + 1) * 512], l2slots[sl],
                                 AF.Sqrt, bias=128.0 * 1e-6, scale=128.0)
                else:
                    A.activation(stat_f[0:1, f2 * 512:(f2 + 1) * 512], l2slots[sl],
                                 AF.Sqrt, bias=1e-6, scale=1.0)
                sqrt_at[cti] = ca.n
            if i >= 3 and i - 3 < QK_CT:
                cti = i - 3
                f2 = cti % 2
                ca.wait(V, at=sqrt_at[cti])
                if cti >= 4:
                    ct_.wait(V, at=rep_at[cti - 4])
                V.reciprocal(stat_bx[cti % 4], stat_f[0:1, f2 * 512:(f2 + 1) * 512])
                rec_at[cti] = cv.n
            if i >= 8 and i - 8 < QK_CT:
                cti = i - 8
                ct_.wait(V, at=rep_at[cti])
                V.tensor_mul(qhat[:, cti, :], qhat[:, cti, :], q_repl[cti % 2])
                mul_at[cti] = cv.n
            if i >= 6 and i - 6 < QK_CT:
                cti = i - 6
                cv.wait(T, at=rec_at[cti])
                if cti >= 2:
                    cv.wait(T, at=mul_at[cti - 2])
                ct_.inc(T.matmul(q_repl[cti % 2], ones1_bf[:, :],
                                 stat_bx[cti % 4], start=True, stop=True))
                rep_at[cti] = ct_.n
            if i == 2:
                u_prep_open(c0_)
                u_p1(c0_)
            elif i == 5:
                u_p2(c0_)
            elif i == 14:
                u_p3a(c0_)
            elif i == 18:
                u_p3b(c0_)
            elif i == 20:
                u_p6a(c0_)

        l2done[sc] = cv.n

        if "proj" in dbg and sc == 0:
            cv.wait(S)
            ca.wait(S)
            cdbg.inc(S.dma_start(out=dbg_out["qhat"], in_=qhat[:, :, :]), 16)
            cdbg.inc(S.dma_start(out=dbg_out["zsc"], in_=z_sc[:, :, :]), 16)
            cdbg.inc(S.dma_start(out=dbg_out["bg"][:, 0, :], in_=bfm[:, :]), 16)
            cdbg.inc(S.dma_start(out=dbg_out["bg"][:, 1, :], in_=lnbfm[:, :]), 16)
            cdbg.inc(S.dma_start(out=dbg_out["bg"][:, 2, :], in_=gfm[:, :]), 16)

        if stop_after == "A0" and sc == 0:
            cdbg.wait(S)
            return nc, dbg_out

        _scA.__exit__(None, None, None)
        _scB = scope(f"B{sc}"); _scB.__enter__()
        bank_free[0] = (ca.n, cv.n)
        bank_free[1] = (ca.n, cv.n)
        for br in range(NCH + 2):
            pre0 = (br == 0)
            if br == 1 and cc_issued < sc:
                cog.wait(P, at=og_stored)
                ccc.inc(P.collective_compute(
                    "AllGather", ALU.bypass, replica_groups=PAIRS,
                    ins=[cc_in.ap()[sc - 1].opt()], outs=[cc_out.ap()[sc - 1].opt()]))
                cc_issued = sc
            base = sc * NCH
            has_p = br < NCH
            has_c = 1 <= br <= NCH
            has_y = 2 <= br
            cP, cC, cY = base + br, base + br - 1, base + br - 2
            lfq.extend(chq)
            chq = build_chain(cC) if has_c else []
            if has_y:
                u_y0(cY)
            cpop()
            if has_y:
                u_y1(cY)
            cpop()
            if has_p and not pre0:
                u_prep_open(cP)
                u_p1(cP)
            cpop()
            if has_p and not pre0:
                u_p2(cP)
            cpop()
            cpop()
            if has_p and not pre0:
                u_p3a(cP)
            while lfq:
                lfq.pop(0)()
            cpop()
            if has_y:
                u_y2(cY)
            cpop()
            if has_p and not pre0:
                u_p3b(cP)
            cpop()
            if has_y:
                u_y3(cY)
            cpop()
            if has_y:
                u_y4(cY)
            if has_p:
                u_p4(cP)
            cpop()
            cpop()
            if has_p:
                u_p5(cP)
            cpop()
            cpop()
            if has_p and not pre0:
                u_p6a(cP)
            cpop()
            if has_p:
                u_p6b(cP)
            cpop()
            cpop()
            if has_p:
                u_p7(cP)
            cpop()
            cpop()
            if br >= NCH:
                while lfq or chq:
                    cpop()
            if br == NCH - 1 and sc + 1 < NSC:
                ct_.wait(P)
                cv.wait(P)
                ca.wait(P)
                cx.inc(P.dma_start(
                    out=qhat[:, 0:16, :],
                    in_=xT[:, (sc + 1) * SC:(sc + 2) * SC]
                    .rearrange("(k p) t -> p k t", p=128)), 16)

        osq = ln1[:, 0:8, :]
        zsil = ln1[:, 8:16, :]
        ca.wait(T)
        cv.wait(T)
        ct_.wait(A)
        A.activation(osq[:, :, :], og_sc[:, :, :], AF.Square, bias=0.0, scale=1.0)
        osq_at = ca.n
        A.activation(zsil[:, :, :], z_sc[:, :, :], AF.Silu, bias=0.0, scale=1.0)
        zsil_at = ca.n
        ssum_at = {}
        sqrt_at = {}
        rec_at = {}
        rep_at = {}
        mul_at = {}
        o_repl = [ps_b1[:, 0:512], ps_b1[:, 512:1024], ps_m1[:, 0:512], ps_m1[:, 512:1024]]
        ca.wait(T, at=osq_at)
        for i in range(16):
            if i < 8:
                h = i
                ct_.inc(T.matmul(ss_slots[h], ones128_bf[:, :], osq[:, h, :],
                                 start=True, stop=True))
                ssum_at[h] = ct_.n
                if h == 7:
                    ct_.wait(V)
                    ca.wait(V, at=zsil_at)
                    V.scalar_tensor_tensor(osq[:, :, :], og_sc[:, :, :], normw_sb[:, :],
                                           zsil[:, :, :], ALU.mult, ALU.mult)
            if i >= 2 and i - 2 < 8:
                h = i - 2
                f2 = h % 2
                ct_.wait(A, at=ssum_at[h])
                if h >= 2:
                    cv.wait(A, at=rec_at[h - 2])
                A.activation(stat_f[0:1, f2 * 512:(f2 + 1) * 512], ss_slots[h],
                             AF.Sqrt, bias=1e-6, scale=1.0 / 128.0)
                sqrt_at[h] = ca.n
            if i >= 3 and i - 3 < 8:
                h = i - 3
                f2 = h % 2
                ca.wait(V, at=sqrt_at[h])
                if h >= 2:
                    ct_.wait(V, at=rep_at[h - 2])
                V.reciprocal(stat_b[0:1, f2 * 512:(f2 + 1) * 512],
                             stat_f[0:1, f2 * 512:(f2 + 1) * 512])
                rec_at[h] = cv.n
            if i >= 8 and i - 8 < 8:
                h = i - 8
                ct_.wait(V, at=rep_at[h])
                V.tensor_mul(og_sc[:, h, :], osq[:, h, :], o_repl[h % 4])
                mul_at[h] = cv.n
            if i >= 4 and i - 4 < 8:
                h = i - 4
                f2 = h % 2
                cv.wait(T, at=rec_at[h])
                if h >= 4:
                    cv.wait(T, at=mul_at[h - 4])
                ct_.inc(T.matmul(o_repl[h % 4], ones1_bf[:, :],
                                 stat_b[0:1, f2 * 512:(f2 + 1) * 512], start=True, stop=True))
                rep_at[h] = ct_.n

        cv.wait(S)
        cog.inc(S.dma_start(
            out=cc_in.ap()[sc].rearrange("(h p) t -> p h t", p=128),
            in_=og_sc[:, :, :]), 16)
        og_stored = cog.n
        _scB.__exit__(None, None, None)

    cog.wait(P, at=og_stored)
    ccc.inc(P.collective_compute(
        "AllGather", ALU.bypass, replica_groups=PAIRS,
        ins=[cc_in.ap()[NSC - 1].opt()], outs=[cc_out.ap()[NSC - 1].opt()]))
    for ctr in (ct_, cv, ca, cp, cx, *cw_s, cog, csd):
        for eng in (T, V, A, P, S):
            ctr.wait(eng)
    sbctx_ab.close()
    sbctx_w = contextlib.ExitStack()
    wgsl = sbctx_w.enter_context(nc.sbuf_tensor("wgsl", [128, 2, KCH * 128], BF16))
    wusl = sbctx_w.enter_context(nc.sbuf_tensor("wusl", [128, 2, KCH * 128], BF16))
    wg8 = sbctx_w.enter_context(nc.sbuf_tensor("wg8", [128, 2, KCH * 128], U8))
    wu8 = sbctx_w.enter_context(nc.sbuf_tensor("wu8", [128, 2, KCH * 128], U8))
    gu_loaded = {}
    cast_a = {}
    cast_v = {}

    def gu_dma(it, buf):
        sl8 = it % 2
        qe = S if sl8 == 0 else P
        if sl8 in cast_a:
            ca.wait(qe, at=cast_a[sl8][1])
            cv.wait(qe, at=cast_v[sl8][1])
        cgu_s[sl8].inc(qe.dma_start(out=wg8[:, sl8, :], in_=wg_in[it].rearrange("p k c -> p (k c)")), 16)
        cgu_s[sl8].inc(qe.dma_start(out=wu8[:, sl8, :], in_=wu_in[it].rearrange("p k c -> p (k c)")), 16)
        gu_loaded[sl8] = cgu_s[sl8].n

    def gu_cast(it, buf):
        sl8 = it % 2
        if ("g", buf) in w_used:
            ct_.wait(A, at=w_used[("g", buf)])
            ct_.wait(V, at=w_used[("g", buf)])
        cgu_s[sl8].wait(A, at=gu_loaded[sl8])
        cgu_s[sl8].wait(V, at=gu_loaded[sl8])
        A.activation(wgsl[:, buf, :], wg8[:, sl8, :], AF.Copy, bias=0.0, scale=1.0)
        cast_a[sl8] = (buf, ca.n)
        V.tensor_copy(wusl[:, buf, :], wu8[:, sl8, :])
        cast_v[sl8] = (buf, cv.n)

    og_my = sb2("og_my", [128, NH, TOKH], BF16)
    og_peer = sb2("og_peer", [128, NH, TOKH], BF16)
    x_half = sb2("x_half", [128, KCH, TOKH], F32)
    araw = sb2("araw", [128, KCH, TOKH], BF16)
    wosl = sb2("wosl", [128, 2, 16 * 128], BF16)
    dsq2 = og_my[:, 2:4, :]
    dcp2 = og_my[:, 0:2, :]
    mrepD = og_my[:, 4, :]
    rrepD = og_my[:, 5, :]
    tD = ps_b1[:, 0:1024]
    hgv = pid % 2
    ccin_r = cc_in.ap().rearrange("s (h p) t -> p (s h) t", p=128)
    ccout_r = cc_out.ap().rearrange("s (b h p) t -> p (s b h) t", p=128, b=2)
    for i in range(2):
        cph.inc(P.dma_start(out=og_my[:, :, i * SC:(i + 1) * SC],
                            in_=ccin_r[:, ds((2 * hgv + i) * NH, NH), :]), 16)

    _scD = scope("D"); _scD.__enter__()
    wo_loaded = {}

    def wout_dma(dt, buf):
        if ("wo", buf) in w_used:
            ct_.wait(P, at=w_used[("wo", buf)])
        cwo_s[buf].inc(P.dma_start(out=wosl[:, buf, :], in_=wout[dt].rearrange("p v d -> p (v d)")), 16)
        wo_loaded[buf] = cwo_s[buf].n

    wout_dma(0, 0)
    wout_dma(1, 1)
    ccc.wait(P)
    for i in range(2):
        cph.inc(P.dma_start(
            out=og_peer[:, :, i * SC:(i + 1) * SC],
            in_=ccout_r[:, ds((2 * hgv + i) * 2 * NH + (1 - hgv) * NH, NH), :]), 16)
    og_done = cph.n
    cph.inc(P.dma_start(out=x_half[:, :, :],
                        in_=xT[:, ds(half_off, TOKH)].rearrange("(k p) t -> p k t", p=128)), 16)
    rotD = [ps_m1, ps_b1]
    sq_at = {}
    stat_at = {}

    def _sqv(dt):
        return araw[:, dt + 1, :] if dt + 1 < DT_FF else og_my[:, 2, :]

    def _araw_stats(dt):
        ca.wait(T, at=sq_at[dt])
        sv = _sqv(dt)
        _tmm = T.matmul(psA[0:1, 0:512], ones128_bf[:, :], araw[:, dt, 0:512],
                        start=(dt == 0), stop=(dt == DT_FF - 1))
        _tmm = T.matmul(psA[64:65, 0:512], ones128_bf[:, :], araw[:, dt, 512:1024],
                        start=(dt == 0), stop=(dt == DT_FF - 1))
        _tmm = T.matmul(psA[32:33, 0:512], ones128_bf[:, :], sv[:, 0:512],
                        start=(dt == 0), stop=(dt == DT_FF - 1))
        _tmm = T.matmul(ps_b2[0:1, 0:512], ones128_bf[:, :], sv[:, 512:1024],
                        start=(dt == 0), stop=(dt == DT_FF - 1))
        ct_.inc(_tmm)
        stat_at[dt % 2] = ct_.n

    for dt in range(DT_FF):
        buf = dt % 2
        if 1 <= dt and dt + 1 < DT_FF:
            wout_dma(dt + 1, (dt + 1) % 2)
        cph.wait(T, at=og_done)
        cwo_s[buf].wait(T, at=wo_loaded[buf])
        if dt >= 2:
            ca.wait(T)
        pb = rotD[dt % 2]
        for vc in range(16):
            og_src = og_my[:, vc, :] if vc < 8 else og_peer[:, vc - 8, :]
            lhs = wosl[:, buf, vc * 128:(vc + 1) * 128]
            _tmm = T.matmul(pb[:, 0:512], lhs, og_src[:, 0:512], start=(vc == 0), stop=(vc == 15))
            _tmm = T.matmul(pb[:, 512:1024], lhs, og_src[:, 512:1024], start=(vc == 0), stop=(vc == 15))
        ct_.inc(_tmm)
        w_used[("wo", buf)] = ct_.n
        if dt >= 1:
            _araw_stats(dt - 1)
        ct_.wait(A)
        A.activation(araw[:, dt, :], pb[:, 0:1024], AF.Copy, bias=0.0, scale=1.0)
        A.activation(_sqv(dt), araw[:, dt, :], AF.Square, bias=0.0, scale=1.0)
        sq_at[dt] = ca.n
    _araw_stats(DT_FF - 1)

    gu_dma(0, 0)
    gu_dma(1, 1)
    stats_epilogue(psA[0:1, 0:512], psA[32:33, 0:512],
                   mrepD[:, 0:512], rrepD[:, 0:512], 1e-5, repl_ps=ps_b1)
    stats_epilogue(psA[64:65, 0:512], ps_b2[0:1, 0:512],
                   mrepD[:, 512:1024], rrepD[:, 512:1024], 1e-5, repl_ps=ps_b1)

    ca.wait(V)
    cph.wait(V)
    ca.wait(T)
    cv.wait(T)
    for k in range(KCH):
        db = k % 2
        V.tensor_sub(tD[:, :], araw[:, k, :], mrepD[:, :])
        V.tensor_mul(tD[:, :], tD[:, :], rrepD[:, :])
        V.tensor_add(x_half[:, k, :], x_half[:, k, :], tD[:, :])
        V.engine_nop()
        mid_at = cv.n
        cv.wait(A, at=mid_at)
        if k >= 2:
            ct_.wait(A, at=stat_at[db])
        A.activation(dcp2[:, db, :], x_half[:, k, :], AF.Copy, bias=0.0, scale=1.0)
        A.activation(dsq2[:, db, :], dcp2[:, db, :], AF.Square, bias=0.0, scale=1.0)
        sq_at[db] = ca.n
        ca.wait(T, at=sq_at[db])
        _tmm = T.matmul(psA[0:1, 0:512], ones128_bf[:, :], dcp2[:, db, 0:512],
                        start=(k == 0), stop=(k == KCH - 1))
        _tmm = T.matmul(psA[64:65, 0:512], ones128_bf[:, :], dcp2[:, db, 512:1024],
                        start=(k == 0), stop=(k == KCH - 1))
        _tmm = T.matmul(psA[32:33, 0:512], ones128_bf[:, :], dsq2[:, db, 0:512],
                        start=(k == 0), stop=(k == KCH - 1))
        _tmm = T.matmul(ps_b2[0:1, 0:512], ones128_bf[:, :], dsq2[:, db, 512:1024],
                        start=(k == 0), stop=(k == KCH - 1))
        ct_.inc(_tmm)
        stat_at[db] = ct_.n
    if "mid" in dbg:
        ca.wait(S)
        cv.wait(S)
        cdbg.inc(S.dma_start(out=dbg_out["araw"], in_=araw[:, :, :]), 16)
        cdbg.inc(S.dma_start(out=dbg_out["xmid"], in_=x_half[:, :, :]), 16)

    stats_epilogue(psA[0:1, 0:512], psA[32:33, 0:512],
                   mrepD[:, 0:512], rrepD[:, 0:512], 1e-5, repl_ps=ps_b1)
    stats_epilogue(psA[64:65, 0:512], ps_b2[0:1, 0:512],
                   mrepD[:, 512:1024], rrepD[:, 512:1024], 1e-5, repl_ps=ps_b1)

    gu_cast(0, 0)
    gu_cast(1, 1)
    tD2 = ps_b2[:, 0:1024]
    ca.wait(V)
    ca.wait(T)
    cv.wait(T)
    ca.wait(T, at=cast_a[0][1])
    cv.wait(T, at=cast_v[0][1])
    ca.wait(T, at=cast_a[1][1])
    cv.wait(T, at=cast_v[1][1])
    pbE2 = [ps_m1, ps_b1]
    for k in range(KCH):
        V.tensor_sub(tD2[:, :], x_half[:, k, :], mrepD[:, :])
        V.tensor_mul(ln2[:, k, :], tD2[:, :], rrepD[:, :])
        ln2_at = cv.n
        cv.wait(T, at=ln2_at)
        _tmm = T.matmul(psA[0:1, 0:512], ones128_bf[:, :], ln2[:, k, 0:512],
                        start=(k == 0), stop=(k == KCH - 1))
        _tmm = T.matmul(psA[32:33, 0:512], ones128_bf[:, :], ln2[:, k, 512:1024],
                        start=(k == 0), stop=(k == KCH - 1))
        for itt in range(2):
            pbE = pbE2[itt]
            _tmm = T.matmul(pbE[:, 0:512], wgsl[:, itt, k * 128:(k + 1) * 128],
                            ln2[:, k, 0:512], start=(k == 0), stop=(k == KCH - 1))
            _tmm = T.matmul(pbE[:, 512:1024], wusl[:, itt, k * 128:(k + 1) * 128],
                            ln2[:, k, 0:512], start=(k == 0), stop=(k == KCH - 1))
        ct_.inc(_tmm)
    V.engine_nop()
    if "ln2" in dbg:
        cv.wait(S)
        cdbg.inc(S.dma_start(out=dbg_out["ln2"], in_=ln2[:, :, :]), 16)
    cv.wait(S)
    cspill.inc(S.dma_start(out=xmid_dram.ap().rearrange("(k p) t -> p k t", p=128),
                         in_=x_half[:, :, :]), 16)

    if stop_after == "D":
        cdbg.wait(S)
        cspill.wait(S)
        return nc, dbg_out

    _scD.__exit__(None, None, None)
    _scE = scope("E"); _scE.__enter__()
    xmid_used = {}
    out_stored = {}
    for ctr in (ct_, cv, ca, cp, cph, cwo_s[0], cwo_s[1], cspill, cdbg):
        for eng in (T, V, A, P, S):
            ctr.wait(eng)
    sbctx_2.close()
    h2 = sb3("h2", [128, IT_FF, FGRP], BF16)
    wdsl = sb3("wdsl", [128, 4, 16 * 128], BF16)
    gact = sb3("gact", [128, 2, FGRP], BF16)
    udeq = sb3("udeq", [128, FGRP], BF16)
    draw = sb3("draw", [128, DT_FF, FGRP], BF16)
    rs2grp = sb3("rs2grp", [1, 2, FGRP], BF16)
    rsrep = sb3("rsrep", [128, FGRP], BF16)
    rs2rep = sb3("rs2rep", [128, FGRP], BF16)
    tEu = sb3("tEu", [128, FGRP], F32)
    rs2_bf = sb3("rs2_bf", [1, FGRP], BF16)
    xmid_sb = sb3("xmid_sb", [128, 2, FGRP], F32)
    outb = sb3("outb", [128, 2, FGRP], F32)
    tE = sb3("tE", [128, FGRP], F32)
    dsq3 = sb3("dsq3", [128, 2, 512], BF16)
    mrepE = sb3("mrepE", [128, 512], BF16)
    rrepE = sb3("rrepE", [128, 512], BF16)
    ct_.wait(V)
    V.tensor_copy(rs2grp[0:1, 0, :], psA[0:1, 0:512])
    V.tensor_copy(rs2grp[0:1, 1, :], psA[32:33, 0:512])
    for grp in range(2):
        g0 = grp * FGRP
        rs_bf = rs2grp[0:1, grp, :]
        cv.wait(T)
        ca.wait(T)
        ct_.inc(T.matmul(psA[:, 0:512], ones1_bf[:, :], rs_bf, start=True, stop=True))
        ct_.wait(A)
        A.activation(rsrep[:, :], psA[:, 0:512], AF.Copy, bias=0.0, scale=128.0)
        rsrep_at = ca.n

        if grp == 1:
            gu_cast(0, 0)
        rotE = [ps_m1, ps_b1, ps_b2]
        evac_done = {}
        gact_used = {}
        wd_loaded = {}

        def wd_dma(dt, icg, buf):
            if ("d", buf) in w_used:
                ct_.wait(P, at=w_used[("d", buf)])
            cwd_s[buf].inc(P.dma_start(out=wdsl[:, buf, :],
                                in_=wd_in[dt, :, icg * 16:(icg + 1) * 16, :]
                                .rearrange("p c d -> p (c d)")), 16)
            wd_loaded[buf] = cwd_s[buf].n

        cv.wait(T)
        for it in range(IT_FF):
            buf = it % 2
            if it == IT_FF - 4:
                for _pf in range(3):
                    wd_dma(0, _pf, _pf)
            if it + 2 < IT_FF:
                gu_dma(it + 2, 0)
            if it + 1 < IT_FF and not (grp == 0 and it + 1 < 2):
                gu_cast(it + 1, (it + 1) % 2)
            bank = it % 3
            if bank in evac_done:
                ca.wait(T, at=evac_done[bank][0])
                cv.wait(T, at=evac_done[bank][1])
            pb = rotE[bank]
            if grp == 0 and it < 2:
                w_used[("g", buf)] = ct_.n
            else:
                ca.wait(T, at=cast_a[it % 2][1])
                cv.wait(T, at=cast_v[it % 2][1])
                for k in range(KCH):
                    _tmm = T.matmul(pb[:, 0:512], wgsl[:, buf, k * 128:(k + 1) * 128],
                             ln2[:, k, g0:g0 + 512], start=(k == 0), stop=(k == KCH - 1))
                ct_.inc(_tmm)
                for k in range(KCH):
                    _tmm = T.matmul(pb[:, 512:1024], wusl[:, buf, k * 128:(k + 1) * 128],
                             ln2[:, k, g0:g0 + 512], start=(k == 0), stop=(k == KCH - 1))
                ct_.inc(_tmm)
                w_used[("g", buf)] = ct_.n
            ct_.wait(V)
            ca.wait(V, at=silu_at if it >= 1 else rsrep_at)
            V.tensor_sub(tE[:, :], pb[:, 0:512], rsrep[:, :])
            V.tensor_sub(tEu[:, :], pb[:, 512:1024], rsrep[:, :])
            sub_at = cv.n
            gb = it % 2
            if gb in gact_used:
                cv.wait(A, at=gact_used[gb])
            cv.wait(A, at=sub_at)
            A.activation(gact[:, gb, :], tE[:, :], AF.Silu, bias=0.0,
                                scale=sg_sb[:, it:it + 1])
            silu_at = ca.n
            ca.wait(V)
            V.scalar_tensor_tensor(h2[:, it, :], tEu[:, :], su_sb[:, it:it + 1],
                                   gact[:, gb, :], ALU.mult, ALU.mult)
            gact_used[gb] = cv.n
            evac_done[bank] = (ca.n, sub_at)
        cv.wait(T)
        for ic in range(IT_FF):
            _tmm = T.matmul(psA[0:1, 0:512], ones128_bf[:, :], h2[:, ic, :],
                     start=(ic == 0), stop=(ic == IT_FF - 1))
        ct_.inc(_tmm)
        ct_.wait(V)
        V.tensor_copy(rs2_bf[0:1, :], psA[0:1, 0:512])
        cv.wait(T)
        ct_.inc(T.matmul(psA[:, 0:512], ones1_bf[:, :], rs2_bf[0:1, :], start=True, stop=True))
        ct_.wait(A)
        A.activation(rs2rep[:, :], psA[:, 0:512], AF.Copy, bias=0.0, scale=128.0)
        ca.wait(V)

        if grp == 0:
            gu_dma(0, 0)
            gu_dma(1, 1)
        nslab = 0
        for dt in range(DT_FF):
            bank = dt % 3
            pb = rotE[bank]
            if bank in evac_done:
                ca.wait(T, at=evac_done[bank][0])
                cv.wait(T, at=evac_done[bank][1])
            for icg in range(4):
                buf = nslab % 4
                if nslab + 3 < DT_FF * 4:
                    nxt = nslab + 3
                    wd_dma(nxt // 4, nxt % 4, nxt % 4)
                cwd_s[buf].wait(T, at=wd_loaded[buf])
                for ici in range(16):
                    ic = icg * 16 + ici
                    _tmm = T.matmul(pb[:, 0:512], wdsl[:, buf, ici * 128:(ici + 1) * 128], h2[:, ic, :],
                             start=(ic == 0), stop=(ic == IT_FF - 1))
                ct_.inc(_tmm)
                w_used[("d", buf)] = ct_.n
                nslab += 1
            ct_.wait(V)
            V.tensor_sub(tE[:, :], pb[:, 0:512], rs2rep[:, :])
            V.tensor_scalar(draw[:, dt, :], tE[:, :], sd_sb[:, dt:dt + 1],
                                   None, ALU.mult)
            evac_done[bank] = (ca.n, cv.n)
        ca.wait(T)
        cv.wait(T)
        cv.wait(A)
        for k in range(DT_FF):
            db = k % 2
            if k >= 2:
                ct_.wait(A, at=sq_used[db])
            A.activation(dsq3[:, db, :], draw[:, k, :], AF.Square, bias=0.0, scale=1.0)
            ca.wait(T)
            _tmm = T.matmul(psA[0:1, 0:512], ones128_bf[:, :], draw[:, k, :],
                     start=(k == 0), stop=(k == DT_FF - 1))
            _tmm = T.matmul(psA[32:33, 0:512], ones128_bf[:, :], dsq3[:, db, :],
                     start=(k == 0), stop=(k == DT_FF - 1))
            ct_.inc(_tmm)
            if k == 0:
                sq_used = {}
            sq_used[db] = ct_.n
        stats_epilogue(psA[0:1, 0:512], psA[32:33, 0:512],
                       mrepE[:, :], rrepE[:, :], 1e-5)
        cspill.wait(P)
        xmid_view = xmid_dram.ap().rearrange("(k p) t -> p k t", p=128)
        cxm_s[0].inc(P.dma_start(out=xmid_sb[:, 0, :], in_=xmid_view[:, 0, g0:g0 + 512]), 16)
        xm_loaded = {0: cxm_s[0].n}
        for k in range(KCH):
            bo = k % 2
            if k + 1 < KCH:
                nb = (k + 1) % 2
                if nb in xmid_used:
                    cv.wait(P, at=xmid_used[nb])
                cxm_s[nb].inc(P.dma_start(out=xmid_sb[:, nb, :],
                                    in_=xmid_view[:, k + 1, g0:g0 + 512]), 16)
                xm_loaded[nb] = cxm_s[nb].n
            ca.wait(V)
            cxm_s[bo].wait(V, at=xm_loaded[bo])
            if bo in out_stored:
                cyb[bo].wait(V)
            V.tensor_sub(tE[:, :], draw[:, k, :], mrepE[:, :])
            V.tensor_mul(tE[:, :], tE[:, :], rrepE[:, :])
            V.tensor_add(outb[:, bo, :], xmid_sb[:, bo, :], tE[:, :])
            V.engine_nop()
            xmid_used[bo] = cv.n
            cv.wait(S)
            cyb[bo].inc(S.dma_start(out=y_out.rearrange("(k p) t -> p k t", p=128)[:, k, g0:g0 + 512],
                                    in_=outb[:, bo, :]), 16)
            out_stored[bo] = cyb[bo].n
    cyb[0].wait(S)
    cyb[1].wait(S)
    cdbg.wait(S)
    _scE.__exit__(None, None, None)
    return nc, dbg_out


def _prep_inputs(inputs, c):
    b, hg = c // 2, c % 2
    x = inputs["L0_attn_x"]
    Wqkv = np.asarray(inputs["L0_attn_Wqkv"], np.float32)
    Wz = np.asarray(inputs["L0_attn_Wz"], np.float32)
    Wb = np.asarray(inputs["L0_attn_Wb"], np.float32)
    Wa = np.asarray(inputs["L0_attn_Wa"], np.float32)
    conv_w = np.asarray(inputs["L0_attn_conv_w"], np.float32)
    Wout = np.asarray(inputs["L0_attn_Wout"], np.float32)
    hsl = slice(hg * 1024, hg * 1024 + 1024)
    qs, ks, vs = Wqkv[:, 0:2048][:, hsl], Wqkv[:, 2048:4096][:, hsl], Wqkv[:, 4096:6144][:, hsl]
    Wslice = np.concatenate([qs, ks, vs], axis=1)
    wqkv_t = np.ascontiguousarray(Wslice.reshape(16, 128, 24, 128).transpose(2, 1, 0, 3))
    wz_t = np.ascontiguousarray(Wz[:, hsl].reshape(16, 128, 8, 128).transpose(2, 1, 0, 3))
    hh = slice(hg * 8, hg * 8 + 8)
    wba_t = np.ascontiguousarray(
        np.stack([Wb[:, hh].reshape(16, 128, 8), Wa[:, hh].reshape(16, 128, 8)])
        .transpose(2, 0, 1, 3))
    cw = conv_w[:, 0, :]
    cws = np.concatenate([cw[0:2048][hsl], cw[2048:4096][hsl], cw[4096:6144][hsl]], axis=0)
    convw_t = np.ascontiguousarray(cws.reshape(24, 128, 4).transpose(1, 0, 2))
    Wp = np.concatenate([Wout[hg * 1024:hg * 1024 + 1024],
                         Wout[(1 - hg) * 1024:(1 - hg) * 1024 + 1024]], axis=0)
    wout_t = np.ascontiguousarray(Wp.reshape(16, 128, 16, 128).transpose(2, 1, 0, 3))
    return {
        "xT": np.ascontiguousarray(np.asarray(x, np.float32)[b].T),
        "wqkv": wqkv_t, "wz": wz_t, "wba": wba_t, "convw": convw_t,
        "alog": np.ascontiguousarray(
            np.asarray(inputs["L0_attn_A_log"], np.float32)[hh].reshape(8, 1)),
        "dtb": np.ascontiguousarray(
            np.asarray(inputs["L0_attn_dt_bias"], np.float32)[hh].reshape(8, 1)),
        "normw": np.ascontiguousarray(
            np.asarray(inputs["L0_attn_norm_w"], np.float32).reshape(128, 1)),
        "wout": wout_t,
    }


def _prep_shared(inputs):
    def u8view(a):
        a = np.ascontiguousarray(a)
        assert a.dtype == np.int32
        return a.view(np.uint8).reshape(*a.shape, 4)[..., 0]

    qg = u8view(inputs["L0_ffn_gate_w"])
    qu = u8view(inputs["L0_ffn_up_w"])
    qd = u8view(inputs["L0_ffn_down_w"])
    wg_t = np.ascontiguousarray(qg.reshape(64, 128, 16, 128).transpose(0, 3, 2, 1))
    wu_t = np.ascontiguousarray(qu.reshape(64, 128, 16, 128).transpose(0, 3, 2, 1))
    wd_t = np.ascontiguousarray(qd.reshape(16, 128, 64, 128).transpose(0, 3, 2, 1))
    return {
        "wg": wg_t, "wu": wu_t, "wd": wd_t,
        "sg": np.ascontiguousarray(
            np.asarray(inputs["L0_ffn_gate_scale"], np.float32).reshape(64, 128).T),
        "su": np.ascontiguousarray(
            np.asarray(inputs["L0_ffn_up_scale"], np.float32).reshape(64, 128).T),
        "sd": np.ascontiguousarray(
            np.asarray(inputs["L0_ffn_down_scale"], np.float32).reshape(16, 128).T),
        "id128": np.eye(128, dtype=BF),
        "id8": np.eye(8, dtype=np.float32),
    }


def make_in_maps(inputs):
    shared = _prep_shared(inputs)
    in_maps = []
    for c in range(8):
        m = _prep_inputs(inputs, c)
        m.update(shared)
        in_maps.append(m)
    return in_maps


def kernel(**inputs):
    in_maps = make_in_maps(inputs)
    nc, _ = build_nc()
    res = run_bass_kernel_spmd(nc, in_maps, core_ids=list(range(8)))
    out = np.empty((4, 2048, 2048), np.float32)
    for c in range(8):
        b, hg = c // 2, c % 2
        out[b, hg * 1024:(hg + 1) * 1024, :] = res.results[c]["y"].T
    return out


# version 47
# speedup vs baseline: 1.1944x; 1.0171x over previous
import dataclasses
import numpy as np
import ml_dtypes
import concourse.bass as bass
import concourse.mybir as mybir
from concourse.bass import ds
from concourse.bass_utils import run_bass_kernel_spmd

F32 = mybir.dt.float32
BF16 = mybir.dt.bfloat16
U8 = mybir.dt.uint8
AF = mybir.ActivationFunctionType
ALU = mybir.AluOpType
BF = ml_dtypes.bfloat16

L = 2048
D = 2048
NH = 8
C = 128
SC = 512
NSC = L // SC
NCH = SC // C
KCH = D // 128
QK_CT = 16
N_CT = 24
Z_CT = 8
TOKH = 1024
IT_FF = 64
DT_FF = 16
FGRP = 512


class EngWrap:

    _PASS = {"wait_ge", "nop", "dma_start", "collective_compute", "partition_id",
             "indirect_dma_start", "sem_inc", "drain"}

    def __init__(self, eng, ctr):
        self._e = eng
        self._c = ctr

    def __getattr__(self, name):
        f = getattr(self._e, name)
        if name in self._PASS or name.startswith("_"):
            return f

        def g(*a, **k):
            self._e.wait_ge(self._c.h, self._c.n)
            r = f(*a, **k)
            self._c.inc(r)
            return r
        return g


class Ctr:
    def __init__(self, nc, name):
        self.h = nc.alloc_semaphore(name)
        self.n = 0
        self._w = {}

    def inc(self, instr, k=1):
        instr.then_inc(self.h, k)
        self.n += k
        return instr

    def pre(self, eng):
        eng.wait_ge(self.h, self.n)

    def wait(self, eng, at=None):
        n = self.n if at is None else at
        e = getattr(eng, "_e", eng)
        k = id(e)
        if self._w.get(k, 0) >= n:
            return
        e.wait_ge(self.h, n)
        self._w[k] = n


def _bview(ap, count):
    return dataclasses.replace(ap, ap=[ap.ap[0], [0, count], ap.ap[1]])


def _bview3(ap, inner=128):
    return dataclasses.replace(ap, ap=[ap.ap[0], ap.ap[1], [0, inner]])


def _as3(ap, h=8, inner=128):
    return dataclasses.replace(ap, ap=[ap.ap[0], [inner, h], [1, inner]])


def build_nc(dbg=(), stop_after=None):
    nc = bass.Bass("TRN2", target_bir_lowering=False)
    T, V, A, P, S = nc.tensor, nc.vector, nc.scalar, nc.gpsimd, nc.sync
    _lp = nc.allow_low_precision("bf16 operands with fp32 PSUM accumulation per problem tolerance")
    _lp.__enter__()

    xT = nc.dram_tensor("xT", [D, L], F32, kind="ExternalInput").ap()
    wqkv = nc.dram_tensor("wqkv", [N_CT, 128, KCH, 128], F32, kind="ExternalInput").ap()
    wz = nc.dram_tensor("wz", [Z_CT, 128, KCH, 128], F32, kind="ExternalInput").ap()
    wba = nc.dram_tensor("wba", [128, 2, KCH, 8], F32, kind="ExternalInput").ap()
    convw = nc.dram_tensor("convw", [128, N_CT, 4], F32, kind="ExternalInput").ap()
    alog = nc.dram_tensor("alog", [8, 1], F32, kind="ExternalInput").ap()
    dtb = nc.dram_tensor("dtb", [8, 1], F32, kind="ExternalInput").ap()
    normw = nc.dram_tensor("normw", [128, 1], F32, kind="ExternalInput").ap()
    wout = nc.dram_tensor("wout", [DT_FF, 128, 16, 128], F32, kind="ExternalInput").ap()
    wg_in = nc.dram_tensor("wg", [IT_FF, 128, KCH, 128], U8, kind="ExternalInput").ap()
    wu_in = nc.dram_tensor("wu", [IT_FF, 128, KCH, 128], U8, kind="ExternalInput").ap()
    wd_in = nc.dram_tensor("wd", [DT_FF, 128, IT_FF, 128], U8, kind="ExternalInput").ap()
    sg_in = nc.dram_tensor("sg", [128, IT_FF], F32, kind="ExternalInput").ap()
    su_in = nc.dram_tensor("su", [128, IT_FF], F32, kind="ExternalInput").ap()
    sd_in = nc.dram_tensor("sd", [128, DT_FF], F32, kind="ExternalInput").ap()
    id128_in = nc.dram_tensor("id128", [128, 128], BF16, kind="ExternalInput").ap()
    id8_in = nc.dram_tensor("id8", [8, 8], F32, kind="ExternalInput").ap()
    y_out = nc.dram_tensor("y", [D, TOKH], F32, kind="ExternalOutput").ap()
    dbg_out = {}
    if "proj" in dbg:
        dbg_out["qhat"] = nc.dram_tensor("dbg_qhat", [128, N_CT, SC], BF16, kind="ExternalOutput").ap()
        dbg_out["zsc"] = nc.dram_tensor("dbg_zsc", [128, Z_CT, SC], BF16, kind="ExternalOutput").ap()
        dbg_out["bg"] = nc.dram_tensor("dbg_bg", [8, 3, SC], F32, kind="ExternalOutput").ap()
    if "og" in dbg:
        dbg_out["og"] = nc.dram_tensor("dbg_og", [128, NH, L], BF16, kind="ExternalOutput").ap()
    if "mid" in dbg:
        dbg_out["araw"] = nc.dram_tensor("dbg_araw", [128, KCH, TOKH], BF16, kind="ExternalOutput").ap()
        dbg_out["xmid"] = nc.dram_tensor("dbg_xmid", [128, KCH, TOKH], F32, kind="ExternalOutput").ap()
    if "ln2" in dbg:
        dbg_out["ln2"] = nc.dram_tensor("dbg_ln2", [128, KCH, TOKH], BF16, kind="ExternalOutput").ap()

    dbgb = nc.dram_tensor("dbgb", [4, 128, 1024], BF16, kind="ExternalOutput").ap()
    dbgf = nc.dram_tensor("dbgf", [4, 128, 1024], F32, kind="ExternalOutput").ap()
    cc_in = nc.dram_tensor("cc_in", [NSC, NH * 128, SC], BF16, kind="Internal")
    cc_out = nc.dram_tensor("cc_out", [NSC, 2 * NH * 128, SC], BF16, kind="Internal")
    xmid_dram = nc.dram_tensor("xmid_dram", [D, TOKH], F32, kind="Internal")
    wbf_dram = nc.dram_tensor("wbf_dram", [N_CT + Z_CT, 128, KCH * 128], BF16, kind="Internal")

    import contextlib
    sbctx_ab = contextlib.ExitStack()
    sbctx_2 = contextlib.ExitStack()
    sbctx_3 = contextlib.ExitStack()

    def sb(name, shape, dtype):
        return nc.alloc_sbuf_tensor(name, shape, dtype)

    def sbA(name, shape, dtype):
        return sbctx_ab.enter_context(nc.sbuf_tensor(name, shape, dtype))

    def sb2(name, shape, dtype):
        return sbctx_2.enter_context(nc.sbuf_tensor(name, shape, dtype))

    def sb3(name, shape, dtype):
        return sbctx_3.enter_context(nc.sbuf_tensor(name, shape, dtype))

    ones128_bf = sb("ones128_bf", [128, 1], BF16)
    ones1_bf = sb("ones1_bf", [1, 128], BF16)
    ones8_f = sb("ones8_f", [8, 128], F32)
    tri_ji = sb("tri_ji", [128, 128], F32)
    blockid = sb("blockid", [8, 8 * 128], F32)
    m128row = sb("m128row", [1, 128], BF16)
    id128 = sb("id128s", [128, 128], BF16)
    id8 = sb("id8s", [8, 8], F32)
    convw_sb = sb("convw_sb", [128, N_CT, 4], F32)
    wba_sb = sb("wba_sb", [128, 2, KCH, 8], BF16)
    alog_sb = sb("alog_sb", [8, 1], F32)
    dtb_sb = sb("dtb_sb", [8, 1], F32)
    negA = sb("negA", [8, 1], F32)
    normw_sb = sb("normw_sb", [128, 1], F32)
    sg_sb = sb("sg_sb", [128, IT_FF], F32)
    su_sb = sb("su_sb", [128, IT_FF], F32)
    sd_sb = sb("sd_sb", [128, DT_FF], F32)
    constt = sb("constt", [128, 8], F32)
    stat_f = sb("stat_f", [1, 2 * SC], F32)
    stat_b = sb("stat_b", [1, 2 * SC], BF16)
    ln2 = sb("ln2", [128, KCH, TOKH], BF16)

    ln1 = sbA("ln1", [128, KCH, SC], BF16)
    mrep = sbA("mrep", [128, SC], BF16)
    rrep = sbA("rrep", [128, SC], BF16)
    wslab = sbA("wslab", [128, 3, KCH * 128], BF16)
    qraw = sbA("qraw", [128, N_CT, 520], BF16)
    tails = sbA("tails", [128, N_CT, 4], BF16)
    qhat = sbA("qhat", [128, N_CT, SC], BF16)
    z_sc = sbA("z_sc", [128, Z_CT, SC], BF16)
    bfm = sbA("bfm", [8, SC], F32)
    lnbfm = sbA("lnbfm", [8, SC], F32)
    gfm = sbA("gfm", [8, SC], F32)
    sp8a = sbA("sp8a", [8, SC], F32)
    sp8b = sbA("sp8b", [8, SC], F32)

    og_sc = sbA("og_sc", [128, NH, SC], BF16)
    M_f32 = sbA("M_f32", [128, NH, 128], F32)
    Mbf = sbA("Mbf", [128, NH, 128], BF16)

    g_tm = sbA("g_tm", [128, 8], F32)
    lnb_tm = sbA("lnb_tm", [128, 8], F32)
    G_tm = sbA("G_tm", [128, 8], F32)
    G8 = sbA("G8", [8, 128], F32)
    colA = sbA("colA", [128, 8], F32)
    rowB = sbA("rowB", [8, 128], F32)
    bdG = sbA("bdG", [8, 8 * 128], F32)
    bdB = sbA("bdB", [8, 8 * 128], F32)
    btm3 = sbA("btm3", [128, 3, 8], F32)
    etm3 = sbA("etm3", [128, 3, 8], F32)
    gam3 = sbA("gam3", [128, 3, 8], F32)
    t8a = sbA("t8a", [128, 8], F32)
    eG = sbA("eG", [128, NH, 128], BF16)
    nW = sbA("nW", [128, NH, 128], F32)
    nWy = sbA("nWy", [128, NH, 128], F32)
    eEb = sbA("eEb", [128, NH, 128], BF16)
    kk_sb = sbA("kk_sb", [128, NH, 128], BF16)
    qk_sb = sbA("qk_sb", [128, NH, 128], BF16)
    A_sb2 = sbA("A_sb2", [128, 2, NH, 128], BF16)
    AT_sb2 = sbA("AT_sb2", [128, 2, NH, 128], BF16)
    V6T2 = sbA("V6T2", [128, 2, NH, 128], BF16)
    Pw = [sbA("Pw0", [128, NH, 128], BF16), sbA("Pw1", [128, NH, 128], BF16)]
    PTw = [sbA("PTw0", [128, NH, 128], BF16), sbA("PTw1", [128, NH, 128], BF16)]
    Vcp = [sbA("Vc0", [128, NH, 128], BF16), sbA("Vc1", [128, NH, 128], BF16)]

    def _carve3(base, off, n0):
        b = base[:, :, :]
        return dataclasses.replace(b, offset=b.offset + off,
                                   ap=[b.ap[0], [8 * 128, n0], [128, 8], [1, 128]])

    def _carve1(base, off):
        b = base[:, :, :]
        return dataclasses.replace(b, offset=b.offset + off,
                                   ap=[b.ap[0], [128, 8], [1, 128]])

    vtm3 = _carve3(ln1, 0, 3)
    ktm3 = _carve3(ln1, 3072, 3)
    eEc_t = _carve1(ln1, 6144)
    ST3 = _carve3(qraw, 0, 3)
    qtil3 = _carve3(qraw, 3072, 3)
    ktil3 = _carve3(qraw, 6144, 3)
    R0t = _carve1(qraw, 9216)
    Ut = _carve1(qraw, 10240)
    eEa_t = _carve1(qraw, 11264)

    import contextlib
    pctx = contextlib.ExitStack()
    psA = pctx.enter_context(nc.psum_tensor("psA", [128, 512], F32))
    ps_b1 = pctx.enter_context(nc.psum_tensor("ps_b1", [128, 1024], F32))
    ps_b2 = pctx.enter_context(nc.psum_tensor("ps_b2", [128, 1024], F32))
    ps_m1 = pctx.enter_context(nc.psum_tensor("ps_m1", [128, 1024], F32))
    ps_tr = pctx.enter_context(nc.psum_tensor("ps_tr", [128, 1024], BF16))

    csd = Ctr(nc, "csd")
    cx = Ctr(nc, "cx")
    cw_s = [Ctr(nc, f"cw{i}") for i in range(4)]
    cwo_s = [Ctr(nc, "cwo0"), Ctr(nc, "cwo1")]
    cgu_s = [Ctr(nc, f"cgu{i}") for i in range(4)]
    cwd_s = [Ctr(nc, f"cwd{i}") for i in range(4)]
    cog = Ctr(nc, "cog")
    cph = Ctr(nc, "cph")
    cxm_s = [Ctr(nc, "cxm0"), Ctr(nc, "cxm1")]
    cspill = Ctr(nc, "cspill")
    cwb = Ctr(nc, "cwb")
    cyb = [Ctr(nc, "cy0"), Ctr(nc, "cy1")]
    cdbg = Ctr(nc, "cdbg")
    ct_ = Ctr(nc, "ct")
    cv = Ctr(nc, "cv")
    ca = Ctr(nc, "ca")
    cp = Ctr(nc, "cp")
    ccc = Ctr(nc, "ccc")

    V = EngWrap(V, cv)
    A = EngWrap(A, ca)
    P_raw = P
    P = EngWrap(P, cp)

    pid = P_raw.partition_id()
    half_off = (pid % 2) * TOKH
    peer_row = (pid + 1 - (pid % 2) * 2) * (NH * 128)

    def stats_epilogue(ms_ap, qs_ap, m_dst, r_dst, eps, repl_ps=None):
        ct_.wait(V)
        V.tensor_scalar_mul(stat_f[0:1, 0:512], ms_ap, 1.0 / D)
        V.tensor_copy(stat_b[0:1, 0:512], stat_f[0:1, 0:512])
        V.tensor_mul(stat_f[0:1, 512:1024], stat_f[0:1, 0:512], stat_f[0:1, 0:512])
        V.scalar_tensor_tensor(stat_f[0:1, 0:512], qs_ap, 1.0 / D,
                               stat_f[0:1, 512:1024], ALU.mult, ALU.subtract)
        V.engine_nop()
        cv.wait(A)
        A.activation(stat_f[0:1, 512:1024], stat_f[0:1, 0:512], AF.Sqrt,
                            bias=eps, scale=1.0)
        ca.wait(V)
        V.reciprocal(stat_b[0:1, 512:1024], stat_f[0:1, 512:1024])
        cv.wait(T)
        ca.wait(T)
        rp_ = ps_b2 if repl_ps is None else repl_ps
        _tmm = T.matmul(rp_[:, 0:512], ones1_bf[:, :], stat_b[0:1, 0:512], start=True, stop=True)
        ct_.inc(T.matmul(rp_[:, 512:1024], ones1_bf[:, :], stat_b[0:1, 512:1024],
                         start=True, stop=True))
        ct_.wait(A)
        A.activation(m_dst, rp_[:, 0:512], AF.Copy, bias=0.0, scale=1.0)
        A.activation(r_dst, rp_[:, 512:1024], AF.Copy, bias=0.0, scale=1.0)

    csd.inc(S.dma_start(out=convw_sb[:, :, :], in_=convw), 16)
    cx.inc(P.dma_start(out=wba_sb[:, :, :, :], in_=wba), 16)
    csd.inc(S.dma_start(out=alog_sb[:, :], in_=alog), 16)
    csd.inc(S.dma_start(out=dtb_sb[:, :], in_=dtb), 16)
    csd.inc(S.dma_start(out=normw_sb[:, :], in_=normw), 16)
    csd.inc(S.dma_start(out=sg_sb[:, :], in_=sg_in), 16)
    csd.inc(S.dma_start(out=su_sb[:, :], in_=su_in), 16)
    csd.inc(S.dma_start(out=sd_sb[:, :], in_=sd_in), 16)
    csd.inc(S.dma_start(out=id128[:, :], in_=id128_in), 16)
    csd.inc(S.dma_start(out=id8[:, :], in_=id8_in), 16)

    _cvals = [0.0, 1.0, -1.0, 1e-5, 1e-6, 128.0 * 1e-6, 128.0, 1.0 / 128.0]
    for _i, _v in enumerate(_cvals):
        V.memset(constt[:, _i:_i + 1], _v)
        nc.const_aps.register(F32, _v, constt[:, _i:_i + 1])
    V.memset(ones128_bf[:, :], 1.0)
    V.memset(ones1_bf[:, :], 1.0)
    V.memset(ones8_f[:, :], 1.0)
    V.memset(m128row[:, :], -128.0)
    V.memset(tails[:, :, :], 0.0)
    V.memset(M_f32[:, :, :], 0.0)
    V.memset(tri_ji[:, :], 1.0)
    V.memset(blockid[:, :], 1.0)
    cv.wait(P)
    P.affine_select(tri_ji[:, :], tri_ji[:, :], [[1, 128]], ALU.is_ge, 0.0,
                           channel_multiplier=-1)
    P.affine_select(blockid[:, :], blockid[:, :], [[-1, 8], [0, 128]],
                           ALU.is_equal, 0.0, channel_multiplier=1)
    csd.wait(A)
    cv.wait(A)
    A.activation(negA[:, :], alog_sb[:, :], AF.Exp, bias=0.0, scale=1.0)
    ca.wait(V)
    V.tensor_scalar_mul(negA[:, :], negA[:, :], -1.0)
    cp.wait(T)
    cp.wait(V)
    csd.wait(T)
    csd.wait(V)
    csd.wait(P)

    PAIRS = [[0, 1], [2, 3], [4, 5], [6, 7]]
    w_used = {}
    wb_at = {}
    og_stored = 0
    cc_issued = 0

    def scope(name):
        return nc.named_scope(name)

    for sc in range(NSC):
        _scA = scope(f"A{sc}"); _scA.__enter__()
        tok0 = sc * SC
        if sc > 0:
            ct_.wait(P)
            cv.wait(P)
            ca.wait(P)
            cdbg.wait(P)
        if sc == 0:
            cx.inc(P.dma_start(out=qhat[:, 0:16, :],
                               in_=xT[:, tok0:tok0 + SC].rearrange("(k p) t -> p k t", p=128)), 16)
        if sc == 0:
            l2done = {}
            cviews = {}
            tas = {}
            eg_at = {}
            etm_at = {}
            eea_at = {}
            eeb_at = {}
            eec_at = {}
            amask_at = {}
            stmask_at = {}
            vtm_at = {}
            ktm_at = {}
            ktil_at = {}
            mbf_at = {}
            y_r0 = {}
            u_at = {}
            bank_free = {}
            buf_read = {}
            buf_rdy = {}
            psm1_s = {"a": 0, "v": 0}
            pstr_s = {"a": 0, "v": 0}
            psa_s = {"a": 0, "v": 0}
            rot = [ps_b2, ps_b1]
            vread = {}

        def psm1_take():
            ca.wait(T, at=psm1_s["a"])
            cv.wait(T, at=psm1_s["v"])

        def bwait(eng, key):
            r = buf_rdy.get(key)
            if r is not None:
                (ca if r[0] == "A" else cv if r[0] == "V" else cp).wait(eng, at=r[1])

        def chain_mm(bank_i, lhs, lkey, rhs, rkey):
            if bank_i in bank_free:
                ca.wait(T, at=bank_free[bank_i][0])
                cv.wait(T, at=bank_free[bank_i][1])
            bwait(T, lkey)
            bwait(T, rkey)
            pb = rot[bank_i]
            for h in range(8):
                mm = T.matmul(pb[:, h * 128:(h + 1) * 128], lhs[:, h, :], rhs[:, h, :],
                              start=True, stop=True)
            ct_.inc(mm)
            buf_read[lkey] = ct_.n
            buf_read[rkey] = ct_.n
            return ct_.n

        def evac_copy(bank_i, dst, dkey, prod_at, eng="A"):
            if eng == "A":
                ct_.wait(A, at=max(prod_at, buf_read.get(dkey, 0)))
                A.activation(dst[:, :, :], rot[bank_i][:, 0:1024], AF.Copy, bias=0.0, scale=1.0)
                bank_free[bank_i] = (ca.n, bank_free.get(bank_i, (0, 0))[1])
                buf_rdy[dkey] = ("A", ca.n)
            else:
                ct_.wait(V, at=max(prod_at, buf_read.get(dkey, 0)))
                V.tensor_copy(dst[:, :, :], rot[bank_i][:, 0:1024])
                bank_free[bank_i] = (bank_free.get(bank_i, (0, 0))[0], cv.n)
                buf_rdy[dkey] = ("V", cv.n)

        def evac_add(bank_i, dst, dkey, a_src, prod_at):
            ct_.wait(V, at=max(prod_at, buf_read.get(dkey, 0)))
            V.tensor_add(dst[:, :, :], a_src[:, :, :], _as3(rot[bank_i][:, 0:1024]))
            bank_free[bank_i] = (bank_free.get(bank_i, (0, 0))[0], cv.n)
            buf_rdy[dkey] = ("V", cv.n)

        def u_prep_open(c):
            d2, d3 = c % 2, c % 3
            tas[c] = slice((c % NCH) * C, (c % NCH) * C + C)
            cviews[c] = dict(a=A_sb2[:, d2], at=AT_sb2[:, d2], st=ST3[:, d3],
                             kt=ktil3[:, d3], qt=qtil3[:, d3], vt=vtm3[:, d3],
                             km=ktm3[:, d3], v6=V6T2[:, d2], b=btm3[:, d3, :],
                             e=etm3[:, d3, :], g=gam3[:, d3, :])

        def u_p1(c):
            vw = cviews[c]; ta = tas[c]
            cv.wait(T, at=psa_s["v"])
            ca.wait(T, at=psa_s["a"])
            _tmm = T.transpose(psA[:, 0:8], gfm[:, ta], id8[:, :])
            _tmm = T.transpose(psA[:, 8:16], bfm[:, ta], id8[:, :])
            ct_.inc(T.transpose(psA[:, 16:24], lnbfm[:, ta], id8[:, :]))
            ct_.wait(V)
            V.tensor_copy(g_tm[:, :], psA[:, 0:8])
            V.tensor_copy(vw["b"], psA[:, 8:16])
            V.tensor_copy(lnb_tm[:, :], psA[:, 16:24])

        def u_p2(c):
            ta = tas[c]
            cv.wait(T)
            _tmm = T.matmul(psA[:, 24:32], tri_ji[:, :], g_tm[:, :], start=True, stop=True)
            ct_.inc(T.matmul(psA[0:8, 32:160], g_tm[:, :], tri_ji[:, :], start=True, stop=True))
            ct_.wait(V)
            V.tensor_copy(G_tm[:, :], psA[:, 24:32])
            V.tensor_copy(G8[:, :], psA[0:8, 32:160])
            V.tensor_add(colA[:, :], G_tm[:, :], lnb_tm[:, :])
            V.tensor_add(rowB[:, :], G8[:, :], lnbfm[:, ta])
            V.tensor_mul(bdG[:, :], _bview(G8[:, :], 8), blockid[:, :])
            V.tensor_mul(bdB[:, :], _bview(rowB[:, :], 8), blockid[:, :])
            psa_s["v"] = cv.n

        def u_p3a(c):
            vw = cviews[c]
            psm1_take()
            cv.wait(T)
            _tmm = T.matmul(ps_m1[:, 0:512], ones8_f[:, :], bdG[:, 0:512], start=True, stop=True)
            ct_.inc(T.matmul(ps_m1[:, 512:1024], ones8_f[:, :], bdG[:, 512:1024],
                             start=True, stop=True))
            ct_.wait(A)
            cv.wait(A)
            A.activation(eG[:, :, :], ps_m1[:, 0:1024], AF.Exp, bias=0.0, scale=1.0)
            eg_at[c] = ca.n
            pm1 = ps_m1[:, 0:1024]
            gb_last = dataclasses.replace(pm1, ap=[pm1.ap[0], [128, 8]],
                                          offset=pm1.offset + 127)
            ct_.wait(V)
            ca.wait(V)
            V.tensor_sub(t8a[:, :], gb_last, G_tm[:, :])
            cv.wait(A)
            A.activation(vw["e"], t8a[:, :], AF.Exp, bias=0.0, scale=1.0)
            A.activation(vw["g"], gb_last, AF.Exp, bias=0.0, scale=1.0)
            etm_at[c] = ca.n
            ca.wait(V)
            V.tensor_sub(nW[:, :, :], _as3(pm1), _bview3(colA[:, :]))
            V.tensor_scalar_max(nW[:, :, :], nW[:, :, :], 0.0)
            cv.wait(A)
            A.activation(eEa_t[:, :, :], nW[:, :, :], AF.Exp, bias=0.0, scale=-1.0)
            eea_at[c] = ca.n
            ca.wait(V)
            V.tensor_sub(nW[:, :, :], _as3(pm1), _bview3(G_tm[:, :]))
            V.tensor_scalar_min(nW[:, :, :], nW[:, :, :], 0.0)
            cv.wait(A)
            A.activation(eEc_t[:, :, :], nW[:, :, :], AF.Exp, bias=0.0, scale=1.0)
            eec_at[c] = ca.n
            psm1_s["a"] = ca.n
            psm1_s["v"] = cv.n

        def u_p3b(c):
            psm1_take()
            _tmm = T.matmul(ps_m1[:, 0:512], ones8_f[:, :], bdB[:, 0:512], start=True, stop=True)
            ct_.inc(T.matmul(ps_m1[:, 512:1024], ones8_f[:, :], bdB[:, 512:1024],
                             start=True, stop=True))
            ct_.wait(V)
            ca.wait(V, at=eec_at[c])
            V.tensor_sub(nW[:, :, :], _as3(ps_m1[:, 0:1024]), _bview3(G_tm[:, :]))
            V.tensor_scalar_min(nW[:, :, :], nW[:, :, :], 0.0)
            cv.wait(A)
            A.activation(eEb[:, :, :], nW[:, :, :], AF.Exp, bias=0.0, scale=1.0)
            eeb_at[c] = ca.n
            psm1_s["v"] = cv.n

        def u_p4(c):
            vw = cviews[c]; ta = tas[c]; d2 = c % 2
            psm1_take()
            cv.wait(T, at=l2done[c // NCH])
            for h in range(8):
                _tmm = T.matmul(ps_m1[:, h * 128:(h + 1) * 128], qhat[:, 8 + h, ta],
                                qhat[:, 8 + h, ta], start=True, stop=True)
            ct_.inc(_tmm)
            ct_.wait(A)
            cv.wait(A)
            A.activation(kk_sb[:, :, :], ps_m1[:, 0:1024], AF.Copy, bias=0.0, scale=1.0)
            kk_at = ca.n
            ca.wait(V, at=kk_at)
            ct_.wait(V, at=buf_read.get(f"A2_{d2}", 0))
            bwait(V, f"A2_{d2}")
            V.tensor_mul(vw["a"], eEa_t[:, :, :], kk_sb[:, :, :])
            cv.wait(P)
            P.affine_select(vw["a"], vw["a"], [[0, 8], [-1, 128]],
                            ALU.is_gt, 0.0, channel_multiplier=1)
            ca.wait(V, at=eeb_at[c])
            ct_.wait(V, at=buf_read.get(f"AT2_{d2}", 0))
            bwait(V, f"AT2_{d2}")
            V.tensor_mul(vw["at"], eEb[:, :, :], kk_sb[:, :, :])
            cv.wait(P)
            P.affine_select(vw["at"], vw["at"], [[0, 8], [1, 128]],
                            ALU.is_gt, 0.0, channel_multiplier=-1)
            amask_at[c] = cp.n
            buf_rdy[f"A2_{d2}"] = ("P", cp.n)
            buf_rdy[f"AT2_{d2}"] = ("P", cp.n)
            psm1_s["a"] = kk_at

        def u_p5(c):
            vw = cviews[c]; ta = tas[c]; d3 = c % 3
            psm1_take()
            cv.wait(T, at=l2done[c // NCH])
            for h in range(8):
                _tmm = T.matmul(ps_m1[:, h * 128:(h + 1) * 128], qhat[:, 8 + h, ta],
                                qhat[:, h, ta], start=True, stop=True)
            ct_.inc(_tmm)
            ct_.wait(A)
            cv.wait(A)
            A.activation(qk_sb[:, :, :], ps_m1[:, 0:1024], AF.Copy, bias=0.0, scale=1.0)
            qk_at = ca.n
            ca.wait(V, at=qk_at)
            ct_.wait(V, at=buf_read.get(f"ST_{d3}", 0))
            bwait(V, f"ST_{d3}")
            V.tensor_mul(vw["st"], eEc_t[:, :, :], qk_sb[:, :, :])
            cv.wait(P)
            P.affine_select(vw["st"], vw["st"], [[0, 8], [1, 128]],
                            ALU.is_ge, 0.0, channel_multiplier=-1)
            stmask_at[c] = cp.n
            psm1_s["a"] = qk_at

        def u_p6a(c):
            vw = cviews[c]; ta = tas[c]; d3 = c % 3
            ca.wait(T, at=pstr_s["a"])
            for h in range(8):
                _tmm = T.transpose(ps_tr[:, h * 128:(h + 1) * 128], qhat[:, 16 + h, ta],
                                   id128[:, :])
            ct_.inc(_tmm)
            ct_.wait(A)
            ct_.wait(A, at=buf_read.get(f"VT_{d3}", 0))
            cv.wait(A, at=vread.get(f"VT_{d3}", 0))
            A.activation(vw["vt"], ps_tr[:, :], AF.Copy, bias=0.0, scale=1.0)
            vtm_at[c] = ca.n
            pstr_s["a"] = ca.n

        def u_p6b(c):
            vw = cviews[c]; ta = tas[c]; d3 = c % 3
            ca.wait(T, at=pstr_s["a"])
            cv.wait(T, at=l2done[c // NCH])
            for h in range(8):
                _tmm = T.transpose(ps_tr[:, h * 128:(h + 1) * 128], qhat[:, 8 + h, ta],
                                   id128[:, :])
            ct_.inc(_tmm)
            ct_.wait(A)
            ct_.wait(A, at=buf_read.get(f"KM_{d3}", 0))
            A.activation(vw["km"], ps_tr[:, :], AF.Copy, bias=0.0, scale=1.0)
            ktm_at[c] = ca.n
            pstr_s["a"] = ca.n

        def u_p7(c):
            vw = cviews[c]; ta = tas[c]; d3 = c % 3
            ca.wait(V, at=eg_at[c])
            ct_.wait(V, at=max(buf_read.get(f"KT_{d3}", 0), buf_read.get(f"QT_{d3}", 0)))
            V.tensor_mul(vw["kt"], qhat[:, 8:16, ta], eG[:, :, :])
            V.tensor_mul(vw["qt"], qhat[:, 0:8, ta], eG[:, :, :])
            ktil_at[c] = cv.n

        def build_chain(c):
            vw = cviews[c]; d2 = c % 2
            units = []
            rp = [0]

            def c0():
                cp.wait(V, at=amask_at[c])
                ct_.wait(V, at=buf_read.get("VcA", 0))
                V.tensor_sub(Vcp[0][:, :, :], _bview(id128[:, :], 8), vw["a"])
                buf_rdy["VcA"] = ("V", cv.n)
            units.append(c0)
            cur, nxt = "VcA", "VcB"
            tl = {"VcA": Vcp[0], "VcB": Vcp[1], "Pw0": Pw[0], "Pw1": Pw[1],
                  "PTw0": PTw[0], "PTw1": PTw[1]}
            pk, ptk = f"A2_{d2}", f"AT2_{d2}"
            pt_, p_ = vw["at"], vw["a"]
            pend_v = []
            for lev in range(1, 7):
                npk, nptk = f"Pw{lev % 2}", f"PTw{lev % 2}"

                def s_unit(pk=pk, ptk=ptk, p_=p_, pt_=pt_, nptk=nptk, lev=lev):
                    b = rp[0] % 2
                    at = chain_mm(b, p_, pk, pt_, ptk)
                    evac_copy(b, tl[nptk], nptk, at)
                    rp[0] += 1
                units.append(s_unit)

                def v_unit(nptk=nptk, cur=cur, nxt=nxt):
                    b = rp[0] % 2
                    at = chain_mm(b, tl[nptk], nptk, tl[cur], cur)
                    evac_add(b, tl[nxt], nxt, tl[cur], at)
                    rp[0] += 1
                pend_v.append(v_unit)
                if lev < 6:
                    def p_unit(pk=pk, ptk=ptk, p_=p_, pt_=pt_, npk=npk):
                        b = rp[0] % 2
                        at = chain_mm(b, pt_, ptk, p_, pk)
                        evac_copy(b, tl[npk], npk, at, eng="V")
                        rp[0] += 1
                    units.append(p_unit)
                if len(pend_v) > 1:
                    units.append(pend_v.pop(0))
                cur, nxt = nxt, cur
                pk, ptk, p_, pt_ = npk, nptk, tl[npk], tl[nptk]
            while pend_v:
                units.append(pend_v.pop(0))

            def v6t(cur=cur):
                ca.wait(T, at=pstr_s["a"])
                bwait(T, cur)
                for h in range(8):
                    _tmm = T.transpose(ps_tr[:, h * 128:(h + 1) * 128], tl[cur][:, h, :],
                                       id128[:, :])
                ct_.inc(_tmm)
                buf_read[cur] = ct_.n
                ct_.wait(A)
                ct_.wait(A, at=buf_read.get(f"V6_{d2}", 0))
                A.activation(vw["v6"], ps_tr[:, :], AF.Copy, bias=0.0, scale=1.0)
                buf_rdy[f"V6_{d2}"] = ("A", ca.n)
                pstr_s["a"] = ca.n
            units.append(v6t)
            return units

        def u_y0(c):
            ct_.wait(V, at=buf_read.get("Mbf", 0))
            V.tensor_copy(Mbf[:, :, :], M_f32[:, :, :])
            mbf_at[c] = cv.n

        def u_y1(c):
            vw = cviews[c]; d3 = c % 3
            psm1_take()
            cv.wait(T, at=mbf_at[c])
            cv.wait(T, at=ktil_at[c])
            for h in range(8):
                _tmm = T.matmul(ps_m1[:, h * 128:(h + 1) * 128], vw["kt"][:, h, :],
                                Mbf[:, h, :], start=True, stop=True)
            ct_.inc(_tmm)
            buf_read[f"KT_{d3}"] = ct_.n
            buf_read["Mbf"] = ct_.n
            ct_.wait(V)
            ca.wait(V, at=vtm_at[c])
            ct_.wait(V, at=buf_read.get("R0", 0))
            V.tensor_sub(nWy[:, :, :], vw["vt"], _as3(ps_m1[:, 0:1024]))
            V.tensor_mul(R0t[:, :, :], nWy[:, :, :], _bview3(vw["b"]))
            vread[f"VT_{d3}"] = cv.n
            y_r0[c] = cv.n
            psm1_s["v"] = cv.n

        def u_y2(c):
            d2 = c % 2
            psm1_take()
            bwait(T, f"V6_{d2}")
            cv.wait(T, at=y_r0[c])
            for h in range(8):
                _tmm = T.matmul(ps_m1[:, h * 128:(h + 1) * 128], V6T2[:, d2][:, h, :],
                                R0t[:, h, :], start=True, stop=True)
            ct_.inc(_tmm)
            buf_read[f"V6_{d2}"] = ct_.n
            buf_read["R0"] = ct_.n
            ct_.wait(A)
            ct_.wait(A, at=buf_read.get("Ut", 0))
            cv.wait(A, at=vread.get("Ut", 0))
            A.activation(Ut[:, :, :], ps_m1[:, 0:1024], AF.Copy, bias=0.0, scale=1.0)
            u_at[c] = ca.n
            psm1_s["a"] = ca.n

        def u_y3(c):
            vw = cviews[c]; ta = tas[c]; d3 = c % 3
            psm1_take()
            ca.wait(T, at=u_at[c])
            cv.wait(T, at=ktil_at[c])
            cp.wait(T, at=stmask_at[c])
            for h in range(8):
                _tmm = T.matmul(ps_m1[:, h * 128:(h + 1) * 128], Mbf[:, h, :],
                                vw["qt"][:, h, :], start=True, stop=False)
                _tmm = T.matmul(ps_m1[:, h * 128:(h + 1) * 128], Ut[:, h, :],
                                vw["st"][:, h, :], start=False, stop=True)
            ct_.inc(_tmm)
            buf_read[f"QT_{d3}"] = ct_.n
            buf_read[f"ST_{d3}"] = ct_.n
            buf_read["Ut"] = ct_.n
            buf_read["Mbf"] = ct_.n
            ct_.wait(A)
            if c == 0 and sc > 0:
                cog.wait(A, at=og_stored)
            A.activation(og_sc[:, :, ta], ps_m1[:, 0:1024], AF.Copy, bias=0.0, scale=1.0)
            psm1_s["a"] = ca.n

        def u_y4(c):
            vw = cviews[c]; d3 = c % 3
            ca.wait(V, at=etm_at[c])
            ct_.wait(V, at=buf_read.get("R0", 0))
            ca.wait(V, at=u_at[c])
            V.tensor_mul(R0t[:, :, :], Ut[:, :, :], _bview3(vw["e"]))
            vread["Ut"] = cv.n
            ee2_at = cv.n
            psm1_take()
            ca.wait(T, at=ktm_at[c])
            cv.wait(T, at=ee2_at)
            for h in range(8):
                _tmm = T.matmul(ps_m1[:, h * 128:(h + 1) * 128], vw["km"][:, h, :],
                                R0t[:, h, :], start=True, stop=True)
            ct_.inc(_tmm)
            buf_read[f"KM_{d3}"] = ct_.n
            buf_read["R0"] = ct_.n
            ct_.wait(V)
            V.tensor_mul(nWy[:, :, :], M_f32[:, :, :], _bview3(vw["g"]))
            V.tensor_add(M_f32[:, :, :], nWy[:, :, :], _as3(ps_m1[:, 0:1024]))
            psm1_s["v"] = cv.n

        chq = []
        lfq = []

        def cpop():
            if lfq:
                lfq.pop(0)()
            elif chq:
                chq.pop(0)()

        cx.wait(T)
        cv.wait(T)
        ca.wait(T)
        for k in range(KCH):
            _tmm = T.matmul(psA[0:1, 0:512], ones128_bf[:, :], qhat[:, k, :],
                     start=(k == 0), stop=(k == KCH - 1))
        ct_.inc(_tmm)
        cx.wait(A)
        if sc > 0:
            ct_.wait(A)
        for k in range(KCH):
            A.activation(ln1[:, k, :], qhat[:, k, :], AF.Square, bias=0.0, scale=1.0)
            _sq = ca.n
            ca.wait(T, at=_sq)
            _tmm = T.matmul(psA[32:33, 0:512], ones128_bf[:, :], ln1[:, k, :],
                     start=(k == 0), stop=(k == KCH - 1))
        ct_.inc(_tmm)
        A.nop()
        stats_epilogue(psA[0:1, 0:512], psA[32:33, 0:512], mrep[:, :], rrep[:, :], 1e-5)
        ca.wait(V)

        w_loaded = {}

        def w_dma(ci, buf):
            if sc == 0:
                srcw = (wqkv[ci] if ci < N_CT else wz[ci - N_CT]).rearrange("p k c -> p (k c)")
                qe = P
            else:
                srcw = wbf_dram.ap()[ci]
                qe = S
            if ("w", buf) in w_used:
                ct_.wait(qe, at=w_used[("w", buf)])
            if buf in wb_at:
                cwb.wait(qe, at=wb_at[buf])
            cw_s[buf].inc(qe.dma_start(out=wslab[:, buf, :], in_=srcw), 16)
            w_loaded[buf] = cw_s[buf].n
            if sc == 0:
                cw_s[buf].wait(S, at=w_loaded[buf])
                cwb.inc(S.dma_start(out=wbf_dram.ap()[ci], in_=wslab[:, buf, :]), 16)
                wb_at[buf] = cwb.n

        for _pf in range(2):
            w_dma(_pf, _pf)

        cv.wait(T)
        ca.wait(T)
        cx.wait(T)
        for k in range(KCH):
            V.tensor_sub(ln1[:, k, :], qhat[:, k, :], mrep[:, :])
            V.tensor_mul(ln1[:, k, :], ln1[:, k, :], rrep[:, :])
            _lnk = cv.n
            cv.wait(T, at=_lnk)
            _tmm = T.matmul(psA[0:8, 0:512], wba_sb[:, 0, k, :], ln1[:, k, :],
                     start=(k == 0), stop=(k == KCH - 1))
            _tmm = T.matmul(psA[32:40, 0:512], wba_sb[:, 1, k, :], ln1[:, k, :],
                     start=(k == 0), stop=(k == KCH - 1))
        ct_.inc(_tmm)
        lnv = cv.n
        ct_.wait(V)
        ct_.wait(A)
        A.activation(bfm[:, :], psA[0:8, 0:512], AF.Sigmoid, bias=0.0, scale=1.0)
        V.tensor_scalar_mul(sp8a[:, :], psA[0:8, 0:512], -1.0)
        cv.wait(A)
        A.activation(sp8b[:, :], sp8a[:, :], AF.Abs, bias=0.0, scale=1.0)
        ca.wait(V)
        V.tensor_scalar_max(sp8a[:, :], sp8a[:, :], 0.0)
        cv.wait(A)
        A.activation(sp8b[:, :], sp8b[:, :], AF.Exp, bias=0.0, scale=-1.0)
        A.activation(sp8b[:, :], sp8b[:, :], AF.Ln, bias=1.0, scale=1.0)
        ca.wait(V)
        V.tensor_add(sp8a[:, :], sp8a[:, :], sp8b[:, :])
        V.tensor_scalar_mul(lnbfm[:, :], sp8a[:, :], -1.0)
        V.tensor_scalar_add(sp8a[:, :], psA[32:40, 0:512], dtb_sb[:, :])
        ba_free = (ca.n, cv.n)
        cv.wait(A)
        A.activation(sp8b[:, :], sp8a[:, :], AF.Abs, bias=0.0, scale=1.0)
        ca.wait(V)
        V.tensor_scalar_max(sp8a[:, :], sp8a[:, :], 0.0)
        cv.wait(A)
        A.activation(sp8b[:, :], sp8b[:, :], AF.Exp, bias=0.0, scale=-1.0)
        A.activation(sp8b[:, :], sp8b[:, :], AF.Ln, bias=1.0, scale=1.0)
        ca.wait(V)
        V.tensor_add(sp8a[:, :], sp8a[:, :], sp8b[:, :])
        V.tensor_scalar_mul(gfm[:, :], sp8a[:, :], negA[:, :])


        projb = [ps_m1[:, 0:512], ps_m1[:, 512:1024], ps_b1[:, 0:512]]
        pevac = {}
        praw_evac = {}
        ca.wait(T)
        cv.wait(T)
        cp.wait(V)
        V.tensor_copy(qraw[:, :, 0:3], tails[:, :, 0:3])
        tails_in_v = cv.n
        cv.wait(P, at=tails_in_v)
        n_tiles = N_CT + Z_CT
        for cti in range(n_tiles):
            buf = cti % 3
            bank = cti % 3
            if cti + 2 < n_tiles:
                w_dma(cti + 2, (cti + 2) % 3)
            cw_s[buf].wait(T, at=w_loaded[buf])
            cv.wait(T, at=lnv)
            if bank in pevac:
                ca.wait(T, at=pevac[bank])
            ph = projb[bank]
            for k in range(KCH):
                _tmm = T.matmul(ph, wslab[:, buf, k * 128:(k + 1) * 128], ln1[:, k, :],
                                start=(k == 0), stop=(k == KCH - 1))
            ct_.inc(_tmm)
            w_used[("w", buf)] = ct_.n
            ct_.wait(A)
            if cti < N_CT:
                A.activation(qraw[:, cti, 3:515], ph, AF.Copy, bias=0.0, scale=1.0)
                praw_evac[cti] = ca.n
                eng = V
                ca.wait(eng, at=praw_evac[cti])
                w4 = [convw_sb[:, cti, j:j + 1] for j in range(4)]
                eng.tensor_scalar_mul(qhat[:, cti, :], qraw[:, cti, 0:512], w4[0])
                for j in range(1, 4):
                    eng.scalar_tensor_tensor(qhat[:, cti, :], qraw[:, cti, j:j + 512], w4[j],
                                             qhat[:, cti, :], ALU.mult, ALU.add)
            else:
                A.activation(z_sc[:, cti - N_CT, :], ph, AF.Copy, bias=0.0, scale=1.0)
            pevac[bank] = ca.n
        cv.wait(A)
        A.activation(qhat[:, 0:12, :], qhat[:, 0:12, :], AF.Silu, bias=0.0, scale=1.0)
        A.activation(qhat[:, 12:24, :], qhat[:, 12:24, :], AF.Silu, bias=0.0, scale=1.0)
        cv.wait(V, at=tails_in_v)
        V.tensor_copy(tails[:, :, 0:3], qraw[:, :, 512:515])

        ss_slots = [psA[0:1, 0:512], psA[32:33, 0:512], psA[64:65, 0:512],
                    ps_b2[0:1, 0:512], ps_b2[32:33, 0:512], ps_b2[64:65, 0:512],
                    ps_b2[0:1, 512:1024], ps_b2[32:33, 512:1024], ps_b2[64:65, 512:1024]]
        sq_at = {}
        ssum_at = {}
        sqrt_at = {}
        rec_at = {}
        rep_at = {}
        mul_at = {}
        ct_.wait(A)
        ca.wait(T)
        cv.wait(T)
        A.activation(ln1[:, 0:8, :], qhat[:, 0:8, :], AF.Square, bias=0.0, scale=1.0)
        sq_at[0] = ca.n
        A.activation(ln1[:, 8:16, :], qhat[:, 8:16, :], AF.Square, bias=0.0, scale=1.0)
        sq_at[1] = ca.n
        stat_bx = [stat_b[0:1, 0:512], stat_b[0:1, 512:1024],
                   qraw[0:1, 2, 3:515], qraw[0:1, 3, 3:515]]
        l2slots = [ps_b2[0:1, 0:512], ps_b2[32:33, 0:512], ps_b2[64:65, 0:512],
                   ps_b2[0:1, 512:1024], ps_b2[32:33, 512:1024], ps_b2[64:65, 512:1024]]
        q_repl = [ps_b1[:, 0:512], ps_b1[:, 512:1024]]
        psm1_s["a"] = ca.n; psm1_s["v"] = cv.n
        pstr_s["a"] = ca.n
        psa_s["a"] = ca.n; psa_s["v"] = cv.n
        c0_ = sc * NCH
        for i in range(QK_CT + 8):
            if i < QK_CT:
                cti = i
                sl = cti % 6
                ca.wait(T, at=sq_at[cti // 8])
                if cti >= 6:
                    ca.wait(T, at=sqrt_at[cti - 6])
                if cti < 3:
                    ca.wait(T, at=ba_free[0])
                    cv.wait(T, at=ba_free[1])
                ct_.inc(T.matmul(l2slots[sl], ones128_bf[:, :], ln1[:, cti, :],
                                 start=True, stop=True))
                ssum_at[cti] = ct_.n
            if i >= 2 and i - 2 < QK_CT:
                cti = i - 2
                sl = cti % 6
                f2 = cti % 2
                ct_.wait(A, at=ssum_at[cti])
                if cti >= 2:
                    cv.wait(A, at=rec_at[cti - 2])
                if cti < 8:
                    A.activation(stat_f[0:1, f2 * 512:(f2 + 1) * 512], l2slots[sl],
                                 AF.Sqrt, bias=128.0 * 1e-6, scale=128.0)
                else:
                    A.activation(stat_f[0:1, f2 * 512:(f2 + 1) * 512], l2slots[sl],
                                 AF.Sqrt, bias=1e-6, scale=1.0)
                sqrt_at[cti] = ca.n
            if i >= 3 and i - 3 < QK_CT:
                cti = i - 3
                f2 = cti % 2
                ca.wait(V, at=sqrt_at[cti])
                if cti >= 4:
                    ct_.wait(V, at=rep_at[cti - 4])
                V.reciprocal(stat_bx[cti % 4], stat_f[0:1, f2 * 512:(f2 + 1) * 512])
                rec_at[cti] = cv.n
            if i >= 8 and i - 8 < QK_CT:
                cti = i - 8
                ct_.wait(V, at=rep_at[cti])
                V.tensor_mul(qhat[:, cti, :], qhat[:, cti, :], q_repl[cti % 2])
                mul_at[cti] = cv.n
            if i >= 6 and i - 6 < QK_CT:
                cti = i - 6
                cv.wait(T, at=rec_at[cti])
                if cti >= 2:
                    cv.wait(T, at=mul_at[cti - 2])
                ct_.inc(T.matmul(q_repl[cti % 2], ones1_bf[:, :],
                                 stat_bx[cti % 4], start=True, stop=True))
                rep_at[cti] = ct_.n
            if i == 2:
                u_prep_open(c0_)
                u_p1(c0_)
            elif i == 5:
                u_p2(c0_)
            elif i == 14:
                u_p3a(c0_)
            elif i == 18:
                u_p3b(c0_)
            elif i == 20:
                u_p6a(c0_)

        l2done[sc] = cv.n

        if "proj" in dbg and sc == 0:
            cv.wait(S)
            ca.wait(S)
            cdbg.inc(S.dma_start(out=dbg_out["qhat"], in_=qhat[:, :, :]), 16)
            cdbg.inc(S.dma_start(out=dbg_out["zsc"], in_=z_sc[:, :, :]), 16)
            cdbg.inc(S.dma_start(out=dbg_out["bg"][:, 0, :], in_=bfm[:, :]), 16)
            cdbg.inc(S.dma_start(out=dbg_out["bg"][:, 1, :], in_=lnbfm[:, :]), 16)
            cdbg.inc(S.dma_start(out=dbg_out["bg"][:, 2, :], in_=gfm[:, :]), 16)

        if stop_after == "A0" and sc == 0:
            cdbg.wait(S)
            return nc, dbg_out

        _scA.__exit__(None, None, None)
        _scB = scope(f"B{sc}"); _scB.__enter__()
        bank_free[0] = (ca.n, cv.n)
        bank_free[1] = (ca.n, cv.n)
        for br in range(NCH + 2):
            pre0 = (br == 0)
            if br == 1 and cc_issued < sc:
                cog.wait(P, at=og_stored)
                ccc.inc(P.collective_compute(
                    "AllGather", ALU.bypass, replica_groups=PAIRS,
                    ins=[cc_in.ap()[sc - 1].opt()], outs=[cc_out.ap()[sc - 1].opt()]))
                cc_issued = sc
            base = sc * NCH
            has_p = br < NCH
            has_c = 1 <= br <= NCH
            has_y = 2 <= br
            cP, cC, cY = base + br, base + br - 1, base + br - 2
            lfq.extend(chq)
            chq = build_chain(cC) if has_c else []
            if has_y:
                u_y0(cY)
            cpop()
            if has_y:
                u_y1(cY)
            cpop()
            if has_p and not pre0:
                u_prep_open(cP)
                u_p1(cP)
            cpop()
            if has_p and not pre0:
                u_p2(cP)
            cpop()
            cpop()
            if has_p and not pre0:
                u_p3a(cP)
            while lfq:
                lfq.pop(0)()
            cpop()
            if has_y:
                u_y2(cY)
            cpop()
            if has_p and not pre0:
                u_p3b(cP)
            cpop()
            if has_y:
                u_y3(cY)
            cpop()
            if has_y:
                u_y4(cY)
            if has_p:
                u_p4(cP)
            cpop()
            cpop()
            if has_p:
                u_p5(cP)
            cpop()
            cpop()
            if has_p and not pre0:
                u_p6a(cP)
            cpop()
            if has_p:
                u_p6b(cP)
            cpop()
            cpop()
            if has_p:
                u_p7(cP)
            cpop()
            cpop()
            if br >= NCH:
                while lfq or chq:
                    cpop()
            if br == NCH - 1 and sc + 1 < NSC:
                ct_.wait(P)
                cv.wait(P)
                ca.wait(P)
                cx.inc(P.dma_start(
                    out=qhat[:, 0:16, :],
                    in_=xT[:, (sc + 1) * SC:(sc + 2) * SC]
                    .rearrange("(k p) t -> p k t", p=128)), 16)

        osq = ln1[:, 0:8, :]
        zsil = ln1[:, 8:16, :]
        ca.wait(T)
        cv.wait(T)
        ct_.wait(A)
        A.activation(osq[:, 0:4, :], og_sc[:, 0:4, :], AF.Square, bias=0.0, scale=1.0)
        osq_at0 = ca.n
        A.activation(osq[:, 4:8, :], og_sc[:, 4:8, :], AF.Square, bias=0.0, scale=1.0)
        osq_at1 = ca.n
        A.activation(zsil[:, :, :], z_sc[:, :, :], AF.Silu, bias=0.0, scale=1.0)
        zsil_at = ca.n
        ssum_at = {}
        sqrt_at = {}
        rec_at = {}
        rep_at = {}
        mul_at = {}
        o_repl = [ps_b1[:, 0:512], ps_b1[:, 512:1024], ps_m1[:, 0:512], ps_m1[:, 512:1024]]
        for i in range(16):
            if i < 8:
                h = i
                ca.wait(T, at=osq_at0 if h < 4 else osq_at1)
                ct_.inc(T.matmul(ss_slots[h], ones128_bf[:, :], osq[:, h, :],
                                 start=True, stop=True))
                ssum_at[h] = ct_.n
                if h == 7:
                    ct_.wait(V)
                    ca.wait(V, at=zsil_at)
                    V.scalar_tensor_tensor(osq[:, :, :], og_sc[:, :, :], normw_sb[:, :],
                                           zsil[:, :, :], ALU.mult, ALU.mult)
            if i >= 2 and i - 2 < 8:
                h = i - 2
                f2 = h % 2
                ct_.wait(A, at=ssum_at[h])
                if h >= 2:
                    cv.wait(A, at=rec_at[h - 2])
                A.activation(stat_f[0:1, f2 * 512:(f2 + 1) * 512], ss_slots[h],
                             AF.Sqrt, bias=1e-6, scale=1.0 / 128.0)
                sqrt_at[h] = ca.n
            if i >= 3 and i - 3 < 8:
                h = i - 3
                f2 = h % 2
                ca.wait(V, at=sqrt_at[h])
                if h >= 2:
                    ct_.wait(V, at=rep_at[h - 2])
                V.reciprocal(stat_b[0:1, f2 * 512:(f2 + 1) * 512],
                             stat_f[0:1, f2 * 512:(f2 + 1) * 512])
                rec_at[h] = cv.n
            if i >= 8 and i - 8 < 8:
                h = i - 8
                ct_.wait(V, at=rep_at[h])
                V.tensor_mul(og_sc[:, h, :], osq[:, h, :], o_repl[h % 4])
                mul_at[h] = cv.n
            if i >= 4 and i - 4 < 8:
                h = i - 4
                f2 = h % 2
                cv.wait(T, at=rec_at[h])
                if h >= 4:
                    cv.wait(T, at=mul_at[h - 4])
                ct_.inc(T.matmul(o_repl[h % 4], ones1_bf[:, :],
                                 stat_b[0:1, f2 * 512:(f2 + 1) * 512], start=True, stop=True))
                rep_at[h] = ct_.n

        cv.wait(S)
        cog.inc(S.dma_start(
            out=cc_in.ap()[sc].rearrange("(h p) t -> p h t", p=128),
            in_=og_sc[:, :, :]), 16)
        og_stored = cog.n
        _scB.__exit__(None, None, None)

    cog.wait(P, at=og_stored)
    ccc.inc(P.collective_compute(
        "AllGather", ALU.bypass, replica_groups=PAIRS,
        ins=[cc_in.ap()[NSC - 1].opt()], outs=[cc_out.ap()[NSC - 1].opt()]))
    for ctr in (ct_, cv, ca, cp, cx, *cw_s, cog, csd):
        for eng in (T, V, A, P, S):
            ctr.wait(eng)
    sbctx_ab.close()
    sbctx_w = contextlib.ExitStack()
    wgsl = sbctx_w.enter_context(nc.sbuf_tensor("wgsl", [128, 2, KCH * 128], BF16))
    wusl = sbctx_w.enter_context(nc.sbuf_tensor("wusl", [128, 2, KCH * 128], BF16))
    wg8 = sbctx_w.enter_context(nc.sbuf_tensor("wg8", [128, 2, KCH * 128], U8))
    wu8 = sbctx_w.enter_context(nc.sbuf_tensor("wu8", [128, 2, KCH * 128], U8))
    gu_loaded = {}
    cast_a = {}
    cast_v = {}

    def gu_dma(it, buf):
        sl8 = it % 2
        qe = S if sl8 == 0 else P
        if sl8 in cast_a:
            ca.wait(qe, at=cast_a[sl8][1])
            cv.wait(qe, at=cast_v[sl8][1])
        cgu_s[sl8].inc(qe.dma_start(out=wg8[:, sl8, :], in_=wg_in[it].rearrange("p k c -> p (k c)")), 16)
        cgu_s[sl8].inc(qe.dma_start(out=wu8[:, sl8, :], in_=wu_in[it].rearrange("p k c -> p (k c)")), 16)
        gu_loaded[sl8] = cgu_s[sl8].n

    def gu_cast(it, buf):
        sl8 = it % 2
        if ("g", buf) in w_used:
            ct_.wait(A, at=w_used[("g", buf)])
            ct_.wait(V, at=w_used[("g", buf)])
        cgu_s[sl8].wait(A, at=gu_loaded[sl8])
        cgu_s[sl8].wait(V, at=gu_loaded[sl8])
        A.activation(wgsl[:, buf, :], wg8[:, sl8, :], AF.Copy, bias=0.0, scale=1.0)
        cast_a[sl8] = (buf, ca.n)
        V.tensor_copy(wusl[:, buf, :], wu8[:, sl8, :])
        cast_v[sl8] = (buf, cv.n)

    og_my = sb2("og_my", [128, NH, TOKH], BF16)
    og_peer = sb2("og_peer", [128, NH, TOKH], BF16)
    x_half = sb2("x_half", [128, KCH, TOKH], F32)
    araw = sb2("araw", [128, KCH, TOKH], BF16)
    wosl = sb2("wosl", [128, 2, 16 * 128], BF16)
    dsq2 = og_my[:, 2:4, :]
    dcp2 = og_my[:, 0:2, :]
    mrepD = og_my[:, 4, :]
    rrepD = og_my[:, 5, :]
    tD = ps_b1[:, 0:1024]
    hgv = pid % 2
    ccin_r = cc_in.ap().rearrange("s (h p) t -> p (s h) t", p=128)
    ccout_r = cc_out.ap().rearrange("s (b h p) t -> p (s b h) t", p=128, b=2)
    for i in range(2):
        cph.inc(P.dma_start(out=og_my[:, :, i * SC:(i + 1) * SC],
                            in_=ccin_r[:, ds((2 * hgv + i) * NH, NH), :]), 16)

    _scD = scope("D"); _scD.__enter__()
    wo_loaded = {}

    def wout_dma(dt, buf):
        if ("wo", buf) in w_used:
            ct_.wait(P, at=w_used[("wo", buf)])
        cwo_s[buf].inc(P.dma_start(out=wosl[:, buf, :], in_=wout[dt].rearrange("p v d -> p (v d)")), 16)
        wo_loaded[buf] = cwo_s[buf].n

    wout_dma(0, 0)
    wout_dma(1, 1)
    ccc.wait(P)
    for i in range(2):
        cph.inc(P.dma_start(
            out=og_peer[:, :, i * SC:(i + 1) * SC],
            in_=ccout_r[:, ds((2 * hgv + i) * 2 * NH + (1 - hgv) * NH, NH), :]), 16)
    og_done = cph.n
    cph.inc(P.dma_start(out=x_half[:, :, :],
                        in_=xT[:, ds(half_off, TOKH)].rearrange("(k p) t -> p k t", p=128)), 16)
    rotD = [ps_m1, ps_b1]
    sq_at = {}
    stat_at = {}

    def _sqv(dt):
        return araw[:, dt + 1, :] if dt + 1 < DT_FF else og_my[:, 2, :]

    def _araw_stats(dt):
        ca.wait(T, at=sq_at[dt])
        sv = _sqv(dt)
        _tmm = T.matmul(psA[0:1, 0:512], ones128_bf[:, :], araw[:, dt, 0:512],
                        start=(dt == 0), stop=(dt == DT_FF - 1))
        _tmm = T.matmul(psA[64:65, 0:512], ones128_bf[:, :], araw[:, dt, 512:1024],
                        start=(dt == 0), stop=(dt == DT_FF - 1))
        _tmm = T.matmul(psA[32:33, 0:512], ones128_bf[:, :], sv[:, 0:512],
                        start=(dt == 0), stop=(dt == DT_FF - 1))
        _tmm = T.matmul(ps_b2[0:1, 0:512], ones128_bf[:, :], sv[:, 512:1024],
                        start=(dt == 0), stop=(dt == DT_FF - 1))
        ct_.inc(_tmm)
        stat_at[dt % 2] = ct_.n

    for dt in range(DT_FF):
        buf = dt % 2
        if 1 <= dt and dt + 1 < DT_FF:
            wout_dma(dt + 1, (dt + 1) % 2)
        cph.wait(T, at=og_done)
        cwo_s[buf].wait(T, at=wo_loaded[buf])
        if dt >= 2:
            ca.wait(T)
        pb = rotD[dt % 2]
        for vc in range(16):
            og_src = og_my[:, vc, :] if vc < 8 else og_peer[:, vc - 8, :]
            lhs = wosl[:, buf, vc * 128:(vc + 1) * 128]
            _tmm = T.matmul(pb[:, 0:512], lhs, og_src[:, 0:512], start=(vc == 0), stop=(vc == 15))
            _tmm = T.matmul(pb[:, 512:1024], lhs, og_src[:, 512:1024], start=(vc == 0), stop=(vc == 15))
        ct_.inc(_tmm)
        w_used[("wo", buf)] = ct_.n
        if dt >= 1:
            _araw_stats(dt - 1)
        ct_.wait(A)
        A.activation(araw[:, dt, :], pb[:, 0:1024], AF.Copy, bias=0.0, scale=1.0)
        A.activation(_sqv(dt), araw[:, dt, :], AF.Square, bias=0.0, scale=1.0)
        sq_at[dt] = ca.n
    _araw_stats(DT_FF - 1)

    gu_dma(0, 0)
    gu_dma(1, 1)
    stats_epilogue(psA[0:1, 0:512], psA[32:33, 0:512],
                   mrepD[:, 0:512], rrepD[:, 0:512], 1e-5, repl_ps=ps_b1)
    stats_epilogue(psA[64:65, 0:512], ps_b2[0:1, 0:512],
                   mrepD[:, 512:1024], rrepD[:, 512:1024], 1e-5, repl_ps=ps_b1)

    ca.wait(V)
    cph.wait(V)
    ca.wait(T)
    cv.wait(T)
    for k in range(KCH):
        db = k % 2
        V.tensor_sub(tD[:, :], araw[:, k, :], mrepD[:, :])
        V.tensor_mul(tD[:, :], tD[:, :], rrepD[:, :])
        V.tensor_add(x_half[:, k, :], x_half[:, k, :], tD[:, :])
        V.engine_nop()
        mid_at = cv.n
        cv.wait(A, at=mid_at)
        if k >= 2:
            ct_.wait(A, at=stat_at[db])
        A.activation(dcp2[:, db, :], x_half[:, k, :], AF.Copy, bias=0.0, scale=1.0)
        A.activation(dsq2[:, db, :], dcp2[:, db, :], AF.Square, bias=0.0, scale=1.0)
        sq_at[db] = ca.n
        ca.wait(T, at=sq_at[db])
        _tmm = T.matmul(psA[0:1, 0:512], ones128_bf[:, :], dcp2[:, db, 0:512],
                        start=(k == 0), stop=(k == KCH - 1))
        _tmm = T.matmul(psA[64:65, 0:512], ones128_bf[:, :], dcp2[:, db, 512:1024],
                        start=(k == 0), stop=(k == KCH - 1))
        _tmm = T.matmul(psA[32:33, 0:512], ones128_bf[:, :], dsq2[:, db, 0:512],
                        start=(k == 0), stop=(k == KCH - 1))
        _tmm = T.matmul(ps_b2[0:1, 0:512], ones128_bf[:, :], dsq2[:, db, 512:1024],
                        start=(k == 0), stop=(k == KCH - 1))
        ct_.inc(_tmm)
        stat_at[db] = ct_.n
    if "mid" in dbg:
        ca.wait(S)
        cv.wait(S)
        cdbg.inc(S.dma_start(out=dbg_out["araw"], in_=araw[:, :, :]), 16)
        cdbg.inc(S.dma_start(out=dbg_out["xmid"], in_=x_half[:, :, :]), 16)

    stats_epilogue(psA[0:1, 0:512], psA[32:33, 0:512],
                   mrepD[:, 0:512], rrepD[:, 0:512], 1e-5, repl_ps=ps_b1)
    stats_epilogue(psA[64:65, 0:512], ps_b2[0:1, 0:512],
                   mrepD[:, 512:1024], rrepD[:, 512:1024], 1e-5, repl_ps=ps_b1)

    gu_cast(0, 0)
    gu_cast(1, 1)
    tD2 = ps_b2[:, 0:1024]
    ca.wait(V)
    ca.wait(T)
    cv.wait(T)
    ca.wait(T, at=cast_a[0][1])
    cv.wait(T, at=cast_v[0][1])
    ca.wait(T, at=cast_a[1][1])
    cv.wait(T, at=cast_v[1][1])
    pbE2 = [ps_m1, ps_b1]
    for k in range(KCH):
        V.tensor_sub(tD2[:, :], x_half[:, k, :], mrepD[:, :])
        V.tensor_mul(ln2[:, k, :], tD2[:, :], rrepD[:, :])
        ln2_at = cv.n
        cv.wait(T, at=ln2_at)
        _tmm = T.matmul(psA[0:1, 0:512], ones128_bf[:, :], ln2[:, k, 0:512],
                        start=(k == 0), stop=(k == KCH - 1))
        _tmm = T.matmul(psA[32:33, 0:512], ones128_bf[:, :], ln2[:, k, 512:1024],
                        start=(k == 0), stop=(k == KCH - 1))
        for itt in range(2):
            pbE = pbE2[itt]
            _tmm = T.matmul(pbE[:, 0:512], wgsl[:, itt, k * 128:(k + 1) * 128],
                            ln2[:, k, 0:512], start=(k == 0), stop=(k == KCH - 1))
            _tmm = T.matmul(pbE[:, 512:1024], wusl[:, itt, k * 128:(k + 1) * 128],
                            ln2[:, k, 0:512], start=(k == 0), stop=(k == KCH - 1))
        ct_.inc(_tmm)
    V.engine_nop()
    if "ln2" in dbg:
        cv.wait(S)
        cdbg.inc(S.dma_start(out=dbg_out["ln2"], in_=ln2[:, :, :]), 16)
    cv.wait(S)
    cspill.inc(S.dma_start(out=xmid_dram.ap().rearrange("(k p) t -> p k t", p=128),
                         in_=x_half[:, :, :]), 16)

    if stop_after == "D":
        cdbg.wait(S)
        cspill.wait(S)
        return nc, dbg_out

    _scD.__exit__(None, None, None)
    _scE = scope("E"); _scE.__enter__()
    xmid_used = {}
    out_stored = {}
    for ctr in (ct_, cv, ca, cp, cph, cwo_s[0], cwo_s[1], cspill, cdbg):
        for eng in (T, V, A, P, S):
            ctr.wait(eng)
    sbctx_2.close()
    h2 = sb3("h2", [128, IT_FF, FGRP], BF16)
    wdsl = sb3("wdsl", [128, 4, 16 * 128], BF16)
    gact = sb3("gact", [128, 2, FGRP], BF16)
    udeq = sb3("udeq", [128, FGRP], BF16)
    draw = sb3("draw", [128, DT_FF, FGRP], BF16)
    rs2grp = sb3("rs2grp", [1, 2, FGRP], BF16)
    rsrep = sb3("rsrep", [128, FGRP], BF16)
    rs2rep = sb3("rs2rep", [128, FGRP], BF16)
    tEu = sb3("tEu", [128, FGRP], F32)
    rs2_bf = sb3("rs2_bf", [1, FGRP], BF16)
    xmid_sb = sb3("xmid_sb", [128, 2, FGRP], F32)
    outb = sb3("outb", [128, 2, FGRP], F32)
    tE = sb3("tE", [128, FGRP], F32)
    dsq3 = sb3("dsq3", [128, 2, 512], BF16)
    mrepE = sb3("mrepE", [128, 512], BF16)
    rrepE = sb3("rrepE", [128, 512], BF16)
    ct_.wait(V)
    V.tensor_copy(rs2grp[0:1, 0, :], psA[0:1, 0:512])
    V.tensor_copy(rs2grp[0:1, 1, :], psA[32:33, 0:512])
    for grp in range(2):
        g0 = grp * FGRP
        rs_bf = rs2grp[0:1, grp, :]
        cv.wait(T)
        ca.wait(T)
        ct_.inc(T.matmul(psA[:, 0:512], ones1_bf[:, :], rs_bf, start=True, stop=True))
        ct_.wait(A)
        A.activation(rsrep[:, :], psA[:, 0:512], AF.Copy, bias=0.0, scale=128.0)
        rsrep_at = ca.n

        if grp == 1:
            gu_cast(0, 0)
        rotE = [ps_m1, ps_b1, ps_b2]
        evac_done = {}
        gact_used = {}
        wd_loaded = {}

        def wd_dma(dt, icg, buf):
            if ("d", buf) in w_used:
                ct_.wait(P, at=w_used[("d", buf)])
            cwd_s[buf].inc(P.dma_start(out=wdsl[:, buf, :],
                                in_=wd_in[dt, :, icg * 16:(icg + 1) * 16, :]
                                .rearrange("p c d -> p (c d)")), 16)
            wd_loaded[buf] = cwd_s[buf].n

        cv.wait(T)
        for it in range(IT_FF):
            buf = it % 2
            if it == IT_FF - 4:
                for _pf in range(3):
                    wd_dma(0, _pf, _pf)
            if it + 2 < IT_FF:
                gu_dma(it + 2, 0)
            if it + 1 < IT_FF and not (grp == 0 and it + 1 < 2):
                gu_cast(it + 1, (it + 1) % 2)
            bank = it % 3
            if bank in evac_done:
                ca.wait(T, at=evac_done[bank][0])
                cv.wait(T, at=evac_done[bank][1])
            pb = rotE[bank]
            if grp == 0 and it < 2:
                w_used[("g", buf)] = ct_.n
            else:
                ca.wait(T, at=cast_a[it % 2][1])
                cv.wait(T, at=cast_v[it % 2][1])
                for k in range(KCH):
                    _tmm = T.matmul(pb[:, 0:512], wgsl[:, buf, k * 128:(k + 1) * 128],
                             ln2[:, k, g0:g0 + 512], start=(k == 0), stop=(k == KCH - 1))
                ct_.inc(_tmm)
                for k in range(KCH):
                    _tmm = T.matmul(pb[:, 512:1024], wusl[:, buf, k * 128:(k + 1) * 128],
                             ln2[:, k, g0:g0 + 512], start=(k == 0), stop=(k == KCH - 1))
                ct_.inc(_tmm)
                w_used[("g", buf)] = ct_.n
            ct_.wait(V)
            ca.wait(V, at=silu_at if it >= 1 else rsrep_at)
            V.tensor_sub(tE[:, :], pb[:, 0:512], rsrep[:, :])
            V.tensor_sub(tEu[:, :], pb[:, 512:1024], rsrep[:, :])
            sub_at = cv.n
            gb = it % 2
            if gb in gact_used:
                cv.wait(A, at=gact_used[gb])
            cv.wait(A, at=sub_at)
            A.activation(gact[:, gb, :], tE[:, :], AF.Silu, bias=0.0,
                                scale=sg_sb[:, it:it + 1])
            silu_at = ca.n
            ca.wait(V)
            V.scalar_tensor_tensor(h2[:, it, :], tEu[:, :], su_sb[:, it:it + 1],
                                   gact[:, gb, :], ALU.mult, ALU.mult)
            gact_used[gb] = cv.n
            evac_done[bank] = (ca.n, sub_at)
        cv.wait(T)
        for ic in range(IT_FF):
            _tmm = T.matmul(psA[0:1, 0:512], ones128_bf[:, :], h2[:, ic, :],
                     start=(ic == 0), stop=(ic == IT_FF - 1))
        ct_.inc(_tmm)
        ct_.wait(V)
        V.tensor_copy(rs2_bf[0:1, :], psA[0:1, 0:512])
        cv.wait(T)
        ct_.inc(T.matmul(psA[:, 0:512], ones1_bf[:, :], rs2_bf[0:1, :], start=True, stop=True))
        ct_.wait(A)
        A.activation(rs2rep[:, :], psA[:, 0:512], AF.Copy, bias=0.0, scale=128.0)
        ca.wait(V)

        if grp == 0:
            gu_dma(0, 0)
            gu_dma(1, 1)
        nslab = 0
        for dt in range(DT_FF):
            bank = dt % 3
            pb = rotE[bank]
            if bank in evac_done:
                ca.wait(T, at=evac_done[bank][0])
                cv.wait(T, at=evac_done[bank][1])
            for icg in range(4):
                buf = nslab % 4
                if nslab + 3 < DT_FF * 4:
                    nxt = nslab + 3
                    wd_dma(nxt // 4, nxt % 4, nxt % 4)
                cwd_s[buf].wait(T, at=wd_loaded[buf])
                for ici in range(16):
                    ic = icg * 16 + ici
                    _tmm = T.matmul(pb[:, 0:512], wdsl[:, buf, ici * 128:(ici + 1) * 128], h2[:, ic, :],
                             start=(ic == 0), stop=(ic == IT_FF - 1))
                ct_.inc(_tmm)
                w_used[("d", buf)] = ct_.n
                nslab += 1
            ct_.wait(V)
            V.tensor_sub(tE[:, :], pb[:, 0:512], rs2rep[:, :])
            V.tensor_scalar(draw[:, dt, :], tE[:, :], sd_sb[:, dt:dt + 1],
                                   None, ALU.mult)
            evac_done[bank] = (ca.n, cv.n)
        ca.wait(T)
        cv.wait(T)
        cv.wait(A)
        for k in range(DT_FF):
            db = k % 2
            if k >= 2:
                ct_.wait(A, at=sq_used[db])
            A.activation(dsq3[:, db, :], draw[:, k, :], AF.Square, bias=0.0, scale=1.0)
            ca.wait(T)
            _tmm = T.matmul(psA[0:1, 0:512], ones128_bf[:, :], draw[:, k, :],
                     start=(k == 0), stop=(k == DT_FF - 1))
            _tmm = T.matmul(psA[32:33, 0:512], ones128_bf[:, :], dsq3[:, db, :],
                     start=(k == 0), stop=(k == DT_FF - 1))
            ct_.inc(_tmm)
            if k == 0:
                sq_used = {}
            sq_used[db] = ct_.n
        stats_epilogue(psA[0:1, 0:512], psA[32:33, 0:512],
                       mrepE[:, :], rrepE[:, :], 1e-5)
        cspill.wait(P)
        xmid_view = xmid_dram.ap().rearrange("(k p) t -> p k t", p=128)
        cxm_s[0].inc(P.dma_start(out=xmid_sb[:, 0, :], in_=xmid_view[:, 0, g0:g0 + 512]), 16)
        xm_loaded = {0: cxm_s[0].n}
        for k in range(KCH):
            bo = k % 2
            if k + 1 < KCH:
                nb = (k + 1) % 2
                if nb in xmid_used:
                    cv.wait(P, at=xmid_used[nb])
                cxm_s[nb].inc(P.dma_start(out=xmid_sb[:, nb, :],
                                    in_=xmid_view[:, k + 1, g0:g0 + 512]), 16)
                xm_loaded[nb] = cxm_s[nb].n
            ca.wait(V)
            cxm_s[bo].wait(V, at=xm_loaded[bo])
            if bo in out_stored:
                cyb[bo].wait(V)
            V.tensor_sub(tE[:, :], draw[:, k, :], mrepE[:, :])
            V.tensor_mul(tE[:, :], tE[:, :], rrepE[:, :])
            V.tensor_add(outb[:, bo, :], xmid_sb[:, bo, :], tE[:, :])
            V.engine_nop()
            xmid_used[bo] = cv.n
            cv.wait(S)
            cyb[bo].inc(S.dma_start(out=y_out.rearrange("(k p) t -> p k t", p=128)[:, k, g0:g0 + 512],
                                    in_=outb[:, bo, :]), 16)
            out_stored[bo] = cyb[bo].n
    cyb[0].wait(S)
    cyb[1].wait(S)
    cdbg.wait(S)
    _scE.__exit__(None, None, None)
    return nc, dbg_out


def _prep_inputs(inputs, c):
    b, hg = c // 2, c % 2
    x = inputs["L0_attn_x"]
    Wqkv = np.asarray(inputs["L0_attn_Wqkv"], np.float32)
    Wz = np.asarray(inputs["L0_attn_Wz"], np.float32)
    Wb = np.asarray(inputs["L0_attn_Wb"], np.float32)
    Wa = np.asarray(inputs["L0_attn_Wa"], np.float32)
    conv_w = np.asarray(inputs["L0_attn_conv_w"], np.float32)
    Wout = np.asarray(inputs["L0_attn_Wout"], np.float32)
    hsl = slice(hg * 1024, hg * 1024 + 1024)
    qs, ks, vs = Wqkv[:, 0:2048][:, hsl], Wqkv[:, 2048:4096][:, hsl], Wqkv[:, 4096:6144][:, hsl]
    Wslice = np.concatenate([qs, ks, vs], axis=1)
    wqkv_t = np.ascontiguousarray(Wslice.reshape(16, 128, 24, 128).transpose(2, 1, 0, 3))
    wz_t = np.ascontiguousarray(Wz[:, hsl].reshape(16, 128, 8, 128).transpose(2, 1, 0, 3))
    hh = slice(hg * 8, hg * 8 + 8)
    wba_t = np.ascontiguousarray(
        np.stack([Wb[:, hh].reshape(16, 128, 8), Wa[:, hh].reshape(16, 128, 8)])
        .transpose(2, 0, 1, 3))
    cw = conv_w[:, 0, :]
    cws = np.concatenate([cw[0:2048][hsl], cw[2048:4096][hsl], cw[4096:6144][hsl]], axis=0)
    convw_t = np.ascontiguousarray(cws.reshape(24, 128, 4).transpose(1, 0, 2))
    Wp = np.concatenate([Wout[hg * 1024:hg * 1024 + 1024],
                         Wout[(1 - hg) * 1024:(1 - hg) * 1024 + 1024]], axis=0)
    wout_t = np.ascontiguousarray(Wp.reshape(16, 128, 16, 128).transpose(2, 1, 0, 3))
    return {
        "xT": np.ascontiguousarray(np.asarray(x, np.float32)[b].T),
        "wqkv": wqkv_t, "wz": wz_t, "wba": wba_t, "convw": convw_t,
        "alog": np.ascontiguousarray(
            np.asarray(inputs["L0_attn_A_log"], np.float32)[hh].reshape(8, 1)),
        "dtb": np.ascontiguousarray(
            np.asarray(inputs["L0_attn_dt_bias"], np.float32)[hh].reshape(8, 1)),
        "normw": np.ascontiguousarray(
            np.asarray(inputs["L0_attn_norm_w"], np.float32).reshape(128, 1)),
        "wout": wout_t,
    }


def _prep_shared(inputs):
    def u8view(a):
        a = np.ascontiguousarray(a)
        assert a.dtype == np.int32
        return a.view(np.uint8).reshape(*a.shape, 4)[..., 0]

    qg = u8view(inputs["L0_ffn_gate_w"])
    qu = u8view(inputs["L0_ffn_up_w"])
    qd = u8view(inputs["L0_ffn_down_w"])
    wg_t = np.ascontiguousarray(qg.reshape(64, 128, 16, 128).transpose(0, 3, 2, 1))
    wu_t = np.ascontiguousarray(qu.reshape(64, 128, 16, 128).transpose(0, 3, 2, 1))
    wd_t = np.ascontiguousarray(qd.reshape(16, 128, 64, 128).transpose(0, 3, 2, 1))
    return {
        "wg": wg_t, "wu": wu_t, "wd": wd_t,
        "sg": np.ascontiguousarray(
            np.asarray(inputs["L0_ffn_gate_scale"], np.float32).reshape(64, 128).T),
        "su": np.ascontiguousarray(
            np.asarray(inputs["L0_ffn_up_scale"], np.float32).reshape(64, 128).T),
        "sd": np.ascontiguousarray(
            np.asarray(inputs["L0_ffn_down_scale"], np.float32).reshape(16, 128).T),
        "id128": np.eye(128, dtype=BF),
        "id8": np.eye(8, dtype=np.float32),
    }


def make_in_maps(inputs):
    shared = _prep_shared(inputs)
    in_maps = []
    for c in range(8):
        m = _prep_inputs(inputs, c)
        m.update(shared)
        in_maps.append(m)
    return in_maps


def kernel(**inputs):
    in_maps = make_in_maps(inputs)
    nc, _ = build_nc()
    res = run_bass_kernel_spmd(nc, in_maps, core_ids=list(range(8)))
    out = np.empty((4, 2048, 2048), np.float32)
    for c in range(8):
        b, hg = c // 2, c % 2
        out[b, hg * 1024:(hg + 1) * 1024, :] = res.results[c]["y"].T
    return out
